# version 06
# baseline (speedup 1.0000x reference)
from contextlib import ExitStack

import numpy as np
import ml_dtypes

import concourse.bass as bass
import concourse.mybir as mybir
from concourse.bass_utils import run_bass_kernel_spmd

F32 = mybir.dt.float32
BF16 = mybir.dt.bfloat16
I32 = mybir.dt.int32
AF = mybir.ActivationFunctionType
ALU = mybir.AluOpType
AX = mybir.AxisListType

S = 8192
D = 2048
NH = 16
NKV = 4
KV = 64
LAT = 1024
DFF = 8192
KSEL = 2048
NCORE = 8
TOK = 1024
NT = 8
CAP = 384
NJC = 3
EPS = 1e-5
BIG = 50000.0
NITER_BIS = 30

DEBUG = ()
PHASES = ("p1", "comm", "attn", "ffn", "epi")


def kblock_src(kb, g):
    rb = kb if kb < 8 else 15 - kb
    loc = 0 if kb < 8 else 512
    krow = NKV * KV * rb + KV * g
    kcol = loc
    vrow = NKV * TOK * rb + TOK * g + loc
    return krow, kcol, vrow


class Sems:
    def __init__(self, nc, stack):
        self.nc = nc
        self.stack = stack
        self.h = {}
        self.n = {}

    def get(self, name):
        if name not in self.h:
            self.h[name] = self.stack.enter_context(self.nc.semaphore(name))
        if name not in self.n:
            self.n[name] = 0
        return self.h[name]

    def inc(self, name, instr, k=1):
        instr.then_inc(self.get(name), k)
        self.n[name] += k
        return self.n[name]

    def val(self, name):
        self.get(name)
        return self.n[name]

    def wait(self, eng, name, v=None):
        eng.wait_ge(self.get(name), self.n[name] if v is None else v)


def build(debug=(), phases=PHASES):
    nc = bass.Bass("TRN2", target_bir_lowering=False)
    TE, VE, SE, GP, SY = nc.tensor, nc.vector, nc.scalar, nc.gpsimd, nc.sync

    def din(name, shape, dt=F32):
        return nc.dram_tensor(name, shape, dt, kind="ExternalInput").ap()

    x_in = din("x_own", [TOK, D])
    role_in = din("role", [1, 1], I32)
    Wq = din("Wqp", [16 * 128, D]); bq = din("bq", [1, D])
    Wl = din("Wlp", [8 * 128, D]); bl = din("bl", [1, LAT])
    Wk = din("Wk", [LAT, NKV * KV]); bk = din("bk", [1, NKV * KV])
    Wv = din("Wv", [LAT, NKV * KV]); bv = din("bv", [1, NKV * KV])
    Wq2kv = din("Wq2kv", [128, KV])
    Wkv2h = din("Wkv2h", [KV, 128]); bkv2h = din("bkv2h", [128, 1])
    Wo = din("Wo", [D, D]); bo = din("bo", [1, D])
    rw = din("rw", [1, D]); rb = din("rb", [1, 1])
    w1 = din("w1p", [64 * 128, D]); b1 = din("b1", [1, DFF])
    w2 = din("w2", [DFF, D]); b2 = din("b2", [1, D])
    y_out = nc.dram_tensor("y", [TOK, D], F32, kind="ExternalOutput").ap()

    dbg_specs = {
        "xn_tok": ([128, NT * D], BF16),
        "scores": ([128, NT], F32),
        "qt": ([65, NH * TOK], BF16),
        "lat": ([128, NT * LAT], BF16),
        "s_all": ([NCORE * 1026, 1], F32),
        "bis": ([1, 8], F32),
        "posrow": ([1, TOK], F32),
        "ctxd": ([128, NT * TOK], BF16),
        "sel": ([128, 16 * CAP], BF16),
        "proc": ([128, NJC * D], BF16),
        "woe": ([128, NT * D], BF16),
        "hff": ([128, 4 * CAP], BF16),
        "boeff": ([1, D], F32),
        "sigmask": ([128, NT], F32),
    }
    dbg = {}
    for name in debug:
        shp, dt = dbg_specs[name]
        dbg[name] = nc.dram_tensor("dbg_" + name, shp, dt, kind="ExternalOutput").ap()

    ksend = nc.dram_tensor("ksend", [NKV * KV, TOK], BF16).ap()
    k_all = nc.dram_tensor("k_all", [NCORE * NKV * KV, TOK], BF16).ap()
    vsend = nc.dram_tensor("vsend", [NKV * TOK, KV], BF16).ap()
    v_all = nc.dram_tensor("v_all", [NCORE * NKV * TOK, KV], BF16).ap()
    ssend = nc.dram_tensor("ssend", [1026, 1], F32).ap()
    s_all = nc.dram_tensor("s_all", [NCORE * 1026, 1], F32).ap()
    xn_dram = nc.dram_tensor("xn_dram", [TOK, D], BF16).ap()
    wkvT_dram = nc.dram_tensor("wkvT_dram", [KV, 128], BF16).ap()

    est = ExitStack()
    with est:
        sem = Sems(nc, est)

        def sb(name, shape, dt, **kw):
            return est.enter_context(nc.sbuf_tensor(name, shape, dt, **kw))

        xn_tok = sb("xn_tok", [128, NT * D], BF16)
        ctxd = sb("ctxd", [128, NT * TOK], BF16)
        stats = sb("stats", [128, 2 * NT], F32)
        scown = sb("scown", [128, NT], F32)
        sig_sb = sb("sig_sb", [128, NT], F32)
        sigmask = sb("sigmask", [128, NT], F32)
        mask128 = sb("mask128", [128, NT], F32)
        ones_bf = sb("ones_bf", [1, 512], BF16)
        ones_f32 = sb("ones_f32", [1, 128], F32)
        ones64 = sb("ones64", [64, 1], F32)
        ones128 = sb("ones128", [128, 1], F32)
        ones1x64f = sb("ones1x64f", [1, 64], F32)
        rbcol = sb("rbcol", [128, 1], F32)
        kmax2 = sb("kmax2", [1, 2], F32)
        wq2kv_sb = sb("wq2kv_sb", [128, KV], BF16)
        wkvT_sb = sb("wkvT_sb", [128, KV], BF16)
        bkvcol = sb("bkvcol", [128, 1], BF16)
        role_sb = sb("role_sb", [1, 1], I32)
        rb_sb = sb("rb_sb", [1, 1], F32)
        bq_r = sb("bq_r", [1, D], BF16)
        bl_r = sb("bl_r", [1, LAT], BF16)
        bk_r = sb("bk_r", [1, NKV * KV], BF16)
        bv_r = sb("bv_r", [1, NKV * KV], BF16)
        b2_r = sb("b2_r", [1, D], BF16)
        boeff_r = sb("boeff_r", [1, D], F32)
        t32a = sb("t32a", [32, 32], F32)
        t32b = sb("t32b", [32, 32], F32)
        bis = sb("bis", [1, 8], F32)
        z32 = sb("z32", [1, 32], F32)
        k8 = sb("k8", [1, 8], F32)
        pos128 = sb("pos128", [128, NT], F32)
        qt_stack = ExitStack()
        qt = qt_stack.enter_context(nc.sbuf_tensor("qt", [65, NH * TOK], BF16, side="right"))
        bis_stack = ExitStack()
        scg = bis_stack.enter_context(nc.sbuf_tensor("scg", [32, 256], F32, side="right"))
        cmpb = bis_stack.enter_context(nc.sbuf_tensor("cmpb", [32, 256], F32, side="right"))
        sc_row = sb("sc_row", [1, TOK], F32)
        mask_row = sb("mask_row", [1, TOK], F32)
        pos_row = sb("pos_row", [1, TOK], F32)
        zrow = sb("zrow", [1, TOK], F32)


        ps = est.enter_context(nc.psum_tensor("ps", [128, 4096], F32))

        def bank(b, w=512, p0=0, pn=128):
            return ps[p0:p0 + pn, 512 * b:512 * b + w]

        sv = sem.get("v")
        ss = sem.get("s")
        marks = {}

        def vch(instr):
            return sem.inc("v", instr)

        def sch(instr):
            return sem.inc("s", instr)

        sem.inc("role", SY.dma_start(out=role_sb[:, :], in_=role_in[:, :]), 16)
        sem.inc("ld_rb", SY.dma_start(out=rb_sb[:, :], in_=rb[:, :]), 16)

        for src, dst in ((bq, bq_r), (bl, bl_r), (bk, bk_r), (bv, bv_r), (b2, b2_r)):
            sem.inc("ld_bias", GP.dma_start(out=dst[:, :], in_=src[:, :]), 16)
        sem.inc("ld_wq2kv", GP.dma_start(out=wq2kv_sb[:, :], in_=Wq2kv[:, :]), 16)
        sem.inc("ld_bkv", GP.dma_start(out=bkvcol[:, :], in_=bkv2h[:, :]), 16)
        s_wkv0 = sem.inc("ld_wkvT0", GP.dma_start(out=wkvT_dram[:, :], in_=Wkv2h[:, :]), 16)
        SY.wait_ge(sem.get("ld_wkvT0"), s_wkv0)
        sem.inc("ld_wkvT", SY.dma_start_transpose(out=wkvT_sb[:, 0:KV], in_=wkvT_dram[:, :]), 16)

        VE.memset(ones_bf[:, :], 1.0)
        VE.memset(ones_f32[:, :], 1.0)
        VE.memset(ones64[:, :], 1.0)
        VE.memset(ones128[:, :], 1.0)
        VE.memset(ones1x64f[:, :], 1.0)
        VE.memset(zrow[:, :], 0.0)
        VE.memset(z32[:, :], 0.0)
        VE.memset(t32a[:, :], 0.0)
        VE.memset(t32b[:, :], 0.0)
        sem.inc("const", VE.memset(bis[:, :], 0.0))

        with nc.sbuf_tensor("w_rep", [128, D], F32) as w_rep, \
             nc.sbuf_tensor("rw_r", [1, D], F32, side="right") as rw_r, \
             nc.sbuf_tensor("xst", [128, 2 * D], F32) as xst, \
             nc.sbuf_tensor("xnf", [128, 2 * D], F32) as xnf:

            sem.inc("ld_rw", SY.dma_start(out=rw_r[:, :], in_=rw[:, :]), 16)
            for tt in range(2):
                sem.inc("ld_x", SY.dma_start(out=xst[:, D * tt:D * (tt + 1)],
                                             in_=x_in[128 * tt:128 * (tt + 1), :]), 16)

            sem.wait(TE, "ld_rw")
            sem.wait(TE, "ld_rb")
            sem.wait(TE, "const")
            for i in range(4):
                sem.inc("wrepmm", TE.matmul(bank(i), ones_f32[0:1, 0:128],
                                            rw_r[0:1, 512 * i:512 * (i + 1)],
                                            start=True, stop=True))
            sem.inc("wrepmm", TE.matmul(bank(4, w=1), ones_f32[0:1, 0:128],
                                        rb_sb[0:1, 0:1], start=True, stop=True))
            VE.wait_ge(sem.get("wrepmm"), sem.val("wrepmm"))
            for i in range(4):
                vch(VE.tensor_copy(w_rep[:, 512 * i:512 * (i + 1)], bank(i)))
            sem.inc("wrep", VE.tensor_copy(rbcol[:, :], bank(4, w=1)))

            sem.wait(VE, "wrep")
            cast_s = []
            for tt in range(NT):
                slot = tt % 2
                xs = xst[:, D * slot:D * (slot + 1)]
                xf = xnf[:, D * slot:D * (slot + 1)]
                m_c = sig_sb[:, tt:tt + 1]
                v_c = pos128[:, tt:tt + 1]
                VE.wait_ge(sem.get("ld_x"), 16 * (tt + 1))
                if tt >= 2:
                    VE.wait_ge(ss, cast_s[tt - 2])
                nv = vch(VE.scalar_tensor_tensor(out=xf, in0=xs, scalar=1.0, in1=xs,
                                                 op0=ALU.mult, op1=ALU.mult,
                                                 accum_out=stats[:, tt:tt + 1]))
                nv = vch(VE.tensor_reduce(out=stats[:, NT + tt:NT + tt + 1], in_=xs,
                                          axis=AX.X, op=ALU.add))
                VE.wait_ge(sv, nv)
                nv = vch(VE.tensor_scalar(out=m_c, in0=stats[:, NT + tt:NT + tt + 1],
                                          scalar1=1.0 / D, scalar2=None, op0=ALU.mult))
                VE.wait_ge(sv, nv)
                nv = vch(VE.tensor_scalar(out=v_c, in0=stats[:, tt:tt + 1],
                                          scalar1=1.0 / D, scalar2=None, op0=ALU.mult))
                VE.wait_ge(sv, nv)
                nv = vch(VE.scalar_tensor_tensor(out=v_c, in0=m_c, scalar=m_c, in1=v_c,
                                                 op0=ALU.mult, op1=ALU.subtract))
                VE.wait_ge(sv, nv)
                nv = vch(VE.tensor_scalar(out=v_c, in0=v_c, scalar1=-1.0, scalar2=EPS,
                                          op0=ALU.mult, op1=ALU.add))
                VE.wait_ge(sv, nv)
                nv = vch(VE.reciprocal(out=v_c, in_=v_c))
                SE.wait_ge(sv, nv)
                ns = sch(SE.activation(out=stats[:, tt:tt + 1], in_=v_c, func=AF.Sqrt))
                VE.wait_ge(ss, ns)
                nv = vch(VE.tensor_mul(out=stats[:, NT + tt:NT + tt + 1], in0=m_c,
                                       in1=stats[:, tt:tt + 1]))
                VE.wait_ge(sv, nv)
                nv = vch(VE.tensor_scalar(out=xf, in0=xs, scalar1=stats[:, tt:tt + 1],
                                          scalar2=stats[:, NT + tt:NT + tt + 1],
                                          op0=ALU.mult, op1=ALU.subtract))
                VE.wait_ge(sv, nv)
                nv = vch(VE.scalar_tensor_tensor(out=xs, in0=xf, scalar=1.0,
                                                 in1=w_rep[:, :], op0=ALU.mult, op1=ALU.mult,
                                                 accum_out=scown[:, tt:tt + 1]))
                if tt + 2 < NT:
                    SY.wait_ge(sv, nv)
                    sem.inc("ld_x", SY.dma_start(out=xst[:, D * slot:D * (slot + 1)],
                                                 in_=x_in[128 * (tt + 2):128 * (tt + 3), :]), 16)
                SE.wait_ge(sv, nv)
                si = SE.activation(out=xn_tok[:, D * tt:D * (tt + 1)], in_=xf, func=AF.Copy)
                cast_s.append(sch(si))

            SY.wait_ge(ss, sem.val("s"))
            s_xnw = sem.inc("xnw", SY.dma_start(
                out=xn_dram.rearrange("(a p) d -> p a d", p=128), in_=xn_tok[:, :]), 16)

        with nc.sbuf_tensor("xn_d", [128, 16 * TOK], BF16) as xn_d, \
             nc.sbuf_tensor("qful", [128, 2 * TOK], BF16) as qful, \
             nc.sbuf_tensor("lat_sb", [128, NT * TOK], BF16) as lat_sb, \
             nc.sbuf_tensor("wq_sb", [128, 2 * 2048], BF16) as wq_sb, \
             nc.sbuf_tensor("wk_sb", [128, NT * 256], BF16) as wk_sb, \
             nc.sbuf_tensor("wv_sb", [128, NT * 256], BF16) as wv_sb, \
             nc.sbuf_tensor("qsq", [64, TOK], F32) as qsq, \
             nc.sbuf_tensor("kd_sb", [128, 2 * TOK], BF16) as kd_sb, \
             nc.sbuf_tensor("vs_sb", [128, 256], BF16) as vs_sb, \
             nc.sbuf_tensor("ksq2", [128, TOK], F32) as ksq2:

            SY.wait_ge(sem.get("xnw"), s_xnw)
            for ec in range(16):
                sem.inc("xnd", SY.dma_start_transpose(
                    out=xn_d[:, TOK * ec:TOK * (ec + 1)],
                    in_=xn_dram[:, 128 * ec:128 * (ec + 1)]), 16)

            GP.wait_ge(sem.get("xnw"), s_xnw)
            for dc in range(16):
                if dc >= 2:
                    GP.wait_ge(sem.get("qfmm"), 2 * (dc - 1))
                for half in range(2):
                    sem.inc("ld_wq", GP.dma_start(
                        out=wq_sb[:, 2048 * (dc % 2) + 1024 * half:
                                  2048 * (dc % 2) + 1024 * (half + 1)],
                        in_=Wq[128 * dc:128 * (dc + 1),
                               1024 * half:1024 * (half + 1)]), 16)
            for lc in range(8):
                if lc >= 2:
                    GP.wait_ge(sem.get("latmm"), 2 * (lc - 1))
                else:
                    GP.wait_ge(sem.get("qfmm"), 30 + 2 * lc)
                for half in range(2):
                    sem.inc("ld_wl", GP.dma_start(
                        out=wq_sb[:, 2048 * (lc % 2) + 1024 * half:
                                  2048 * (lc % 2) + 1024 * (half + 1)],
                        in_=Wl[128 * lc:128 * (lc + 1),
                               1024 * half:1024 * (half + 1)]), 16)
            sem.inc("ld_wk", GP.dma_start(
                out=wk_sb[:, :], in_=Wk.rearrange("(a p) d -> p a d", p=128)), 16)
            sem.inc("ld_wv", GP.dma_start(
                out=wv_sb[:, :], in_=Wv.rearrange("(a p) d -> p a d", p=128)), 16)

            TE.wait_ge(sem.get("xnd"), sem.val("xnd"))
            sem.wait(TE, "ld_bias")
            sem.wait(TE, "ld_wq2kv")
            qnve_marks = []
            for h in range(NH):
                slot = h % 2
                b0 = 2 * slot
                TE.wait_ge(sem.get("ld_wq"), 32 * (h + 1))
                if h >= 2:
                    TE.wait_ge(sem.get("qevac"), 2 * (h - 1))
                for th in range(2):
                    for ec in range(16):
                        TE.matmul(bank(b0 + th),
                                  wq_sb[:, 2048 * slot + 128 * ec:2048 * slot + 128 * (ec + 1)],
                                  xn_d[:, TOK * ec + 512 * th:TOK * ec + 512 * (th + 1)],
                                  start=(ec == 0), stop=False)
                    sem.inc("qfmm", TE.matmul(bank(b0 + th), bq_r[0:1, 128 * h:128 * (h + 1)],
                                              ones_bf[0:1, 0:512], start=False, stop=True))
                SE.wait_ge(sem.get("qfmm"), 2 * (h + 1))
                if h >= 2:
                    SE.wait_ge(sem.get("qbmm"), 2 * (h - 1))
                se1 = SE.activation(out=qful[:, TOK * slot:TOK * slot + 512],
                                    in_=bank(b0), func=AF.Copy)
                se2 = SE.activation(out=qful[:, TOK * slot + 512:TOK * (slot + 1)],
                                    in_=bank(b0 + 1), func=AF.Copy)
                sem.inc("qevac", se1)
                sem.inc("qevac", se2)
                TE.wait_ge(sem.get("qevac"), 2 * (h + 1))
                if h >= 1:
                    TE.wait_ge(sem.get("qsqa"), h)
                for th in range(2):
                    sem.inc("qbmm", TE.matmul(bank(4 + th, w=512, pn=64), wq2kv_sb[:, :],
                                              qful[:, TOK * slot + 512 * th:
                                                    TOK * slot + 512 * (th + 1)],
                                              start=True, stop=True))
                SE.wait_ge(sem.get("qbmm"), 2 * (h + 1))
                if h >= 1:
                    SE.wait_ge(sv, qnve_marks[h - 1])
                e1 = SE.activation(out=qt[0:64, TOK * h:TOK * h + 512],
                                   in_=bank(4, pn=64), func=AF.Copy)
                e2 = SE.activation(out=qt[0:64, TOK * h + 512:TOK * (h + 1)],
                                   in_=bank(5, pn=64), func=AF.Copy)
                SE.activation(out=qsq[:, 0:512], in_=bank(4, pn=64), func=AF.Square)
                sq2 = SE.activation(out=qsq[:, 512:1024], in_=bank(5, pn=64), func=AF.Square)
                sem.inc("qtev", e1)
                sem.inc("qtev", e2)
                sem.inc("qsqa", sq2)
                TE.wait_ge(sem.get("qsqa"), h + 1)
                if h >= 1:
                    TE.wait_ge(sv, qnve_marks[h - 1])
                TE.matmul(bank(6, w=512, pn=1), ones64[:, :], qsq[:, 0:512],
                          start=True, stop=True)
                sem.inc("qnmm", TE.matmul(bank(7, w=512, pn=1), ones64[:, :],
                                          qsq[:, 512:1024], start=True, stop=True))
                VE.wait_ge(sem.get("qnmm"), h + 1)
                vch(VE.tensor_scalar(out=qt[64:65, TOK * h:TOK * h + 512],
                                     in0=bank(6, w=512, pn=1),
                                     scalar1=-0.5, scalar2=None, op0=ALU.mult))
                v2 = VE.tensor_scalar(out=qt[64:65, TOK * h + 512:TOK * (h + 1)],
                                      in0=bank(7, w=512, pn=1),
                                      scalar1=-0.5, scalar2=None, op0=ALU.mult)
                qnve_marks.append(vch(v2))
                marks["qnve_last"] = qnve_marks[-1]

            TE.wait_ge(sem.get("qevac"), 2 * NH)
            for lc in range(8):
                slot = lc % 2
                b0 = 2 * slot
                TE.wait_ge(sem.get("ld_wl"), 32 * (lc + 1))
                if lc >= 2:
                    TE.wait_ge(sem.get("latev"), 2 * (lc - 1))
                for th in range(2):
                    for ec in range(16):
                        TE.matmul(bank(b0 + th),
                                  wq_sb[:, 2048 * slot + 128 * ec:2048 * slot + 128 * (ec + 1)],
                                  xn_d[:, TOK * ec + 512 * th:TOK * ec + 512 * (th + 1)],
                                  start=(ec == 0), stop=False)
                    sem.inc("latmm", TE.matmul(bank(b0 + th), bl_r[0:1, 128 * lc:128 * (lc + 1)],
                                               ones_bf[0:1, 0:512], start=False, stop=True))
                SE.wait_ge(sem.get("latmm"), 2 * (lc + 1))
                l1 = SE.activation(out=lat_sb[:, TOK * lc:TOK * lc + 512],
                                   in_=bank(b0), func=AF.Silu)
                l2 = SE.activation(out=lat_sb[:, TOK * lc + 512:TOK * (lc + 1)],
                                   in_=bank(b0 + 1), func=AF.Silu)
                sem.inc("latev", l1)
                sem.inc("latev", l2)

            sem.wait(TE, "ld_wk")
            sem.wait(TE, "ld_wv")
            TE.wait_ge(sem.get("latev"), 16)
            for dt in range(2):
                for th in range(2):
                    b_ = 2 * dt + th
                    for lc in range(8):
                        TE.matmul(bank(b_),
                                  wk_sb[:, 256 * lc + 128 * dt:256 * lc + 128 * (dt + 1)],
                                  lat_sb[:, TOK * lc + 512 * th:TOK * lc + 512 * (th + 1)],
                                  start=(lc == 0), stop=False)
                    sem.inc("kdmm", TE.matmul(bank(b_), bk_r[0:1, 128 * dt:128 * (dt + 1)],
                                              ones_bf[0:1, 0:512], start=False, stop=True))
            VE.wait_ge(sem.get("kdmm"), 4)
            for dt in range(2):
                for th in range(2):
                    kv_ = VE.tensor_copy(kd_sb[:, TOK * dt + 512 * th:TOK * dt + 512 * (th + 1)],
                                         bank(2 * dt + th))
                    sem.inc("kdev", kv_)
            SY.wait_ge(sem.get("kdev"), 4)
            sem.inc("ksend", SY.dma_start(
                out=ksend.rearrange("(a p) t -> p a t", p=128), in_=kd_sb[:, :]), 16)
            for wave in range(2):
                SE.wait_ge(sem.get("kdev"), 4)
                if wave == 1:
                    SE.wait_ge(sem.get("knmm2"), 4)
                ns = sch(SE.activation(out=ksq2[:, :], in_=kd_sb[:, TOK * wave:TOK * (wave + 1)],
                                       func=AF.Square))
                TE.wait_ge(ss, ns)
                if wave == 1:
                    TE.wait_ge(sv, marks["knorm0"])
                for i in range(4):
                    g_half = i // 2
                    th = i % 2
                    sem.inc("knmm2", TE.matmul(
                        bank(4 + i, w=512, p0=64 * g_half, pn=1),
                        ones128[64 * g_half:64 * (g_half + 1), :],
                        ksq2[64 * g_half:64 * (g_half + 1), 512 * th:512 * (th + 1)],
                        start=True, stop=True))
                VE.wait_ge(sem.get("knmm2"), 4 * (wave + 1))
                for i in range(4):
                    g_half = i // 2
                    nv = vch(VE.tensor_reduce(out=k8[0:1, 4 * wave + i:4 * wave + i + 1],
                                              in_=bank(4 + i, w=512, p0=64 * g_half, pn=1),
                                              axis=AX.X, op=ALU.max))
                marks["knorm0"] = nv
            VE.wait_ge(sv, marks["knorm0"])
            nv = vch(VE.tensor_reduce(out=t32a[0:1, 0:1], in_=k8[:, :], axis=AX.X, op=ALU.max))
            for tt in range(NT):
                if tt >= 1:
                    TE.wait_ge(sem.get("kvev"), tt)
                for lc in range(8):
                    TE.matmul(bank(1, w=256),
                              lat_sb[:, TOK * lc + 128 * tt:TOK * lc + 128 * (tt + 1)],
                              wv_sb[:, 256 * lc:256 * (lc + 1)],
                              start=(lc == 0), stop=False)
                sem.inc("kvmm", TE.matmul(bank(1, w=256), ones_bf[0:1, 0:128], bv_r[0:1, :],
                                          start=False, stop=True))
                VE.wait_ge(sem.get("kvmm"), tt + 1)
                if tt >= 1:
                    VE.wait_ge(sem.get("kvsend"), 64 * tt)
                ve2 = VE.tensor_copy(vs_sb[:, :], bank(1, w=256))
                sem.inc("kvev", ve2)
                SY.wait_ge(sem.get("kvev"), tt + 1)
                for g in range(NKV):
                    sem.inc("kvsend", SY.dma_start(
                        out=vsend[TOK * g + 128 * tt:TOK * g + 128 * (tt + 1), :],
                        in_=vs_sb[:, KV * g:KV * (g + 1)]), 16)
            SY.wait_ge(sv, nv)
            sem.inc("ssend", SY.dma_start(
                out=ssend[0:TOK, :].rearrange("(p a) b -> p (a b)", p=128), in_=scown[:, :]), 16)
            sem.inc("ssend", SY.dma_start(out=ssend[1024:1025, :], in_=t32a[0:1, 0:1]), 16)

            if "comm" in phases:
                GP.wait_ge(sem.get("ksend"), sem.val("ksend"))
                GP.wait_ge(sem.get("kvsend"), sem.val("kvsend"))
                GP.wait_ge(sem.get("ssend"), sem.val("ssend"))
                sem.inc("cc", GP.collective_compute(
                    "AllGather", ALU.bypass, replica_groups=[list(range(NCORE))],
                    ins=[ksend.opt()], outs=[k_all.opt()]))
                sem.inc("cc", GP.collective_compute(
                    "AllGather", ALU.bypass, replica_groups=[list(range(NCORE))],
                    ins=[vsend.opt()], outs=[v_all.opt()]))
                sem.inc("cc", GP.collective_compute(
                    "AllGather", ALU.bypass, replica_groups=[list(range(NCORE))],
                    ins=[ssend.opt()], outs=[s_all.opt()]))

            if "xn_tok" in dbg:
                sem.inc("dbg", SE.dma_start(out=dbg["xn_tok"][:, :], in_=xn_tok[:, :]), 16)
            if "scores" in dbg:
                SE.wait_ge(sv, sem.val("v"))
                sem.inc("dbg", SE.dma_start(out=dbg["scores"][:, :], in_=scown[:, :]), 16)
            if "qt" in dbg and "comm" not in phases:
                SE.wait_ge(sv, qnve_marks[-1])
                sem.inc("dbg", SE.dma_start(out=dbg["qt"][:, :], in_=qt[:, :]), 16)
            if "lat" in dbg:
                SE.wait_ge(sem.get("latev"), 16)
                sem.inc("dbg", SE.dma_start(out=dbg["lat"][:, :], in_=lat_sb[:, :]), 16)

        if "comm" in phases:
            SY.wait_ge(sem.get("cc"), 3)
            with nc.allow_non_contiguous_dma(reason="8-element strided kmax gather"):
                sem.inc("kg", SY.dma_start(
                    out=k8[:, :],
                    in_=s_all[1024:NCORE * 1026:1026, 0:1].rearrange("a b -> b a"),
                    allow_slow_non_contiguous=True), 16)
            for r in range(NCORE):
                sem.inc("scg", SY.dma_start(
                    out=scg[:, 32 * r:32 * (r + 1)],
                    in_=s_all[1026 * r:1026 * r + 1024, :].rearrange("(p a) b -> p (a b)", p=32)), 16)
            VE.wait_ge(sem.get("kg"), 16)
            nv = vch(VE.tensor_reduce(out=kmax2[:, 0:1], in_=k8[:, :], axis=AX.X, op=ALU.max))
            VE.wait_ge(sv, nv)
            nv = vch(VE.tensor_scalar(out=kmax2[:, 1:2], in0=kmax2[:, 0:1],
                                      scalar1=-0.5, scalar2=None, op0=ALU.mult))
            VE.wait_ge(sv, nv)
            VE.wait_ge(sv, marks["qnve_last"])
            vqb = VE.tensor_scalar(out=qt[64:65, :], in0=qt[64:65, :],
                                   scalar1=kmax2[0:1, 1:2], scalar2=None, op0=ALU.add)
            marks["qbias"] = vch(vqb)
            if "s_all" in dbg:
                sem.inc("dbg", SY.dma_start(out=dbg["s_all"][:, :], in_=s_all[:, :]), 16)
            if "qt" in dbg:
                SE.wait_ge(sv, marks["qbias"])
                sem.inc("dbg", SE.dma_start(out=dbg["qt"][:, :], in_=qt[:, :]), 16)

        woe = sb("woe", [128, NT * D], BF16)
        if "fold" in phases or "attn" in phases or "epi" in phases:
            with nc.sbuf_tensor("wo_sb", [128, 2 * D], BF16) as wo_sb, \
                 nc.sbuf_tensor("bo_acc", [1, D], F32, side="right") as bo_acc:
                GP.wait_ge(sem.get("kvmm"), NT)
                GP.wait_ge(sem.get("kdmm"), 4)
                for h in range(NH):
                    if h >= 2:
                        GP.wait_ge(sem.get("woev"), h - 1)
                    sem.inc("ld_wo", GP.dma_start(
                        out=wo_sb[:, D * (h % 2):D * (h % 2 + 1)],
                        in_=Wo[128 * h:128 * (h + 1), :]), 16)
                sem.wait(TE, "ld_wkvT")
                sem.wait(TE, "ld_bkv")
                TE.wait_ge(sem.get("kvmm"), NT)
                for h in range(NH):
                    TE.wait_ge(sem.get("ld_wo"), 16 * (h + 1))
                    if h >= 1:
                        TE.wait_ge(sem.get("woev"), h)
                    for eq in range(4):
                        TE.matmul(bank(eq, w=512, pn=64), wkvT_sb[:, :],
                                  wo_sb[:, D * (h % 2) + 512 * eq:D * (h % 2) + 512 * (eq + 1)],
                                  start=True, stop=True)
                        bo_mm = TE.matmul(
                            bank(4 + eq, w=512, pn=1), bkvcol[:, :],
                            wo_sb[:, D * (h % 2) + 512 * eq:D * (h % 2) + 512 * (eq + 1)],
                            start=(h == 0), stop=(h == NH - 1))
                    sem.inc("womm", bo_mm)
                    VE.wait_ge(sem.get("womm"), h + 1)
                    pc = h // 2
                    p0 = 64 * (h % 2)
                    for eq in range(4):
                        wv_ = VE.tensor_copy(
                            woe[p0:p0 + 64, D * pc + 512 * eq:D * pc + 512 * (eq + 1)],
                            bank(eq, w=512, pn=64))
                    sem.inc("woev", wv_)
                sem.inc("ld_bo", SY.dma_start(out=bo_acc[:, :], in_=bo[:, :]), 16)
                VE.wait_ge(sem.get("ld_bo"), sem.val("ld_bo"))
                VE.wait_ge(sem.get("womm"), sem.val("womm"))
                for eq in range(4):
                    bv_ = VE.tensor_add(out=boeff_r[0:1, 512 * eq:512 * (eq + 1)],
                                        in0=bo_acc[0:1, 512 * eq:512 * (eq + 1)],
                                        in1=bank(4 + eq, w=512, pn=1))
                    marks["boeff"] = vch(bv_)
                if "woe" in dbg:
                    SE.wait_ge(sem.get("woev"), NH)
                    sem.inc("dbg", SE.dma_start(out=dbg["woe"][:, :], in_=woe[:, :]), 16)
                if "boeff" in dbg:
                    SE.wait_ge(sv, marks["boeff"])
                    sem.inc("dbg", SE.dma_start(out=dbg["boeff"][:, :], in_=boeff_r[:, :]), 16)

        bg_ops = []
        if "comm" in phases:
            bstate = {"nv": None}

            def bg(f):
                bg_ops.append(f)

            def bgv(make):
                def run():
                    if bstate["nv"] is not None:
                        VE.wait_ge(sv, bstate["nv"])
                    ins_ = make()
                    bstate["nv"] = vch(ins_)
                    return ins_
                bg(run)

            def w_scg():
                VE.wait_ge(sem.get("scg"), sem.val("scg"))
            bg(w_scg)
            bgv(lambda: VE.tensor_reduce(out=t32a[:, 0:1], in_=scg[:, :], axis=AX.X, op=ALU.max))
            bgv(lambda: VE.transpose(out=t32b[:, :], in_=t32a[:, :]))
            bgv(lambda: VE.tensor_reduce(out=bis[0:1, 1:2], in_=t32b[0:1, :], axis=AX.X, op=ALU.max))
            bgv(lambda: VE.tensor_reduce(out=t32a[:, 0:1], in_=scg[:, :], axis=AX.X, op=ALU.min))
            bgv(lambda: VE.transpose(out=t32b[:, :], in_=t32a[:, :]))
            bgv(lambda: VE.tensor_reduce(out=bis[0:1, 0:1], in_=t32b[0:1, :], axis=AX.X, op=ALU.min))
            bgv(lambda: VE.tensor_scalar(out=bis[0:1, 0:1], in0=bis[0:1, 0:1],
                                         scalar1=-1.0, scalar2=None, op0=ALU.add))
            bgv(lambda: VE.tensor_scalar(out=bis[0:1, 1:2], in0=bis[0:1, 1:2],
                                         scalar1=1e-3, scalar2=None, op0=ALU.add))

            def mk_mid():
                bgv(lambda: VE.scalar_tensor_tensor(out=bis[0:1, 2:3], in0=bis[0:1, 0:1],
                                                    scalar=1.0, in1=bis[0:1, 1:2],
                                                    op0=ALU.mult, op1=ALU.add))
                bgv(lambda: VE.tensor_scalar(out=bis[0:1, 2:3], in0=bis[0:1, 2:3],
                                             scalar1=0.5, scalar2=None, op0=ALU.mult))
                bgv(lambda: VE.tensor_scalar(out=t32a[0:1, :], in0=z32[:, :],
                                             scalar1=bis[0:1, 2:3], scalar2=None, op0=ALU.add))
                bgv(lambda: VE.transpose(out=t32b[:, :], in_=t32a[:, :]))
            mk_mid()
            for _ in range(NITER_BIS):
                bgv(lambda: VE.tensor_scalar(out=cmpb[:, :], in0=scg[:, :],
                                             scalar1=t32b[:, 0:1], scalar2=None, op0=ALU.is_gt))
                bgv(lambda: VE.tensor_reduce(out=t32a[:, 0:1], in_=cmpb[:, :],
                                             axis=AX.X, op=ALU.add))
                bgv(lambda: VE.transpose(out=t32b[:, :], in_=t32a[:, :]))
                bgv(lambda: VE.tensor_reduce(out=bis[0:1, 3:4], in_=t32b[0:1, :],
                                             axis=AX.X, op=ALU.add))
                bgv(lambda: VE.tensor_scalar(out=bis[0:1, 4:5], in0=bis[0:1, 3:4],
                                             scalar1=float(KSEL), scalar2=None, op0=ALU.is_ge))
                bgv(lambda: VE.tensor_sub(out=bis[0:1, 5:6], in0=bis[0:1, 2:3],
                                          in1=bis[0:1, 0:1]))
                bgv(lambda: VE.scalar_tensor_tensor(out=bis[0:1, 0:1], in0=bis[0:1, 4:5],
                                                    scalar=bis[0:1, 5:6], in1=bis[0:1, 0:1],
                                                    op0=ALU.mult, op1=ALU.add))
                bgv(lambda: VE.tensor_sub(out=bis[0:1, 6:7], in0=bis[0:1, 1:2],
                                          in1=bis[0:1, 2:3]))
                bgv(lambda: VE.scalar_tensor_tensor(out=bis[0:1, 1:2], in0=bis[0:1, 4:5],
                                                    scalar=bis[0:1, 6:7], in1=bis[0:1, 2:3],
                                                    op0=ALU.mult, op1=ALU.add))
                mk_mid()

            def _fin():
                if bstate["nv"] is not None:
                    VE.wait_ge(sv, bstate["nv"])
                ins_ = VE.tensor_copy(bis[0:1, 7:8], bis[0:1, 0:1])
                bstate["nv"] = vch(ins_)
                marks["bisdone"] = bstate["nv"]
            bg(_fin)

        def drain_bg(k):
            npop = min(k, len(bg_ops))
            for _ in range(npop):
                bg_ops.pop(0)()

        if "attn" in phases:
            with nc.sbuf_tensor("ktb", [65, 2 * 512], BF16) as ktb, \
                 nc.sbuf_tensor("vtb", [128, 2 * 4 * 65], BF16) as vtb, \
                 nc.sbuf_tensor("expb", [128, 2 * 1024], BF16) as expb, \
                 nc.sbuf_tensor("recr", [1, 2 * 512], F32) as recr, \
                 nc.sbuf_tensor("recrep", [128, 512], F32) as recrep, \
                 nc.sbuf_tensor("masks", [128, 4 * 512], BF16) as masks, \
                 nc.sbuf_tensor("mscr", [128, 512], F32) as mscr:

                GP.wait_ge(sem.get("womm"), sem.val("womm"))
                for d_ in range(4):
                    if d_ > 0:
                        GP.wait_ge(sem.get("mskdone"), d_)
                    s_mio = sem.inc("mskio", GP.iota(
                        mscr[:, :], pattern=[[1, 512]], base=-128 * d_, channel_multiplier=-1,
                        allow_small_or_imprecise_dtypes=True))
                    VE.wait_ge(sem.get("mskio"), s_mio)
                    vm = VE.tensor_scalar(out=masks[:, 512 * d_:512 * (d_ + 1)], in0=mscr[:, :],
                                          scalar1=0.0, scalar2=None, op0=ALU.is_ge)
                    sem.inc("mskdone", vm)
                for sl in range(2):
                    for j in range(4):
                        VE.memset(vtb[:, 260 * sl + 65 * j + 64:260 * sl + 65 * j + 65], 1.0)
                sem.inc("ktones", VE.memset(ktb[64:65, :], 1.0))

                engs = [TE, VE, SE, GP, SY]
                regs = nc.alloc_registers("role_r", engines=[e.engine for e in engs])
                for eng in engs:
                    eng.wait_ge(sem.get("role"), 16)
                nc.regs_load(regs, role_sb[0:1, 0:1])
                TE.wait_ge(sv, marks["qbias"])
                sem.wait(TE, "ktones")
                sem.wait(TE, "mskdone")
                TE.wait_ge(sem.get("womm"), sem.val("womm"))
                SY.wait_ge(sem.get("womm"), sem.val("womm"))
                SY.wait_ge(sem.get("cc"), 2)
                GP.wait_ge(sem.get("cc"), 2)
                SE.wait_ge(sem.get("woev"), sem.val("woev"))

                base_sems = dict(sem.n)
                base_bg = list(bg_ops)
                arm_end = None
                for arm_role in range(NCORE):
                    sem.n = dict(base_sems)
                    bg_ops.clear()
                    bg_ops.extend(base_bg)
                    if "comm" in phases:
                        bstate["nv"] = None
                    with nc.If_cmp(regs, arm_role, "IS_EQ"):
                        emit_attention(nc, sem, arm_role, qt, ktb, vtb, expb, recr,
                                       recrep, masks, ctxd, k_all, v_all, ps, bank,
                                       TE, VE, SE, GP, SY, ones1x64f, drain_bg)
                    nc.end_ifs()
                    if arm_end is None:
                        arm_end = dict(sem.n)
                    else:
                        diff = {k: (arm_end.get(k), sem.n.get(k))
                                for k in set(arm_end) | set(sem.n)
                                if arm_end.get(k) != sem.n.get(k)}
                        assert not diff, f"non-uniform arm semaphores: {diff}"
                sem.n = dict(arm_end)
                bg_ops.clear()

                if "ctxd" in dbg:
                    SE.wait_ge(sem.get("cdone"), sem.val("cdone"))
                    sem.inc("dbg", SE.dma_start(out=dbg["ctxd"][:, :], in_=ctxd[:, :]), 16)
        else:
            drain_bg(10 ** 9)

        if "comm" in phases:
            drain_bg(10 ** 9)
            SY.wait_ge(sv, marks["bisdone"])
            sem.inc("scrow", SY.dma_start(out=sc_row[:, :], in_=scown[:, :]), 16)
            VE.wait_ge(sem.get("scrow"), 16)
            VE.wait_ge(sv, sem.val("v"))
            nv = vch(VE.tensor_scalar(out=mask_row[:, :], in0=sc_row[:, :],
                                      scalar1=bis[0:1, 0:1], scalar2=None, op0=ALU.is_gt))
            VE.wait_ge(sv, nv)
            nv = vch(VE.tensor_tensor_scan(out=pos_row[:, :], data0=mask_row[:, :],
                                           data1=zrow[:, :], initial=0.0,
                                           op0=ALU.add, op1=ALU.add))
            VE.wait_ge(sv, nv)
            nv = vch(VE.tensor_sub(out=pos_row[:, :], in0=pos_row[:, :], in1=mask_row[:, :]))
            VE.wait_ge(sv, nv)
            nv = vch(VE.tensor_scalar(out=pos_row[:, :], in0=pos_row[:, :],
                                      scalar1=-BIG, scalar2=None, op0=ALU.add))
            VE.wait_ge(sv, nv)
            nv = vch(VE.tensor_mul(out=pos_row[:, :], in0=pos_row[:, :], in1=mask_row[:, :]))
            VE.wait_ge(sv, nv)
            vps = VE.tensor_scalar(out=pos_row[:, :], in0=pos_row[:, :],
                                   scalar1=BIG, scalar2=None, op0=ALU.add)
            nv = vch(vps)
            marks["posrow"] = nv
            SY.wait_ge(sv, nv)
            sem.inc("pos128", SY.dma_start(out=pos128[:, :], in_=pos_row[:, :]), 16)
            sem.inc("m128", SY.dma_start(out=mask128[:, :], in_=mask_row[:, :]), 16)
            SE.wait_ge(sem.get("scrow"), 16)
            sgm = SE.activation(out=sig_sb[:, :], in_=scown[:, :], func=AF.Sigmoid,
                                bias=rbcol[:, 0:1])
            sem.inc("sig", sgm)
            VE.wait_ge(sem.get("sig"), 1)
            VE.wait_ge(sem.get("m128"), 16)
            vsm = VE.tensor_mul(out=sigmask[:, :], in0=sig_sb[:, :], in1=mask128[:, :])
            marks["sigmask"] = vch(vsm)
            if "bis" in dbg:
                SE.wait_ge(sv, marks["bisdone"])
                sem.inc("dbg", SE.dma_start(out=dbg["bis"][:, :], in_=bis[:, :]), 16)
            if "posrow" in dbg:
                SE.wait_ge(sv, marks["posrow"])
                sem.inc("dbg", SE.dma_start(out=dbg["posrow"][:, :], in_=pos_row[:, :]), 16)
            if "sigmask" in dbg:
                SE.wait_ge(sv, marks["sigmask"])
                sem.inc("dbg", SE.dma_start(out=dbg["sigmask"][:, :], in_=sigmask[:, :]), 16)

        bis_stack.close()
        qt_stack.close()

        if "ffn" in phases:
            with nc.sbuf_tensor("PT_sb", [128, NJC * TOK], BF16) as PT_sb, \
                 nc.sbuf_tensor("proc", [128, NJC * D], BF16) as proc:
                emit_ffn(nc, sem, marks, dbg, ps, bank, TE, VE, SE, GP, SY,
                         xn_tok, pos128, pos_row, sigmask,
                         ones_bf, ones_f32, b2_r, w1, b1, w2, sv, ss, vch, sch,
                         PT_sb, proc)
                if "epi" in phases:
                    emit_epilogue(nc, sem, marks, dbg, ps, bank, TE, VE, SE, GP, SY,
                                  x_in, y_out, stats, ctxd, woe, boeff_r, sigmask,
                                  ones_f32, sv, ss, vch, sch, PT_sb, proc)

        if "epi" not in phases or "ffn" not in phases:
            yz = sb("yz", [128, D], F32)
            VE.wait_ge(sv, sem.val("v"))
            nv = vch(VE.memset(yz[:, :], 0.0))
            SY.wait_ge(sv, nv)
            for tt in range(NT):
                sem.inc("yout", SY.dma_start(out=y_out[128 * tt:128 * (tt + 1), :],
                                             in_=yz[:, :]), 16)

        SY.wait_ge(sem.get("yout"), sem.val("yout"))
        if dbg:
            SY.wait_ge(sem.get("dbg"), sem.val("dbg"))
        GP.wait_ge(sem.get("yout"), sem.val("yout"))
    return nc


def emit_attention(nc, sem, role, qt, ktb, vtb, expb, recr, recrep, masks, ctxd,
                   k_all, v_all, ps, bank, TE, VE, SE, GP, SY, ones1x64f, drain_bg):
    import os
    max_groups = int(os.environ.get("MK_ATTN_GROUPS", "999"))
    max_blocks = int(os.environ.get("MK_ATTN_BLOCKS", "999"))
    """One switch arm. Event counts are identical for every role."""
    gidx = 0
    grp = 0
    nkst = 0
    nvst = 0
    pending = []
    qblocks = [(1, min(16 - role, max_blocks)), (0, min(role + 1, max_blocks))]

    def flush_pending():
        while pending:
            pending.pop(0)()

    for qb, nkb in qblocks:
        for g in range(NKV):
            for h2 in range(2):
                if grp >= max_groups:
                    continue
                hA = 4 * g + 2 * h2
                pc = 2 * g + h2
                cb = 4 + 2 * (grp % 2)
                g0 = gidx
                pend_ctx = [None]

                def flush_ctx():
                    if pend_ctx[0] is not None:
                        f = pend_ctx[0]
                        pend_ctx[0] = None
                        f()

                for kb in range(nkb):
                    sl = (gidx // 4) % 2
                    krow, kcol, vrow = kblock_src(kb, g)
                    blk = gidx // 4
                    if blk >= 2:
                        SY.wait_ge(sem.get("ctec"), 4 * blk - 4)
                        GP.wait_ge(sem.get("ctec"), 4 * blk - 4)
                    nkst = sem.inc("kst", SY.dma_start(
                        out=ktb[0:64, 512 * sl:512 * (sl + 1)],
                        in_=k_all[krow:krow + KV, kcol:kcol + 512]), 16)
                    nvst = sem.inc("vst", GP.dma_start(
                        out=vtb.rearrange("p (s a d) -> p s a d", s=2, a=4)[:, sl, :, 0:64],
                        in_=v_all[vrow:vrow + 512, :].rearrange(
                            "(a p) d -> p a d", p=128)), 16)
                    diag = (kb == nkb - 1)
                    TE.wait_ge(sem.get("kst"), nkst)
                    TE.wait_ge(sem.get("vst"), nvst)
                    for i4 in range(4):
                        sb_ = 2 * (gidx % 2)
                        if gidx >= 2:
                            TE.wait_ge(sem.get("expdone"), gidx - 1)
                        for hh in range(2):
                            mm = TE.matmul(
                                bank(sb_ + hh),
                                ktb[:, 512 * sl + 128 * i4:512 * sl + 128 * (i4 + 1)],
                                qt[:, TOK * (hA + hh) + 512 * qb:
                                   TOK * (hA + hh) + 512 * (qb + 1)],
                                start=True, stop=True)
                        sem.inc("scmm", mm)
                        flush_ctx()
                        SE.wait_ge(sem.get("scmm"), gidx + 1)
                        if gidx >= 2:
                            SE.wait_ge(sem.get("ctec"), gidx - 1)
                        ex = SE.activation(out=expb[:, 1024 * (gidx % 2):1024 * (gidx % 2 + 1)],
                                           in_=ps[:, 512 * sb_:512 * sb_ + 1024],
                                           func=AF.Exp, scale=0.125)
                        sem.inc("expdone", ex)
                        if diag:
                            VE.wait_ge(sem.get("expdone"), gidx + 1)
                            for hh in range(2):
                                vm = VE.tensor_mul(
                                    out=expb[:, 1024 * (gidx % 2) + 512 * hh:
                                             1024 * (gidx % 2) + 512 * (hh + 1)],
                                    in0=expb[:, 1024 * (gidx % 2) + 512 * hh:
                                             1024 * (gidx % 2) + 512 * (hh + 1)],
                                    in1=masks[:, 512 * i4:512 * (i4 + 1)])
                                sem.inc("mskmul", vm)
                        first = (kb == 0 and i4 == 0)
                        last = (kb == nkb - 1 and i4 == 3)
                        g_cur = gidx
                        sl_cur = sl
                        i4_cur = i4
                        grp_cur = grp
                        cb_cur = cb
                        diag_cur = diag
                        mskval = sem.val("mskmul")
                        vstval = nvst

                        def emit_ctx(g_cur=g_cur, sl_cur=sl_cur, i4_cur=i4_cur,
                                     grp_cur=grp_cur, cb_cur=cb_cur, first=first,
                                     last=last, diag_cur=diag_cur, mskval=mskval,
                                     vstval=vstval):
                            if diag_cur:
                                TE.wait_ge(sem.get("mskmul"), mskval)
                            else:
                                TE.wait_ge(sem.get("expdone"), g_cur + 1)
                            if first and grp_cur >= 2:
                                TE.wait_ge(sem.get("cdone"), grp_cur - 1)
                            for hh in range(2):
                                cm = TE.matmul(
                                    bank(cb_cur + hh, w=512, pn=65),
                                    vtb[:, 260 * sl_cur + 65 * i4_cur:
                                        260 * sl_cur + 65 * (i4_cur + 1)],
                                    expb[:, 1024 * (g_cur % 2) + 512 * hh:
                                         1024 * (g_cur % 2) + 512 * (hh + 1)],
                                    start=first, stop=last)
                                if last:
                                    sem.inc("ctxg", cm)
                            if last:
                                sem.inc("ctec", TE.nop())
                            else:
                                sem.inc("ctec", cm)

                        pend_ctx[0] = emit_ctx
                        gidx += 1
                        if gidx == g0 + 2:
                            flush_pending()
                flush_ctx()
                pending.append(make_group_epilogue(
                    nc, sem, grp, qb, pc, cb, qt, recr, recrep, ctxd, ps, bank,
                    TE, VE, SE, ones1x64f))
                grp += 1
                drain_bg(26)
    flush_pending()
    drain_bg(10 ** 9)


def make_group_epilogue(nc, sem, grp, qb, pc, cb, qt, recr, recrep, ctxd, ps, bank,
                        TE, VE, SE, ones1x64f):
    def run():
        col = TOK * pc + 512 * qb
        SE.wait_ge(sem.get("ctxg"), 2 * (grp + 1))
        SE.activation(out=ctxd[0:64, col:col + 512],
                      in_=bank(cb, w=512, pn=64), func=AF.Copy)
        SE.activation(out=ctxd[64:128, col:col + 512],
                      in_=bank(cb + 1, w=512, pn=64), func=AF.Copy)
        SE.activation(out=recr[0:1, 0:512], in_=bank(cb, w=512, p0=64, pn=1),
                      func=AF.Copy)
        c2 = SE.activation(out=recr[0:1, 512:1024], in_=bank(cb + 1, w=512, p0=64, pn=1),
                           func=AF.Copy)
        sem.inc("ctevac", c2)
        TE.wait_ge(sem.get("ctevac"), grp + 1)
        TE.matmul(bank(cb, w=512, pn=64), ones1x64f[:, :], recr[0:1, 0:512],
                  start=True, stop=True)
        rr = TE.matmul(bank(cb, w=512, p0=64, pn=64), ones1x64f[:, :],
                       recr[0:1, 512:1024], start=True, stop=True)
        sem.inc("rrep", rr)
        VE.wait_ge(sem.get("rrep"), grp + 1)
        VE.reciprocal(out=recrep[:, :], in_=bank(cb, w=512))
        vs_ = VE.tensor_mul(out=ctxd[:, col:col + 512], in0=ctxd[:, col:col + 512],
                            in1=recrep[:, :])
        sem.inc("cdone", vs_)
    return run


def emit_ffn(nc, sem, marks, dbg, ps, bank, TE, VE, SE, GP, SY,
             xn_tok, pos128, pos_row, sigmask,
             ones_bf, ones_f32, b2_r, w1, b1, w2, sv, ss, vch, sch,
             PT_sb, proc):
    with nc.sbuf_tensor("hff", [128, 64 * CAP], BF16) as hff:
        P_sb = hff
        with nc.sbuf_tensor("posrep", [128, TOK], F32) as posrep, \
             nc.sbuf_tensor("iota384", [128, CAP], F32) as iota384, \
             nc.sbuf_tensor("jcol", [128, NJC], F32) as jcol:
            GP.wait_ge(sem.get("cdone"), sem.val("cdone"))
            sem.inc("ffio", GP.iota(iota384[:, :], pattern=[[1, CAP]], base=0,
                                    channel_multiplier=0,
                                    allow_small_or_imprecise_dtypes=True))
            for jc in range(NJC):
                sem.inc("ffio", GP.iota(jcol[:, jc:jc + 1], pattern=[[0, 1]],
                                        base=128 * jc, channel_multiplier=1,
                                        allow_small_or_imprecise_dtypes=True))
            TE.wait_ge(sv, marks["posrow"])
            TE.wait_ge(sem.get("cdone"), sem.val("cdone"))
            pos_local = pos_row.rearrange("b (p a) -> b a p", p=128)
            for i in range(2):
                sem.inc("prep", TE.matmul(bank(6 + i), ones_f32[0:1, 0:128],
                                          pos_local[0:1, 4 * i:4 * (i + 1), :],
                                          start=True, stop=True))
            VE.wait_ge(sem.get("prep"), 2)
            VE.wait_ge(sv, sem.val("v"))
            for i in range(2):
                vch(VE.tensor_copy(posrep[:, 512 * i:512 * (i + 1)], bank(6 + i)))
            VE.wait_ge(sem.get("ffio"), 4)
            VE.wait_ge(sem.get("pos128"), 16)
            VE.wait_ge(sv, sem.val("v"))
            for tt in range(NT):
                pb = VE.tensor_scalar(out=P_sb[:, CAP * tt:CAP * (tt + 1)],
                                      in0=iota384[:, :], scalar1=pos128[:, tt:tt + 1],
                                      scalar2=None, op0=ALU.is_equal)
                marks["pbuild"] = vch(pb)
            for jc in range(NJC):
                ptb = VE.tensor_scalar(out=PT_sb[:, TOK * jc:TOK * (jc + 1)],
                                       in0=posrep[:, :], scalar1=jcol[:, jc:jc + 1],
                                       scalar2=None, op0=ALU.is_equal)
                marks["ptbuild"] = vch(ptb)

        with nc.sbuf_tensor("sel", [128, 16 * CAP], BF16) as sel, \
             nc.sbuf_tensor("w1_sb", [128, 2 * 2048], BF16) as w1_sb, \
             nc.sbuf_tensor("b1buf", [1, 1024], BF16) as b1buf:
            TE.wait_ge(sv, marks["pbuild"])
            for ec in range(16):
                bnk = ec % 2
                if ec >= 2:
                    TE.wait_ge(sem.get("selev"), ec - 1)
                for tt in range(NT):
                    mm = TE.matmul(bank(bnk, w=CAP),
                                   xn_tok[:, D * tt + 128 * ec:D * tt + 128 * (ec + 1)],
                                   P_sb[:, CAP * tt:CAP * (tt + 1)],
                                   start=(tt == 0), stop=(tt == NT - 1))
                sem.inc("selmm", mm)
                SE.wait_ge(sem.get("selmm"), ec + 1)
                sev = SE.activation(out=sel[:, CAP * ec:CAP * (ec + 1)],
                                    in_=bank(bnk, w=CAP), func=AF.Copy)
                sem.inc("selev", sev)
            if "sel" in dbg:
                SE.wait_ge(sem.get("selev"), 16)
                sem.inc("dbg", SE.dma_start(out=dbg["sel"][:, :], in_=sel[:, :]), 16)

            TE.wait_ge(sem.get("selev"), 16)
            for fc in range(64):
                if fc == 0:
                    GP.wait_ge(sv, marks["ptbuild"])
                if fc % 8 == 0:
                    GP.wait_ge(sem.get("w1mm"), fc)
                    sem.inc("ld_b1", GP.dma_start(
                        out=b1buf[:, :], in_=b1[0:1, 1024 * (fc // 8):1024 * (fc // 8 + 1)]), 16)
                if fc >= 2:
                    GP.wait_ge(sem.get("w1mm"), fc - 1)
                for half in range(2):
                    sem.inc("ld_w1", GP.dma_start(
                        out=w1_sb[:, 2048 * (fc % 2) + 1024 * half:
                                  2048 * (fc % 2) + 1024 * (half + 1)],
                        in_=w1[128 * fc:128 * (fc + 1),
                               1024 * half:1024 * (half + 1)]), 16)
                bnk = fc % 4
                TE.wait_ge(sem.get("ld_w1"), 32 * (fc + 1))
                TE.wait_ge(sem.get("ld_b1"), 16 * (fc // 8 + 1))
                if fc >= 4:
                    TE.wait_ge(sem.get("hgel"), fc - 3)
                for ec in range(16):
                    TE.matmul(bank(bnk, w=CAP),
                              w1_sb[:, 2048 * (fc % 2) + 128 * ec:
                                    2048 * (fc % 2) + 128 * (ec + 1)],
                              sel[:, CAP * ec:CAP * (ec + 1)],
                              start=(ec == 0), stop=False)
                sem.inc("w1mm", TE.matmul(bank(bnk, w=CAP),
                                          b1buf[0:1, 128 * (fc % 8):128 * (fc % 8 + 1)],
                                          ones_bf[0:1, 0:CAP], start=False, stop=True))
                SE.wait_ge(sem.get("w1mm"), fc + 1)
                hg = SE.activation(out=hff[:, CAP * fc:CAP * (fc + 1)],
                                   in_=bank(bnk, w=CAP), func=AF.Gelu_apprx_tanh)
                sem.inc("hgel", hg)
            if "hff" in dbg:
                SE.wait_ge(sem.get("hgel"), 64)
                sem.inc("dbg", SE.dma_start(out=dbg["hff"][:, :], in_=hff[:, 0:4 * CAP]), 16)

        with nc.sbuf_tensor("w2_sb", [128, 2 * 1024], BF16) as w2_sb:
            for halfe in range(2):
                for fc in range(64):
                    nW = 64 * halfe + fc
                    if nW == 0:
                        GP.wait_ge(sem.get("w1mm"), 64)
                    if nW >= 2:
                        GP.wait_ge(sem.get("w2mm"), 6 * (nW - 1))
                    sem.inc("ld_w2", GP.dma_start(
                        out=w2_sb[:, 1024 * (fc % 2):1024 * (fc % 2 + 1)],
                        in_=w2[128 * fc:128 * (fc + 1), 1024 * halfe:1024 * (halfe + 1)]), 16)
                    TE.wait_ge(sem.get("ld_w2"), 16 * (nW + 1))
                    if fc == 0:
                        TE.wait_ge(sem.get("hgel"), 64)
                        if halfe == 1:
                            TE.wait_ge(sem.get("procev"), 6)
                    for tj in range(NJC):
                        for eq in range(2):
                            mm = TE.matmul(bank(2 * tj + eq, w=512),
                                           hff[:, CAP * fc + 128 * tj:CAP * fc + 128 * (tj + 1)],
                                           w2_sb[:, 1024 * (fc % 2) + 512 * eq:
                                                 1024 * (fc % 2) + 512 * (eq + 1)],
                                           start=(fc == 0), stop=False)
                    sem.inc("w2mm", mm, 6)
                for tj in range(NJC):
                    for eq in range(2):
                        sem.inc("w2bias", TE.matmul(
                            bank(2 * tj + eq, w=512), ones_bf[0:1, 0:128],
                            b2_r[0:1, 1024 * halfe + 512 * eq:1024 * halfe + 512 * (eq + 1)],
                            start=False, stop=True))
                SE.wait_ge(sem.get("w2bias"), 6 * (halfe + 1))
                for tj in range(NJC):
                    for eq in range(2):
                        pv = SE.activation(
                            out=proc[:, D * tj + 1024 * halfe + 512 * eq:
                                     D * tj + 1024 * halfe + 512 * (eq + 1)],
                            in_=bank(2 * tj + eq, w=512), func=AF.Copy)
                        sem.inc("procev", pv)
            if "proc" in dbg:
                SE.wait_ge(sem.get("procev"), 12)
                sem.inc("dbg", SE.dma_start(out=dbg["proc"][:, :], in_=proc[:, :]), 16)


def emit_epilogue(nc, sem, marks, dbg, ps, bank, TE, VE, SE, GP, SY,
                  x_in, y_out, stats, ctxd, woe, boeff_r, sigmask,
                  ones_f32, sv, ss, vch, sch, PT_sb, proc):
    with nc.sbuf_tensor("xep", [128, 2 * D], F32) as xep, \
         nc.sbuf_tensor("gep", [128, D], F32) as gep, \
         nc.sbuf_tensor("uep", [128, D], F32) as uep, \
         nc.sbuf_tensor("oep", [128, 2 * D], F32) as oep:
        SY.wait_ge(sem.get("procev"), 12)
        gsil_marks = []
        for tt in range(2):
            sem.inc("ld_xe", SY.dma_start(out=xep[:, D * tt:D * (tt + 1)],
                                          in_=x_in[128 * tt:128 * (tt + 1), :]), 16)
        TE.wait_ge(sem.get("procev"), 12)
        TE.wait_ge(sv, marks["ptbuild"])
        sem.wait(TE, "cdone")
        TE.wait_ge(sv, marks["boeff"])
        for tt in range(NT):
            slot = tt % 2
            if tt >= 1:
                TE.wait_ge(ss, gsil_marks[tt - 1])
            for eq in range(4):
                for jc in range(NJC):
                    mm = TE.matmul(bank(eq),
                                   PT_sb[:, TOK * jc + 128 * tt:TOK * jc + 128 * (tt + 1)],
                                   proc[:, D * jc + 512 * eq:D * jc + 512 * (eq + 1)],
                                   start=(jc == 0), stop=(jc == NJC - 1))
            sem.inc("scat", mm)
            VE.wait_ge(sem.get("ld_xe"), 16 * (tt + 1))
            VE.wait_ge(sem.get("scat"), tt + 1)
            VE.wait_ge(sv, marks["sigmask"])
            if tt >= 1:
                VE.wait_ge(ss, gsil_marks[tt - 1])
            nv = vch(VE.tensor_scalar(out=uep[:, :], in0=xep[:, D * slot:D * (slot + 1)],
                                      scalar1=stats[:, tt:tt + 1],
                                      scalar2=stats[:, NT + tt:NT + tt + 1],
                                      op0=ALU.mult, op1=ALU.subtract))
            VE.wait_ge(sv, nv)
            nv = vch(VE.scalar_tensor_tensor(out=uep[:, :], in0=ps[:, 0:2048],
                                             scalar=sigmask[:, tt:tt + 1], in1=uep[:, :],
                                             op0=ALU.mult, op1=ALU.add))
            SE.wait_ge(sv, nv)
            gs = SE.activation(out=gep[:, :], in_=uep[:, :], func=AF.Silu)
            gsil_marks.append(sch(gs))
            for eq in range(4):
                sb_ = 4 + eq % 2
                n_s = 4 * tt + eq
                if n_s >= 2:
                    TE.wait_ge(sem.get("sev"), n_s - 1)
                TE.wait_ge(ss, gsil_marks[tt])
                for pcc in range(8):
                    TE.matmul(bank(sb_),
                              ctxd[:, TOK * pcc + 128 * tt:TOK * pcc + 128 * (tt + 1)],
                              woe[:, D * pcc + 512 * eq:D * pcc + 512 * (eq + 1)],
                              start=(pcc == 0), stop=False)
                sem.inc("smm", TE.matmul(bank(sb_), ones_f32[0:1, 0:128],
                                         boeff_r[0:1, 512 * eq:512 * (eq + 1)],
                                         start=False, stop=True))
                VE.wait_ge(sem.get("smm"), n_s + 1)
                VE.wait_ge(ss, gsil_marks[tt])
                if tt >= 2:
                    VE.wait_ge(sem.get("yout"), 16 * (tt - 1))
                nv = vch(VE.tensor_mul(out=oep[:, D * slot + 512 * eq:D * slot + 512 * (eq + 1)],
                                       in0=bank(sb_), in1=gep[:, 512 * eq:512 * (eq + 1)]))
                VE.wait_ge(sv, nv)
                va = VE.tensor_add(out=oep[:, D * slot + 512 * eq:D * slot + 512 * (eq + 1)],
                                   in0=oep[:, D * slot + 512 * eq:D * slot + 512 * (eq + 1)],
                                   in1=xep[:, D * slot + 512 * eq:D * slot + 512 * (eq + 1)])
                sem.inc("sev", va)
            SY.wait_ge(sem.get("sev"), sem.val("sev"))
            sem.inc("yout", SY.dma_start(out=y_out[128 * tt:128 * (tt + 1), :],
                                         in_=oep[:, D * slot:D * (slot + 1)]), 16)
            if tt + 2 < NT:
                SY.wait_ge(sem.get("sev"), sem.val("sev"))
                sem.inc("ld_xe", SY.dma_start(out=xep[:, D * slot:D * (slot + 1)],
                                              in_=x_in[128 * (tt + 2):128 * (tt + 3), :]), 16)


def _pack_lhsT(w):
    w = np.asarray(w, dtype=np.float32)
    E, F = w.shape
    nf = F // 128
    na = E // 128
    t = w.reshape(na, 128, nf, 128).transpose(2, 1, 0, 3).reshape(nf * 128, E)
    return np.ascontiguousarray(t)


def _prep_inputs(core, **inp):
    X = np.ascontiguousarray(np.asarray(inp["L0_attn_x"], dtype=np.float32).reshape(S, D))
    r = core
    xo = np.concatenate([X[512 * r:512 * (r + 1)], X[512 * (15 - r):512 * (16 - r)]], axis=0)
    m = {
        "x_own": xo,
        "role": np.array([[r]], dtype=np.int32),
        "Wqp": _pack_lhsT(inp["L0_attn_Wq"]), "bq": np.asarray(inp["L0_attn_bq"]).reshape(1, -1),
        "Wlp": _pack_lhsT(inp["L0_attn_Wl"]), "bl": np.asarray(inp["L0_attn_bl"]).reshape(1, -1),
        "Wk": inp["L0_attn_Wk"], "bk": np.asarray(inp["L0_attn_bk"]).reshape(1, -1),
        "Wv": inp["L0_attn_Wv"], "bv": np.asarray(inp["L0_attn_bv"]).reshape(1, -1),
        "Wq2kv": inp["L0_attn_Wq2kv"], "Wkv2h": inp["L0_attn_Wkv2h"],
        "bkv2h": np.asarray(inp["L0_attn_bkv2h"]).reshape(-1, 1),
        "Wo": inp["L0_attn_Wo"], "bo": np.asarray(inp["L0_attn_bo"]).reshape(1, -1),
        "rw": np.asarray(inp["L0_ffn_router_weight"]).reshape(1, -1),
        "rb": np.asarray(inp["L0_ffn_router_bias"]).reshape(1, 1),
        "w1p": _pack_lhsT(inp["L0_ffn_w1"]), "b1": np.asarray(inp["L0_ffn_b1"]).reshape(1, -1),
        "w2": inp["L0_ffn_w2"], "b2": np.asarray(inp["L0_ffn_b2"]).reshape(1, -1),
    }
    return {k: np.ascontiguousarray(np.asarray(v, dtype=(np.int32 if k == "role" else np.float32)))
            for k, v in m.items()}


def kernel(**inputs):
    nc = build(DEBUG, PHASES)
    in_maps = [_prep_inputs(c, **inputs) for c in range(NCORE)]
    res = run_bass_kernel_spmd(nc, in_maps, core_ids=list(range(NCORE)))
    Y = np.empty((S, D), dtype=np.float32)
    for r in range(NCORE):
        yr = np.asarray(res.results[r]["y"])
        Y[512 * r:512 * (r + 1)] = yr[0:512]
        Y[512 * (15 - r):512 * (16 - r)] = yr[512:1024]
    return Y.reshape(1, S, D)


def kernel_dev(debug, phases, **inputs):
    nc = build(debug, phases)
    in_maps = [_prep_inputs(c, **inputs) for c in range(NCORE)]
    res = run_bass_kernel_spmd(nc, in_maps, core_ids=list(range(NCORE)))
    return res


# version 10
# speedup vs baseline: 1.0282x; 1.0282x over previous
from contextlib import ExitStack

import numpy as np
import ml_dtypes

import concourse.bass as bass
import concourse.mybir as mybir
from concourse.bass_utils import run_bass_kernel_spmd

F32 = mybir.dt.float32
BF16 = mybir.dt.bfloat16
I32 = mybir.dt.int32
AF = mybir.ActivationFunctionType
ALU = mybir.AluOpType
AX = mybir.AxisListType

S = 8192
D = 2048
NH = 16
NKV = 4
KV = 64
LAT = 1024
DFF = 8192
KSEL = 2048
NCORE = 8
TOK = 1024
NT = 8
CAP = 384
NJC = 3
EPS = 1e-5
BIG = 50000.0
NITER_BIS = 30

DEBUG = ()
PHASES = ("p1", "comm", "attn", "ffn", "epi")


def kblock_src(kb, g):
    rb = kb if kb < 8 else 15 - kb
    loc = 0 if kb < 8 else 512
    krow = NKV * KV * rb + KV * g
    kcol = loc
    vrow = NKV * TOK * rb + TOK * g + loc
    return krow, kcol, vrow


class Sems:
    def __init__(self, nc, stack):
        self.nc = nc
        self.stack = stack
        self.h = {}
        self.n = {}

    def get(self, name):
        if name not in self.h:
            self.h[name] = self.stack.enter_context(self.nc.semaphore(name))
        if name not in self.n:
            self.n[name] = 0
        return self.h[name]

    def inc(self, name, instr, k=1):
        instr.then_inc(self.get(name), k)
        self.n[name] += k
        return self.n[name]

    def val(self, name):
        self.get(name)
        return self.n[name]

    def wait(self, eng, name, v=None):
        eng.wait_ge(self.get(name), self.n[name] if v is None else v)


def build(debug=(), phases=PHASES):
    nc = bass.Bass("TRN2", target_bir_lowering=False)
    TE, VE, SE, GP, SY = nc.tensor, nc.vector, nc.scalar, nc.gpsimd, nc.sync

    def din(name, shape, dt=F32):
        return nc.dram_tensor(name, shape, dt, kind="ExternalInput").ap()

    x_in = din("x_own", [TOK, D])
    role_in = din("role", [1, 1], I32)
    Wq = din("Wqp", [16 * 128, D]); bq = din("bq", [1, D])
    Wl = din("Wlp", [8 * 128, D]); bl = din("bl", [1, LAT])
    Wk = din("Wk", [LAT, NKV * KV]); bk = din("bk", [1, NKV * KV])
    Wv = din("Wv", [LAT, NKV * KV]); bv = din("bv", [1, NKV * KV])
    Wq2kv = din("Wq2kv", [128, KV])
    Wkv2h = din("Wkv2h", [KV, 128]); bkv2h = din("bkv2h", [128, 1])
    Wo = din("Wo", [D, D]); bo = din("bo", [1, D])
    rw = din("rw", [1, D]); rb = din("rb", [1, 1])
    w1 = din("w1p", [64 * 128, D]); b1 = din("b1", [1, DFF])
    w2 = din("w2", [DFF, D]); b2 = din("b2", [1, D])
    y_out = nc.dram_tensor("y", [TOK, D], F32, kind="ExternalOutput").ap()

    dbg_specs = {
        "xn_tok": ([128, NT * D], BF16),
        "scores": ([128, NT], F32),
        "qt": ([65, NH * TOK], BF16),
        "lat": ([128, NT * LAT], BF16),
        "s_all": ([NCORE * 1026, 1], F32),
        "bis": ([1, 8], F32),
        "posrow": ([1, TOK], F32),
        "ctxd": ([128, NT * TOK], BF16),
        "sel": ([128, 16 * CAP], BF16),
        "proc": ([128, NJC * D], BF16),
        "woe": ([128, NT * D], BF16),
        "hff": ([128, 4 * CAP], BF16),
        "boeff": ([1, D], F32),
        "sigmask": ([128, NT], F32),
    }
    dbg = {}
    for name in debug:
        shp, dt = dbg_specs[name]
        dbg[name] = nc.dram_tensor("dbg_" + name, shp, dt, kind="ExternalOutput").ap()

    ksend = nc.dram_tensor("ksend", [NKV * KV, TOK], BF16).ap()
    k_all = nc.dram_tensor("k_all", [NCORE * NKV * KV, TOK], BF16).ap()
    vsend = nc.dram_tensor("vsend", [NKV * TOK, KV], BF16).ap()
    v_all = nc.dram_tensor("v_all", [NCORE * NKV * TOK, KV], BF16).ap()
    ssend = nc.dram_tensor("ssend", [1026, 1], F32).ap()
    s_all = nc.dram_tensor("s_all", [NCORE * 1026, 1], F32).ap()
    xn_dram = nc.dram_tensor("xn_dram", [TOK, D], BF16).ap()
    wkvT_dram = nc.dram_tensor("wkvT_dram", [KV, 128], BF16).ap()

    est = ExitStack()
    with est:
        sem = Sems(nc, est)

        def sb(name, shape, dt, **kw):
            return est.enter_context(nc.sbuf_tensor(name, shape, dt, **kw))

        xn_tok = sb("xn_tok", [128, NT * D], BF16)
        ctxd = sb("ctxd", [128, NT * TOK], BF16)
        stats = sb("stats", [128, 2 * NT], F32)
        scown = sb("scown", [128, NT], F32)
        sig_sb = sb("sig_sb", [128, NT], F32)
        sigmask = sb("sigmask", [128, NT], F32)
        mask128 = sb("mask128", [128, NT], F32)
        ones_bf = sb("ones_bf", [1, 512], BF16)
        ones_f32 = sb("ones_f32", [1, 128], F32)
        ones64 = sb("ones64", [64, 1], F32)
        ones128 = sb("ones128", [128, 1], F32)
        ones1x64f = sb("ones1x64f", [1, 64], F32)
        rbcol = sb("rbcol", [128, 1], F32)
        kmax2 = sb("kmax2", [1, 2], F32)
        wq2kv_sb = sb("wq2kv_sb", [128, KV], BF16)
        wkvT_sb = sb("wkvT_sb", [128, KV], BF16)
        bkvcol = sb("bkvcol", [128, 1], BF16)
        role_sb = sb("role_sb", [1, 1], I32)
        rb_sb = sb("rb_sb", [1, 1], F32)
        bq_r = sb("bq_r", [1, D], BF16)
        bl_r = sb("bl_r", [1, LAT], BF16)
        bk_r = sb("bk_r", [1, NKV * KV], BF16)
        bv_r = sb("bv_r", [1, NKV * KV], BF16)
        b2_r = sb("b2_r", [1, D], BF16)
        boeff_r = sb("boeff_r", [1, D], F32)
        t32a = sb("t32a", [32, 32], F32)
        t32b = sb("t32b", [32, 32], F32)
        bis = sb("bis", [1, 8], F32)
        z32 = sb("z32", [1, 32], F32)
        k8 = sb("k8", [1, 8], F32)
        pos128 = sb("pos128", [128, NT], F32)
        qt_stack = ExitStack()
        qt = qt_stack.enter_context(nc.sbuf_tensor("qt", [65, NH * TOK], BF16, side="right"))
        bis_stack = ExitStack()
        scg = bis_stack.enter_context(nc.sbuf_tensor("scg", [32, 256], F32, side="right"))
        cmpb = bis_stack.enter_context(nc.sbuf_tensor("cmpb", [32, 256], F32, side="right"))
        sc_row = sb("sc_row", [1, TOK], F32)
        mask_row = sb("mask_row", [1, TOK], F32)
        pos_row = sb("pos_row", [1, TOK], F32)
        zrow = sb("zrow", [1, TOK], F32)


        ps = est.enter_context(nc.psum_tensor("ps", [128, 4096], F32))

        def bank(b, w=512, p0=0, pn=128):
            return ps[p0:p0 + pn, 512 * b:512 * b + w]

        sv = sem.get("v")
        ss = sem.get("s")
        marks = {}

        def vch(instr):
            return sem.inc("v", instr)

        def sch(instr):
            return sem.inc("s", instr)

        sem.inc("role", SY.dma_start(out=role_sb[:, :], in_=role_in[:, :]), 16)
        sem.inc("ld_rb", SY.dma_start(out=rb_sb[:, :], in_=rb[:, :]), 16)

        for src, dst in ((bq, bq_r), (bl, bl_r), (bk, bk_r), (bv, bv_r), (b2, b2_r)):
            sem.inc("ld_bias", GP.dma_start(out=dst[:, :], in_=src[:, :]), 16)
        sem.inc("ld_wq2kv", GP.dma_start(out=wq2kv_sb[:, :], in_=Wq2kv[:, :]), 16)
        sem.inc("ld_bkv", GP.dma_start(out=bkvcol[:, :], in_=bkv2h[:, :]), 16)
        s_wkv0 = sem.inc("ld_wkvT0", GP.dma_start(out=wkvT_dram[:, :], in_=Wkv2h[:, :]), 16)
        SY.wait_ge(sem.get("ld_wkvT0"), s_wkv0)
        sem.inc("ld_wkvT", SY.dma_start_transpose(out=wkvT_sb[:, 0:KV], in_=wkvT_dram[:, :]), 16)

        VE.memset(ones_bf[:, :], 1.0)
        VE.memset(ones_f32[:, :], 1.0)
        VE.memset(ones64[:, :], 1.0)
        VE.memset(ones128[:, :], 1.0)
        VE.memset(ones1x64f[:, :], 1.0)
        VE.memset(zrow[:, :], 0.0)
        VE.memset(z32[:, :], 0.0)
        VE.memset(t32a[:, :], 0.0)
        VE.memset(t32b[:, :], 0.0)
        sem.inc("const", VE.memset(bis[:, :], 0.0))

        with nc.sbuf_tensor("w_rep", [128, D], F32) as w_rep, \
             nc.sbuf_tensor("rw_r", [1, D], F32, side="right") as rw_r, \
             nc.sbuf_tensor("xst", [128, 2 * D], F32) as xst, \
             nc.sbuf_tensor("xnf", [128, 2 * D], F32) as xnf:

            sem.inc("ld_rw", SY.dma_start(out=rw_r[:, :], in_=rw[:, :]), 16)
            for tt in range(2):
                sem.inc("ld_x", SY.dma_start(out=xst[:, D * tt:D * (tt + 1)],
                                             in_=x_in[128 * tt:128 * (tt + 1), :]), 16)

            sem.wait(TE, "ld_rw")
            sem.wait(TE, "ld_rb")
            sem.wait(TE, "const")
            for i in range(4):
                sem.inc("wrepmm", TE.matmul(bank(i), ones_f32[0:1, 0:128],
                                            rw_r[0:1, 512 * i:512 * (i + 1)],
                                            start=True, stop=True))
            sem.inc("wrepmm", TE.matmul(bank(4, w=1), ones_f32[0:1, 0:128],
                                        rb_sb[0:1, 0:1], start=True, stop=True))
            VE.wait_ge(sem.get("wrepmm"), sem.val("wrepmm"))
            for i in range(4):
                vch(VE.tensor_copy(w_rep[:, 512 * i:512 * (i + 1)], bank(i)))
            sem.inc("wrep", VE.tensor_copy(rbcol[:, :], bank(4, w=1)))

            sem.wait(VE, "wrep")
            cast_s = []
            for tt in range(NT):
                slot = tt % 2
                xs = xst[:, D * slot:D * (slot + 1)]
                xf = xnf[:, D * slot:D * (slot + 1)]
                m_c = sig_sb[:, tt:tt + 1]
                v_c = pos128[:, tt:tt + 1]
                VE.wait_ge(sem.get("ld_x"), 16 * (tt + 1))
                if tt >= 2:
                    VE.wait_ge(ss, cast_s[tt - 2])
                nv = vch(VE.scalar_tensor_tensor(out=xf, in0=xs, scalar=1.0, in1=xs,
                                                 op0=ALU.mult, op1=ALU.mult,
                                                 accum_out=stats[:, tt:tt + 1]))
                nv = vch(VE.tensor_reduce(out=stats[:, NT + tt:NT + tt + 1], in_=xs,
                                          axis=AX.X, op=ALU.add))
                VE.wait_ge(sv, nv)
                nv = vch(VE.tensor_scalar(out=m_c, in0=stats[:, NT + tt:NT + tt + 1],
                                          scalar1=1.0 / D, scalar2=None, op0=ALU.mult))
                VE.wait_ge(sv, nv)
                nv = vch(VE.tensor_scalar(out=v_c, in0=stats[:, tt:tt + 1],
                                          scalar1=1.0 / D, scalar2=None, op0=ALU.mult))
                VE.wait_ge(sv, nv)
                nv = vch(VE.scalar_tensor_tensor(out=v_c, in0=m_c, scalar=m_c, in1=v_c,
                                                 op0=ALU.mult, op1=ALU.subtract))
                VE.wait_ge(sv, nv)
                nv = vch(VE.tensor_scalar(out=v_c, in0=v_c, scalar1=-1.0, scalar2=EPS,
                                          op0=ALU.mult, op1=ALU.add))
                VE.wait_ge(sv, nv)
                nv = vch(VE.reciprocal(out=v_c, in_=v_c))
                SE.wait_ge(sv, nv)
                ns = sch(SE.activation(out=stats[:, tt:tt + 1], in_=v_c, func=AF.Sqrt))
                VE.wait_ge(ss, ns)
                nv = vch(VE.tensor_mul(out=stats[:, NT + tt:NT + tt + 1], in0=m_c,
                                       in1=stats[:, tt:tt + 1]))
                VE.wait_ge(sv, nv)
                nv = vch(VE.tensor_scalar(out=xf, in0=xs, scalar1=stats[:, tt:tt + 1],
                                          scalar2=stats[:, NT + tt:NT + tt + 1],
                                          op0=ALU.mult, op1=ALU.subtract))
                VE.wait_ge(sv, nv)
                nv = vch(VE.scalar_tensor_tensor(out=xs, in0=xf, scalar=1.0,
                                                 in1=w_rep[:, :], op0=ALU.mult, op1=ALU.mult,
                                                 accum_out=scown[:, tt:tt + 1]))
                if tt + 2 < NT:
                    SY.wait_ge(sv, nv)
                    sem.inc("ld_x", SY.dma_start(out=xst[:, D * slot:D * (slot + 1)],
                                                 in_=x_in[128 * (tt + 2):128 * (tt + 3), :]), 16)
                SE.wait_ge(sv, nv)
                si = SE.activation(out=xn_tok[:, D * tt:D * (tt + 1)], in_=xf, func=AF.Copy)
                cast_s.append(sch(si))

            SY.wait_ge(ss, sem.val("s"))
            s_xnw = sem.inc("xnw", SY.dma_start(
                out=xn_dram.rearrange("(a p) d -> p a d", p=128), in_=xn_tok[:, :]), 16)

        with nc.sbuf_tensor("xn_d", [128, 16 * TOK], BF16) as xn_d, \
             nc.sbuf_tensor("qful", [128, 2 * TOK], BF16) as qful, \
             nc.sbuf_tensor("lat_sb", [128, NT * TOK], BF16) as lat_sb, \
             nc.sbuf_tensor("wq_sb", [128, 2 * 2048], BF16) as wq_sb, \
             nc.sbuf_tensor("wk_sb", [128, NT * 256], BF16) as wk_sb, \
             nc.sbuf_tensor("wv_sb", [128, NT * 256], BF16) as wv_sb, \
             nc.sbuf_tensor("qsq", [64, TOK], F32) as qsq, \
             nc.sbuf_tensor("kd_sb", [128, 2 * TOK], BF16) as kd_sb, \
             nc.sbuf_tensor("vs_sb", [128, 256], BF16) as vs_sb, \
             nc.sbuf_tensor("ksq2", [128, TOK], F32) as ksq2:

            SY.wait_ge(sem.get("xnw"), s_xnw)
            for ec in range(16):
                sem.inc("xnd", SY.dma_start_transpose(
                    out=xn_d[:, TOK * ec:TOK * (ec + 1)],
                    in_=xn_dram[:, 128 * ec:128 * (ec + 1)]), 16)

            GP.wait_ge(sem.get("xnw"), s_xnw)
            for dc in range(16):
                if dc >= 2:
                    GP.wait_ge(sem.get("qfmm"), 2 * (dc - 1))
                for half in range(2):
                    sem.inc("ld_wq", GP.dma_start(
                        out=wq_sb[:, 2048 * (dc % 2) + 1024 * half:
                                  2048 * (dc % 2) + 1024 * (half + 1)],
                        in_=Wq[128 * dc:128 * (dc + 1),
                               1024 * half:1024 * (half + 1)]), 16)
            for lc in range(8):
                if lc >= 2:
                    GP.wait_ge(sem.get("latmm"), 2 * (lc - 1))
                else:
                    GP.wait_ge(sem.get("qfmm"), 30 + 2 * lc)
                for half in range(2):
                    sem.inc("ld_wl", GP.dma_start(
                        out=wq_sb[:, 2048 * (lc % 2) + 1024 * half:
                                  2048 * (lc % 2) + 1024 * (half + 1)],
                        in_=Wl[128 * lc:128 * (lc + 1),
                               1024 * half:1024 * (half + 1)]), 16)
            sem.inc("ld_wk", GP.dma_start(
                out=wk_sb[:, :], in_=Wk.rearrange("(a p) d -> p a d", p=128)), 16)
            sem.inc("ld_wv", GP.dma_start(
                out=wv_sb[:, :], in_=Wv.rearrange("(a p) d -> p a d", p=128)), 16)

            TE.wait_ge(sem.get("xnd"), sem.val("xnd"))
            sem.wait(TE, "ld_bias")
            sem.wait(TE, "ld_wq2kv")
            qnve_marks = []
            for h in range(NH):
                slot = h % 2
                b0 = 2 * slot
                TE.wait_ge(sem.get("ld_wq"), 32 * (h + 1))
                if h >= 2:
                    TE.wait_ge(sem.get("qevac"), 2 * (h - 1))
                for th in range(2):
                    for ec in range(16):
                        TE.matmul(bank(b0 + th),
                                  wq_sb[:, 2048 * slot + 128 * ec:2048 * slot + 128 * (ec + 1)],
                                  xn_d[:, TOK * ec + 512 * th:TOK * ec + 512 * (th + 1)],
                                  start=(ec == 0), stop=False)
                    sem.inc("qfmm", TE.matmul(bank(b0 + th), bq_r[0:1, 128 * h:128 * (h + 1)],
                                              ones_bf[0:1, 0:512], start=False, stop=True))
                SE.wait_ge(sem.get("qfmm"), 2 * (h + 1))
                if h >= 2:
                    SE.wait_ge(sem.get("qbmm"), 2 * (h - 1))
                se1 = SE.activation(out=qful[:, TOK * slot:TOK * slot + 512],
                                    in_=bank(b0), func=AF.Copy)
                se2 = SE.activation(out=qful[:, TOK * slot + 512:TOK * (slot + 1)],
                                    in_=bank(b0 + 1), func=AF.Copy)
                sem.inc("qevac", se1)
                sem.inc("qevac", se2)
                TE.wait_ge(sem.get("qevac"), 2 * (h + 1))
                if h >= 1:
                    TE.wait_ge(sem.get("qsqa"), h)
                for th in range(2):
                    sem.inc("qbmm", TE.matmul(bank(4 + th, w=512, pn=64), wq2kv_sb[:, :],
                                              qful[:, TOK * slot + 512 * th:
                                                    TOK * slot + 512 * (th + 1)],
                                              start=True, stop=True))
                SE.wait_ge(sem.get("qbmm"), 2 * (h + 1))
                if h >= 1:
                    SE.wait_ge(sv, qnve_marks[h - 1])
                e1 = SE.activation(out=qt[0:64, TOK * h:TOK * h + 512],
                                   in_=bank(4, pn=64), func=AF.Copy)
                e2 = SE.activation(out=qt[0:64, TOK * h + 512:TOK * (h + 1)],
                                   in_=bank(5, pn=64), func=AF.Copy)
                SE.activation(out=qsq[:, 0:512], in_=bank(4, pn=64), func=AF.Square)
                sq2 = SE.activation(out=qsq[:, 512:1024], in_=bank(5, pn=64), func=AF.Square)
                sem.inc("qtev", e1)
                sem.inc("qtev", e2)
                sem.inc("qsqa", sq2)
                TE.wait_ge(sem.get("qsqa"), h + 1)
                if h >= 1:
                    TE.wait_ge(sv, qnve_marks[h - 1])
                TE.matmul(bank(6, w=512, pn=1), ones64[:, :], qsq[:, 0:512],
                          start=True, stop=True)
                sem.inc("qnmm", TE.matmul(bank(7, w=512, pn=1), ones64[:, :],
                                          qsq[:, 512:1024], start=True, stop=True))
                VE.wait_ge(sem.get("qnmm"), h + 1)
                vch(VE.tensor_scalar(out=qt[64:65, TOK * h:TOK * h + 512],
                                     in0=bank(6, w=512, pn=1),
                                     scalar1=-0.5, scalar2=None, op0=ALU.mult))
                v2 = VE.tensor_scalar(out=qt[64:65, TOK * h + 512:TOK * (h + 1)],
                                      in0=bank(7, w=512, pn=1),
                                      scalar1=-0.5, scalar2=None, op0=ALU.mult)
                qnve_marks.append(vch(v2))
                marks["qnve_last"] = qnve_marks[-1]

            TE.wait_ge(sem.get("qevac"), 2 * NH)
            for lc in range(8):
                slot = lc % 2
                b0 = 2 * slot
                TE.wait_ge(sem.get("ld_wl"), 32 * (lc + 1))
                if lc >= 2:
                    TE.wait_ge(sem.get("latev"), 2 * (lc - 1))
                for th in range(2):
                    for ec in range(16):
                        TE.matmul(bank(b0 + th),
                                  wq_sb[:, 2048 * slot + 128 * ec:2048 * slot + 128 * (ec + 1)],
                                  xn_d[:, TOK * ec + 512 * th:TOK * ec + 512 * (th + 1)],
                                  start=(ec == 0), stop=False)
                    sem.inc("latmm", TE.matmul(bank(b0 + th), bl_r[0:1, 128 * lc:128 * (lc + 1)],
                                               ones_bf[0:1, 0:512], start=False, stop=True))
                SE.wait_ge(sem.get("latmm"), 2 * (lc + 1))
                l1 = SE.activation(out=lat_sb[:, TOK * lc:TOK * lc + 512],
                                   in_=bank(b0), func=AF.Silu)
                l2 = SE.activation(out=lat_sb[:, TOK * lc + 512:TOK * (lc + 1)],
                                   in_=bank(b0 + 1), func=AF.Silu)
                sem.inc("latev", l1)
                sem.inc("latev", l2)

            sem.wait(TE, "ld_wk")
            sem.wait(TE, "ld_wv")
            TE.wait_ge(sem.get("latev"), 16)
            for dt in range(2):
                for th in range(2):
                    b_ = 2 * dt + th
                    for lc in range(8):
                        TE.matmul(bank(b_),
                                  wk_sb[:, 256 * lc + 128 * dt:256 * lc + 128 * (dt + 1)],
                                  lat_sb[:, TOK * lc + 512 * th:TOK * lc + 512 * (th + 1)],
                                  start=(lc == 0), stop=False)
                    sem.inc("kdmm", TE.matmul(bank(b_), bk_r[0:1, 128 * dt:128 * (dt + 1)],
                                              ones_bf[0:1, 0:512], start=False, stop=True))
            VE.wait_ge(sem.get("kdmm"), 4)
            for dt in range(2):
                for th in range(2):
                    kv_ = VE.tensor_copy(kd_sb[:, TOK * dt + 512 * th:TOK * dt + 512 * (th + 1)],
                                         bank(2 * dt + th))
                    sem.inc("kdev", kv_)
            SY.wait_ge(sem.get("kdev"), 4)
            sem.inc("ksend", SY.dma_start(
                out=ksend.rearrange("(a p) t -> p a t", p=128), in_=kd_sb[:, :]), 16)
            for wave in range(2):
                SE.wait_ge(sem.get("kdev"), 4)
                if wave == 1:
                    SE.wait_ge(sem.get("knmm2"), 4)
                ns = sch(SE.activation(out=ksq2[:, :], in_=kd_sb[:, TOK * wave:TOK * (wave + 1)],
                                       func=AF.Square))
                TE.wait_ge(ss, ns)
                if wave == 1:
                    TE.wait_ge(sv, marks["knorm0"])
                for i in range(4):
                    g_half = i // 2
                    th = i % 2
                    sem.inc("knmm2", TE.matmul(
                        bank(4 + i, w=512, p0=64 * g_half, pn=1),
                        ones128[64 * g_half:64 * (g_half + 1), :],
                        ksq2[64 * g_half:64 * (g_half + 1), 512 * th:512 * (th + 1)],
                        start=True, stop=True))
                VE.wait_ge(sem.get("knmm2"), 4 * (wave + 1))
                for i in range(4):
                    g_half = i // 2
                    nv = vch(VE.tensor_reduce(out=k8[0:1, 4 * wave + i:4 * wave + i + 1],
                                              in_=bank(4 + i, w=512, p0=64 * g_half, pn=1),
                                              axis=AX.X, op=ALU.max))
                marks["knorm0"] = nv
            VE.wait_ge(sv, marks["knorm0"])
            nv = vch(VE.tensor_reduce(out=t32a[0:1, 0:1], in_=k8[:, :], axis=AX.X, op=ALU.max))
            for tt in range(NT):
                if tt >= 1:
                    TE.wait_ge(sem.get("kvev"), tt)
                for lc in range(8):
                    TE.matmul(bank(1, w=256),
                              lat_sb[:, TOK * lc + 128 * tt:TOK * lc + 128 * (tt + 1)],
                              wv_sb[:, 256 * lc:256 * (lc + 1)],
                              start=(lc == 0), stop=False)
                sem.inc("kvmm", TE.matmul(bank(1, w=256), ones_bf[0:1, 0:128], bv_r[0:1, :],
                                          start=False, stop=True))
                VE.wait_ge(sem.get("kvmm"), tt + 1)
                if tt >= 1:
                    VE.wait_ge(sem.get("kvsend"), 64 * tt)
                ve2 = VE.tensor_copy(vs_sb[:, :], bank(1, w=256))
                sem.inc("kvev", ve2)
                SY.wait_ge(sem.get("kvev"), tt + 1)
                for g in range(NKV):
                    sem.inc("kvsend", SY.dma_start(
                        out=vsend[TOK * g + 128 * tt:TOK * g + 128 * (tt + 1), :],
                        in_=vs_sb[:, KV * g:KV * (g + 1)]), 16)
            SY.wait_ge(sv, nv)
            sem.inc("ssend", SY.dma_start(
                out=ssend[0:TOK, :].rearrange("(p a) b -> p (a b)", p=128), in_=scown[:, :]), 16)
            sem.inc("ssend", SY.dma_start(out=ssend[1024:1025, :], in_=t32a[0:1, 0:1]), 16)

            if "comm" in phases:
                GP.wait_ge(sem.get("ksend"), sem.val("ksend"))
                GP.wait_ge(sem.get("kvsend"), sem.val("kvsend"))
                GP.wait_ge(sem.get("ssend"), sem.val("ssend"))
                sem.inc("cc", GP.collective_compute(
                    "AllGather", ALU.bypass, replica_groups=[list(range(NCORE))],
                    ins=[ksend.opt()], outs=[k_all.opt()]))
                sem.inc("cc", GP.collective_compute(
                    "AllGather", ALU.bypass, replica_groups=[list(range(NCORE))],
                    ins=[vsend.opt()], outs=[v_all.opt()]))
                sem.inc("cc", GP.collective_compute(
                    "AllGather", ALU.bypass, replica_groups=[list(range(NCORE))],
                    ins=[ssend.opt()], outs=[s_all.opt()]))

            if "xn_tok" in dbg:
                sem.inc("dbg", SE.dma_start(out=dbg["xn_tok"][:, :], in_=xn_tok[:, :]), 16)
            if "scores" in dbg:
                SE.wait_ge(sv, sem.val("v"))
                sem.inc("dbg", SE.dma_start(out=dbg["scores"][:, :], in_=scown[:, :]), 16)
            if "qt" in dbg and "comm" not in phases:
                SE.wait_ge(sv, qnve_marks[-1])
                sem.inc("dbg", SE.dma_start(out=dbg["qt"][:, :], in_=qt[:, :]), 16)
            if "lat" in dbg:
                SE.wait_ge(sem.get("latev"), 16)
                sem.inc("dbg", SE.dma_start(out=dbg["lat"][:, :], in_=lat_sb[:, :]), 16)

        if "comm" in phases:
            SY.wait_ge(sem.get("cc"), 3)
            with nc.allow_non_contiguous_dma(reason="8-element strided kmax gather"):
                sem.inc("kg", SY.dma_start(
                    out=k8[:, :],
                    in_=s_all[1024:NCORE * 1026:1026, 0:1].rearrange("a b -> b a"),
                    allow_slow_non_contiguous=True), 16)
            for r in range(NCORE):
                sem.inc("scg", SY.dma_start(
                    out=scg[:, 32 * r:32 * (r + 1)],
                    in_=s_all[1026 * r:1026 * r + 1024, :].rearrange("(p a) b -> p (a b)", p=32)), 16)
            VE.wait_ge(sem.get("kg"), 16)
            nv = vch(VE.tensor_reduce(out=kmax2[:, 0:1], in_=k8[:, :], axis=AX.X, op=ALU.max))
            VE.wait_ge(sv, nv)
            nv = vch(VE.tensor_scalar(out=kmax2[:, 1:2], in0=kmax2[:, 0:1],
                                      scalar1=-0.5, scalar2=None, op0=ALU.mult))
            VE.wait_ge(sv, nv)
            VE.wait_ge(sv, marks["qnve_last"])
            vqb = VE.tensor_scalar(out=qt[64:65, :], in0=qt[64:65, :],
                                   scalar1=kmax2[0:1, 1:2], scalar2=None, op0=ALU.add)
            marks["qbias"] = vch(vqb)
            if "s_all" in dbg:
                sem.inc("dbg", SY.dma_start(out=dbg["s_all"][:, :], in_=s_all[:, :]), 16)
            if "qt" in dbg:
                SE.wait_ge(sv, marks["qbias"])
                sem.inc("dbg", SE.dma_start(out=dbg["qt"][:, :], in_=qt[:, :]), 16)

        woe = sb("woe", [128, NT * D], BF16)
        if "fold" in phases or "attn" in phases or "epi" in phases:
            with nc.sbuf_tensor("wo_sb", [128, 2 * D], BF16) as wo_sb, \
                 nc.sbuf_tensor("bo_acc", [1, D], F32, side="right") as bo_acc:
                GP.wait_ge(sem.get("kvmm"), NT)
                GP.wait_ge(sem.get("kdmm"), 4)
                for h in range(NH):
                    if h >= 2:
                        GP.wait_ge(sem.get("woev"), h - 1)
                    sem.inc("ld_wo", GP.dma_start(
                        out=wo_sb[:, D * (h % 2):D * (h % 2 + 1)],
                        in_=Wo[128 * h:128 * (h + 1), :]), 16)
                sem.wait(TE, "ld_wkvT")
                sem.wait(TE, "ld_bkv")
                TE.wait_ge(sem.get("kvmm"), NT)
                for h in range(NH):
                    TE.wait_ge(sem.get("ld_wo"), 16 * (h + 1))
                    if h >= 1:
                        TE.wait_ge(sem.get("woev"), h)
                    for eq in range(4):
                        TE.matmul(bank(eq, w=512, pn=64), wkvT_sb[:, :],
                                  wo_sb[:, D * (h % 2) + 512 * eq:D * (h % 2) + 512 * (eq + 1)],
                                  start=True, stop=True)
                        bo_mm = TE.matmul(
                            bank(4 + eq, w=512, pn=1), bkvcol[:, :],
                            wo_sb[:, D * (h % 2) + 512 * eq:D * (h % 2) + 512 * (eq + 1)],
                            start=(h == 0), stop=(h == NH - 1))
                    sem.inc("womm", bo_mm)
                    VE.wait_ge(sem.get("womm"), h + 1)
                    pc = h // 2
                    p0 = 64 * (h % 2)
                    for eq in range(4):
                        wv_ = VE.tensor_copy(
                            woe[p0:p0 + 64, D * pc + 512 * eq:D * pc + 512 * (eq + 1)],
                            bank(eq, w=512, pn=64))
                    sem.inc("woev", wv_)
                sem.inc("ld_bo", SY.dma_start(out=bo_acc[:, :], in_=bo[:, :]), 16)
                VE.wait_ge(sem.get("ld_bo"), sem.val("ld_bo"))
                VE.wait_ge(sem.get("womm"), sem.val("womm"))
                for eq in range(4):
                    bv_ = VE.tensor_add(out=boeff_r[0:1, 512 * eq:512 * (eq + 1)],
                                        in0=bo_acc[0:1, 512 * eq:512 * (eq + 1)],
                                        in1=bank(4 + eq, w=512, pn=1))
                    marks["boeff"] = vch(bv_)
                if "woe" in dbg:
                    SE.wait_ge(sem.get("woev"), NH)
                    sem.inc("dbg", SE.dma_start(out=dbg["woe"][:, :], in_=woe[:, :]), 16)
                if "boeff" in dbg:
                    SE.wait_ge(sv, marks["boeff"])
                    sem.inc("dbg", SE.dma_start(out=dbg["boeff"][:, :], in_=boeff_r[:, :]), 16)

        bg_ops = []
        if "comm" in phases:
            bstate = {"nv": None}

            def bg(f):
                bg_ops.append(f)

            def bgv(make):
                def run():
                    if bstate["nv"] is not None:
                        VE.wait_ge(sv, bstate["nv"])
                    ins_ = make()
                    bstate["nv"] = vch(ins_)
                    return ins_
                bg(run)

            def w_scg():
                VE.wait_ge(sem.get("scg"), sem.val("scg"))
            bg(w_scg)
            bgv(lambda: VE.tensor_reduce(out=t32a[:, 0:1], in_=scg[:, :], axis=AX.X, op=ALU.max))
            bgv(lambda: VE.transpose(out=t32b[:, :], in_=t32a[:, :]))
            bgv(lambda: VE.tensor_reduce(out=bis[0:1, 1:2], in_=t32b[0:1, :], axis=AX.X, op=ALU.max))
            bgv(lambda: VE.tensor_reduce(out=t32a[:, 0:1], in_=scg[:, :], axis=AX.X, op=ALU.min))
            bgv(lambda: VE.transpose(out=t32b[:, :], in_=t32a[:, :]))
            bgv(lambda: VE.tensor_reduce(out=bis[0:1, 0:1], in_=t32b[0:1, :], axis=AX.X, op=ALU.min))
            bgv(lambda: VE.tensor_scalar(out=bis[0:1, 0:1], in0=bis[0:1, 0:1],
                                         scalar1=-1.0, scalar2=None, op0=ALU.add))
            bgv(lambda: VE.tensor_scalar(out=bis[0:1, 1:2], in0=bis[0:1, 1:2],
                                         scalar1=1e-3, scalar2=None, op0=ALU.add))

            def mk_mid():
                bgv(lambda: VE.scalar_tensor_tensor(out=bis[0:1, 2:3], in0=bis[0:1, 0:1],
                                                    scalar=1.0, in1=bis[0:1, 1:2],
                                                    op0=ALU.mult, op1=ALU.add))
                bgv(lambda: VE.tensor_scalar(out=bis[0:1, 2:3], in0=bis[0:1, 2:3],
                                             scalar1=0.5, scalar2=None, op0=ALU.mult))
                bgv(lambda: VE.tensor_scalar(out=t32a[0:1, :], in0=z32[:, :],
                                             scalar1=bis[0:1, 2:3], scalar2=None, op0=ALU.add))
                bgv(lambda: VE.transpose(out=t32b[:, :], in_=t32a[:, :]))
            mk_mid()
            for _ in range(NITER_BIS):
                bgv(lambda: VE.tensor_scalar(out=cmpb[:, :], in0=scg[:, :],
                                             scalar1=t32b[:, 0:1], scalar2=None, op0=ALU.is_gt))
                bgv(lambda: VE.tensor_reduce(out=t32a[:, 0:1], in_=cmpb[:, :],
                                             axis=AX.X, op=ALU.add))
                bgv(lambda: VE.transpose(out=t32b[:, :], in_=t32a[:, :]))
                bgv(lambda: VE.tensor_reduce(out=bis[0:1, 3:4], in_=t32b[0:1, :],
                                             axis=AX.X, op=ALU.add))
                bgv(lambda: VE.tensor_scalar(out=bis[0:1, 4:5], in0=bis[0:1, 3:4],
                                             scalar1=float(KSEL), scalar2=None, op0=ALU.is_ge))
                bgv(lambda: VE.tensor_sub(out=bis[0:1, 5:6], in0=bis[0:1, 2:3],
                                          in1=bis[0:1, 0:1]))
                bgv(lambda: VE.scalar_tensor_tensor(out=bis[0:1, 0:1], in0=bis[0:1, 4:5],
                                                    scalar=bis[0:1, 5:6], in1=bis[0:1, 0:1],
                                                    op0=ALU.mult, op1=ALU.add))
                bgv(lambda: VE.tensor_sub(out=bis[0:1, 6:7], in0=bis[0:1, 1:2],
                                          in1=bis[0:1, 2:3]))
                bgv(lambda: VE.scalar_tensor_tensor(out=bis[0:1, 1:2], in0=bis[0:1, 4:5],
                                                    scalar=bis[0:1, 6:7], in1=bis[0:1, 2:3],
                                                    op0=ALU.mult, op1=ALU.add))
                mk_mid()

            def _fin():
                if bstate["nv"] is not None:
                    VE.wait_ge(sv, bstate["nv"])
                ins_ = VE.tensor_copy(bis[0:1, 7:8], bis[0:1, 0:1])
                bstate["nv"] = vch(ins_)
                marks["bisdone"] = bstate["nv"]
            bg(_fin)

        def drain_bg(k):
            npop = min(k, len(bg_ops))
            for _ in range(npop):
                bg_ops.pop(0)()

        if "attn" in phases:
            with nc.sbuf_tensor("ktb", [65, 2 * 512], BF16) as ktb, \
                 nc.sbuf_tensor("vtb", [128, 2 * 4 * 65], BF16) as vtb, \
                 nc.sbuf_tensor("expb", [128, 2 * 1024], BF16) as expb, \
                 nc.sbuf_tensor("recr", [1, 2 * 512], F32) as recr, \
                 nc.sbuf_tensor("recrep", [128, 512], F32) as recrep, \
                 nc.sbuf_tensor("masks", [128, 4 * 512], BF16) as masks, \
                 nc.sbuf_tensor("mscr", [128, 512], F32) as mscr:

                GP.wait_ge(sem.get("womm"), sem.val("womm"))
                for d_ in range(4):
                    if d_ > 0:
                        GP.wait_ge(sem.get("mskdone"), d_)
                    s_mio = sem.inc("mskio", GP.iota(
                        mscr[:, :], pattern=[[1, 512]], base=-128 * d_, channel_multiplier=-1,
                        allow_small_or_imprecise_dtypes=True))
                    VE.wait_ge(sem.get("mskio"), s_mio)
                    vm = VE.tensor_scalar(out=masks[:, 512 * d_:512 * (d_ + 1)], in0=mscr[:, :],
                                          scalar1=0.0, scalar2=None, op0=ALU.is_ge)
                    sem.inc("mskdone", vm)
                for sl in range(2):
                    for j in range(4):
                        VE.memset(vtb[:, 260 * sl + 65 * j + 64:260 * sl + 65 * j + 65], 1.0)
                sem.inc("ktones", VE.memset(ktb[64:65, :], 1.0))

                engs = [TE, VE, SE, GP, SY]
                regs = nc.alloc_registers("role_r", engines=[e.engine for e in engs])
                for eng in engs:
                    eng.wait_ge(sem.get("role"), 16)
                nc.regs_load(regs, role_sb[0:1, 0:1])
                TE.wait_ge(sv, marks["qbias"])
                sem.wait(TE, "ktones")
                sem.wait(TE, "mskdone")
                TE.wait_ge(sem.get("womm"), sem.val("womm"))
                SY.wait_ge(sem.get("womm"), sem.val("womm"))
                SY.wait_ge(sem.get("cc"), 2)
                GP.wait_ge(sem.get("cc"), 2)
                SE.wait_ge(sem.get("woev"), sem.val("woev"))

                base_sems = dict(sem.n)
                base_bg = list(bg_ops)
                arm_end = None
                for arm_role in range(NCORE):
                    sem.n = dict(base_sems)
                    bg_ops.clear()
                    bg_ops.extend(base_bg)
                    if "comm" in phases:
                        bstate["nv"] = None
                    with nc.If_cmp(regs, arm_role, "IS_EQ"):
                        emit_attention(nc, sem, arm_role, qt, ktb, vtb, expb, recr,
                                       recrep, masks, ctxd, k_all, v_all, ps, bank,
                                       TE, VE, SE, GP, SY, ones1x64f, drain_bg)
                    nc.end_ifs()
                    if arm_end is None:
                        arm_end = dict(sem.n)
                    else:
                        diff = {k: (arm_end.get(k), sem.n.get(k))
                                for k in set(arm_end) | set(sem.n)
                                if arm_end.get(k) != sem.n.get(k)}
                        assert not diff, f"non-uniform arm semaphores: {diff}"
                sem.n = dict(arm_end)
                bg_ops.clear()

                if "ctxd" in dbg:
                    SE.wait_ge(sem.get("cdone"), sem.val("cdone"))
                    sem.inc("dbg", SE.dma_start(out=dbg["ctxd"][:, :], in_=ctxd[:, :]), 16)
        else:
            drain_bg(10 ** 9)

        if "comm" in phases:
            drain_bg(10 ** 9)
            SY.wait_ge(sv, marks["bisdone"])
            sem.inc("scrow", SY.dma_start(out=sc_row[:, :], in_=scown[:, :]), 16)
            VE.wait_ge(sem.get("scrow"), 16)
            VE.wait_ge(sv, sem.val("v"))
            nv = vch(VE.tensor_scalar(out=mask_row[:, :], in0=sc_row[:, :],
                                      scalar1=bis[0:1, 0:1], scalar2=None, op0=ALU.is_gt))
            VE.wait_ge(sv, nv)
            nv = vch(VE.tensor_tensor_scan(out=pos_row[:, :], data0=mask_row[:, :],
                                           data1=zrow[:, :], initial=0.0,
                                           op0=ALU.add, op1=ALU.add))
            VE.wait_ge(sv, nv)
            nv = vch(VE.tensor_sub(out=pos_row[:, :], in0=pos_row[:, :], in1=mask_row[:, :]))
            VE.wait_ge(sv, nv)
            nv = vch(VE.tensor_scalar(out=pos_row[:, :], in0=pos_row[:, :],
                                      scalar1=-BIG, scalar2=None, op0=ALU.add))
            VE.wait_ge(sv, nv)
            nv = vch(VE.tensor_mul(out=pos_row[:, :], in0=pos_row[:, :], in1=mask_row[:, :]))
            VE.wait_ge(sv, nv)
            vps = VE.tensor_scalar(out=pos_row[:, :], in0=pos_row[:, :],
                                   scalar1=BIG, scalar2=None, op0=ALU.add)
            nv = vch(vps)
            marks["posrow"] = nv
            SY.wait_ge(sv, nv)
            sem.inc("pos128", SY.dma_start(out=pos128[:, :], in_=pos_row[:, :]), 16)
            sem.inc("m128", SY.dma_start(out=mask128[:, :], in_=mask_row[:, :]), 16)
            SE.wait_ge(sem.get("scrow"), 16)
            sgm = SE.activation(out=sig_sb[:, :], in_=scown[:, :], func=AF.Sigmoid,
                                bias=rbcol[:, 0:1])
            sem.inc("sig", sgm)
            VE.wait_ge(sem.get("sig"), 1)
            VE.wait_ge(sem.get("m128"), 16)
            vsm = VE.tensor_mul(out=sigmask[:, :], in0=sig_sb[:, :], in1=mask128[:, :])
            marks["sigmask"] = vch(vsm)
            if "bis" in dbg:
                SE.wait_ge(sv, marks["bisdone"])
                sem.inc("dbg", SE.dma_start(out=dbg["bis"][:, :], in_=bis[:, :]), 16)
            if "posrow" in dbg:
                SE.wait_ge(sv, marks["posrow"])
                sem.inc("dbg", SE.dma_start(out=dbg["posrow"][:, :], in_=pos_row[:, :]), 16)
            if "sigmask" in dbg:
                SE.wait_ge(sv, marks["sigmask"])
                sem.inc("dbg", SE.dma_start(out=dbg["sigmask"][:, :], in_=sigmask[:, :]), 16)

        bis_stack.close()
        qt_stack.close()

        if "ffn" in phases:
            with nc.sbuf_tensor("PT_sb", [128, NJC * TOK], BF16) as PT_sb, \
                 nc.sbuf_tensor("proc", [128, NJC * D], BF16) as proc:
                emit_ffn(nc, sem, marks, dbg, ps, bank, TE, VE, SE, GP, SY,
                         xn_tok, pos128, pos_row, sigmask,
                         ones_bf, ones_f32, b2_r, w1, b1, w2, sv, ss, vch, sch,
                         PT_sb, proc)
                if "epi" in phases:
                    emit_epilogue(nc, sem, marks, dbg, ps, bank, TE, VE, SE, GP, SY,
                                  x_in, y_out, stats, ctxd, woe, boeff_r, sigmask,
                                  ones_f32, sv, ss, vch, sch, PT_sb, proc)

        if "epi" not in phases or "ffn" not in phases:
            yz = sb("yz", [128, D], F32)
            VE.wait_ge(sv, sem.val("v"))
            nv = vch(VE.memset(yz[:, :], 0.0))
            SY.wait_ge(sv, nv)
            for tt in range(NT):
                sem.inc("yout", SY.dma_start(out=y_out[128 * tt:128 * (tt + 1), :],
                                             in_=yz[:, :]), 16)

        SY.wait_ge(sem.get("yout"), sem.val("yout"))
        if dbg:
            SY.wait_ge(sem.get("dbg"), sem.val("dbg"))
        GP.wait_ge(sem.get("yout"), sem.val("yout"))
    return nc


def emit_attention(nc, sem, role, qt, ktb, vtb, expb, recr, recrep, masks, ctxd,
                   k_all, v_all, ps, bank, TE, VE, SE, GP, SY, ones1x64f, drain_bg):
    import os
    max_groups = int(os.environ.get("MK_ATTN_GROUPS", "999"))
    max_blocks = int(os.environ.get("MK_ATTN_BLOCKS", "999"))
    """One switch arm. Event counts are identical for every role."""
    gidx = 0
    grp = 0
    nkst = 0
    nvst = 0
    pending = []
    qblocks = [(1, min(16 - role, max_blocks)), (0, min(role + 1, max_blocks))]

    def flush_pending():
        while pending:
            pending.pop(0)()

    for qb, nkb in qblocks:
        for g in range(NKV):
            for h2 in range(2):
                if grp >= max_groups:
                    continue
                hA = 4 * g + 2 * h2
                pc = 2 * g + h2
                cb = 4 + 2 * (grp % 2)
                g0 = gidx
                pend_ctx = [None]

                def flush_ctx():
                    if pend_ctx[0] is not None:
                        f = pend_ctx[0]
                        pend_ctx[0] = None
                        f()

                for kb in range(nkb):
                    sl = (gidx // 4) % 2
                    krow, kcol, vrow = kblock_src(kb, g)
                    blk = gidx // 4
                    if blk >= 2:
                        SY.wait_ge(sem.get("ctec"), 4 * blk - 4)
                        GP.wait_ge(sem.get("ctec"), 4 * blk - 4)
                    nkst = sem.inc("kst", SY.dma_start(
                        out=ktb[0:64, 512 * sl:512 * (sl + 1)],
                        in_=k_all[krow:krow + KV, kcol:kcol + 512]), 16)
                    nvst = sem.inc("vst", GP.dma_start(
                        out=vtb.rearrange("p (s a d) -> p s a d", s=2, a=4)[:, sl, :, 0:64],
                        in_=v_all[vrow:vrow + 512, :].rearrange(
                            "(a p) d -> p a d", p=128)), 16)
                    diag = (kb == nkb - 1)
                    TE.wait_ge(sem.get("kst"), nkst)
                    TE.wait_ge(sem.get("vst"), nvst)
                    for i4 in range(4):
                        sb_ = 2 * (gidx % 2)
                        if gidx >= 2:
                            TE.wait_ge(sem.get("expdone"), gidx - 1)
                        for hh in range(2):
                            mm = TE.matmul(
                                bank(sb_ + hh),
                                ktb[:, 512 * sl + 128 * i4:512 * sl + 128 * (i4 + 1)],
                                qt[:, TOK * (hA + hh) + 512 * qb:
                                   TOK * (hA + hh) + 512 * (qb + 1)],
                                start=True, stop=True)
                        sem.inc("scmm", mm)
                        flush_ctx()
                        SE.wait_ge(sem.get("scmm"), gidx + 1)
                        if gidx >= 2:
                            SE.wait_ge(sem.get("ctec"), gidx - 1)
                        ex = SE.activation(out=expb[:, 1024 * (gidx % 2):1024 * (gidx % 2 + 1)],
                                           in_=ps[:, 512 * sb_:512 * sb_ + 1024],
                                           func=AF.Exp, scale=0.125)
                        sem.inc("expdone", ex)
                        if diag:
                            VE.wait_ge(sem.get("expdone"), gidx + 1)
                            for hh in range(2):
                                vm = VE.tensor_mul(
                                    out=expb[:, 1024 * (gidx % 2) + 512 * hh:
                                             1024 * (gidx % 2) + 512 * (hh + 1)],
                                    in0=expb[:, 1024 * (gidx % 2) + 512 * hh:
                                             1024 * (gidx % 2) + 512 * (hh + 1)],
                                    in1=masks[:, 512 * i4:512 * (i4 + 1)])
                                sem.inc("mskmul", vm)
                        first = (kb == 0 and i4 == 0)
                        last = (kb == nkb - 1 and i4 == 3)
                        g_cur = gidx
                        sl_cur = sl
                        i4_cur = i4
                        grp_cur = grp
                        cb_cur = cb
                        diag_cur = diag
                        mskval = sem.val("mskmul")
                        vstval = nvst

                        def emit_ctx(g_cur=g_cur, sl_cur=sl_cur, i4_cur=i4_cur,
                                     grp_cur=grp_cur, cb_cur=cb_cur, first=first,
                                     last=last, diag_cur=diag_cur, mskval=mskval,
                                     vstval=vstval):
                            if diag_cur:
                                TE.wait_ge(sem.get("mskmul"), mskval)
                            else:
                                TE.wait_ge(sem.get("expdone"), g_cur + 1)
                            if first and grp_cur >= 2:
                                TE.wait_ge(sem.get("cdone"), grp_cur - 1)
                            for hh in range(2):
                                cm = TE.matmul(
                                    bank(cb_cur + hh, w=512, pn=65),
                                    vtb[:, 260 * sl_cur + 65 * i4_cur:
                                        260 * sl_cur + 65 * (i4_cur + 1)],
                                    expb[:, 1024 * (g_cur % 2) + 512 * hh:
                                         1024 * (g_cur % 2) + 512 * (hh + 1)],
                                    start=first, stop=last)
                                if last:
                                    sem.inc("ctxg", cm)
                            if last:
                                sem.inc("ctec", TE.nop())
                            else:
                                sem.inc("ctec", cm)

                        pend_ctx[0] = emit_ctx
                        gidx += 1
                        if gidx == g0 + 2:
                            flush_pending()
                flush_ctx()
                pending.append(make_group_epilogue(
                    nc, sem, grp, qb, pc, cb, qt, recr, recrep, ctxd, ps, bank,
                    TE, VE, SE, ones1x64f))
                grp += 1
                drain_bg(26)
    flush_pending()
    drain_bg(10 ** 9)


def make_group_epilogue(nc, sem, grp, qb, pc, cb, qt, recr, recrep, ctxd, ps, bank,
                        TE, VE, SE, ones1x64f):
    def run():
        col = TOK * pc + 512 * qb
        SE.wait_ge(sem.get("ctxg"), 2 * (grp + 1))
        SE.activation(out=ctxd[0:64, col:col + 512],
                      in_=bank(cb, w=512, pn=64), func=AF.Copy)
        SE.activation(out=ctxd[64:128, col:col + 512],
                      in_=bank(cb + 1, w=512, pn=64), func=AF.Copy)
        SE.activation(out=recr[0:1, 0:512], in_=bank(cb, w=512, p0=64, pn=1),
                      func=AF.Copy)
        c2 = SE.activation(out=recr[0:1, 512:1024], in_=bank(cb + 1, w=512, p0=64, pn=1),
                           func=AF.Copy)
        sem.inc("ctevac", c2)
        TE.wait_ge(sem.get("ctevac"), grp + 1)
        TE.matmul(bank(cb, w=512, pn=64), ones1x64f[:, :], recr[0:1, 0:512],
                  start=True, stop=True)
        rr = TE.matmul(bank(cb, w=512, p0=64, pn=64), ones1x64f[:, :],
                       recr[0:1, 512:1024], start=True, stop=True)
        sem.inc("rrep", rr)
        VE.wait_ge(sem.get("rrep"), grp + 1)
        VE.reciprocal(out=recrep[:, :], in_=bank(cb, w=512))
        vs_ = VE.tensor_mul(out=ctxd[:, col:col + 512], in0=ctxd[:, col:col + 512],
                            in1=recrep[:, :])
        sem.inc("cdone", vs_)
    return run


def emit_ffn(nc, sem, marks, dbg, ps, bank, TE, VE, SE, GP, SY,
             xn_tok, pos128, pos_row, sigmask,
             ones_bf, ones_f32, b2_r, w1, b1, w2, sv, ss, vch, sch,
             PT_sb, proc):
    with nc.sbuf_tensor("hff", [128, 64 * CAP], BF16) as hff:
        P_sb = hff
        with nc.sbuf_tensor("posrep", [128, TOK], F32) as posrep, \
             nc.sbuf_tensor("iota384", [128, CAP], F32) as iota384, \
             nc.sbuf_tensor("jcol", [128, NJC], F32) as jcol:
            GP.wait_ge(sem.get("cdone"), sem.val("cdone"))
            sem.inc("ffio", GP.iota(iota384[:, :], pattern=[[1, CAP]], base=0,
                                    channel_multiplier=0,
                                    allow_small_or_imprecise_dtypes=True))
            for jc in range(NJC):
                sem.inc("ffio", GP.iota(jcol[:, jc:jc + 1], pattern=[[0, 1]],
                                        base=128 * jc, channel_multiplier=1,
                                        allow_small_or_imprecise_dtypes=True))
            TE.wait_ge(sv, marks["posrow"])
            TE.wait_ge(sem.get("cdone"), sem.val("cdone"))
            pos_local = pos_row.rearrange("b (p a) -> b a p", p=128)
            for i in range(2):
                sem.inc("prep", TE.matmul(bank(6 + i), ones_f32[0:1, 0:128],
                                          pos_local[0:1, 4 * i:4 * (i + 1), :],
                                          start=True, stop=True))
            VE.wait_ge(sem.get("prep"), 2)
            VE.wait_ge(sv, sem.val("v"))
            for i in range(2):
                vch(VE.tensor_copy(posrep[:, 512 * i:512 * (i + 1)], bank(6 + i)))
            VE.wait_ge(sem.get("ffio"), 4)
            VE.wait_ge(sem.get("pos128"), 16)
            VE.wait_ge(sv, sem.val("v"))
            for tt in range(NT):
                pb = VE.tensor_scalar(out=P_sb[:, CAP * tt:CAP * (tt + 1)],
                                      in0=iota384[:, :], scalar1=pos128[:, tt:tt + 1],
                                      scalar2=None, op0=ALU.is_equal)
                marks["pbuild"] = vch(pb)
            for jc in range(NJC):
                ptb = VE.tensor_scalar(out=PT_sb[:, TOK * jc:TOK * (jc + 1)],
                                       in0=posrep[:, :], scalar1=jcol[:, jc:jc + 1],
                                       scalar2=None, op0=ALU.is_equal)
                marks["ptbuild"] = vch(ptb)

        with nc.sbuf_tensor("sel", [128, 16 * CAP], BF16) as sel, \
             nc.sbuf_tensor("w1_sb", [128, 2 * 2048], BF16) as w1_sb, \
             nc.sbuf_tensor("b1buf", [1, 1024], BF16) as b1buf:
            TE.wait_ge(sv, marks["pbuild"])
            for ec in range(16):
                bnk = ec % 2
                if ec >= 2:
                    TE.wait_ge(sem.get("selev"), ec - 1)
                for tt in range(NT):
                    mm = TE.matmul(bank(bnk, w=CAP),
                                   xn_tok[:, D * tt + 128 * ec:D * tt + 128 * (ec + 1)],
                                   P_sb[:, CAP * tt:CAP * (tt + 1)],
                                   start=(tt == 0), stop=(tt == NT - 1))
                sem.inc("selmm", mm)
                SE.wait_ge(sem.get("selmm"), ec + 1)
                sev = SE.activation(out=sel[:, CAP * ec:CAP * (ec + 1)],
                                    in_=bank(bnk, w=CAP), func=AF.Copy)
                sem.inc("selev", sev)
            if "sel" in dbg:
                SE.wait_ge(sem.get("selev"), 16)
                sem.inc("dbg", SE.dma_start(out=dbg["sel"][:, :], in_=sel[:, :]), 16)

            TE.wait_ge(sem.get("selev"), 16)
            for fc in range(64):
                if fc == 0:
                    GP.wait_ge(sv, marks["ptbuild"])
                if fc % 8 == 0:
                    GP.wait_ge(sem.get("w1mm"), fc)
                    sem.inc("ld_b1", GP.dma_start(
                        out=b1buf[:, :], in_=b1[0:1, 1024 * (fc // 8):1024 * (fc // 8 + 1)]), 16)
                if fc >= 2:
                    GP.wait_ge(sem.get("w1mm"), fc - 1)
                sem.inc("ld_w1", GP.dma_start(
                    out=w1_sb[:, 2048 * (fc % 2):2048 * (fc % 2 + 1)],
                    in_=w1[128 * fc:128 * (fc + 1), :]), 16)
                bnk = fc % 4
                TE.wait_ge(sem.get("ld_w1"), 16 * (fc + 1))
                TE.wait_ge(sem.get("ld_b1"), 16 * (fc // 8 + 1))
                if fc >= 4:
                    TE.wait_ge(sem.get("hgel"), fc - 3)
                for ec in range(16):
                    TE.matmul(bank(bnk, w=CAP),
                              w1_sb[:, 2048 * (fc % 2) + 128 * ec:
                                    2048 * (fc % 2) + 128 * (ec + 1)],
                              sel[:, CAP * ec:CAP * (ec + 1)],
                              start=(ec == 0), stop=False)
                sem.inc("w1mm", TE.matmul(bank(bnk, w=CAP),
                                          b1buf[0:1, 128 * (fc % 8):128 * (fc % 8 + 1)],
                                          ones_bf[0:1, 0:CAP], start=False, stop=True))
                SE.wait_ge(sem.get("w1mm"), fc + 1)
                hg = SE.activation(out=hff[:, CAP * fc:CAP * (fc + 1)],
                                   in_=bank(bnk, w=CAP), func=AF.Gelu_apprx_tanh)
                sem.inc("hgel", hg)
            if "hff" in dbg:
                SE.wait_ge(sem.get("hgel"), 64)
                sem.inc("dbg", SE.dma_start(out=dbg["hff"][:, :], in_=hff[:, 0:4 * CAP]), 16)

        with nc.sbuf_tensor("w2_sb", [128, 2 * 1024], BF16) as w2_sb:
            for halfe in range(2):
                for fc in range(64):
                    nW = 64 * halfe + fc
                    if nW == 0:
                        GP.wait_ge(sem.get("w1mm"), 64)
                    if nW >= 2:
                        GP.wait_ge(sem.get("w2mm"), 6 * (nW - 1))
                    sem.inc("ld_w2", GP.dma_start(
                        out=w2_sb[:, 1024 * (nW % 2):1024 * (nW % 2 + 1)],
                        in_=w2[128 * fc:128 * (fc + 1), 1024 * halfe:1024 * (halfe + 1)]), 16)
                    TE.wait_ge(sem.get("ld_w2"), 16 * (nW + 1))
                    if fc == 0:
                        TE.wait_ge(sem.get("hgel"), 64)
                        if halfe == 1:
                            TE.wait_ge(sem.get("procev"), 6)
                    for tj in range(NJC):
                        for eq in range(2):
                            mm = TE.matmul(bank(2 * tj + eq, w=512),
                                           hff[:, CAP * fc + 128 * tj:CAP * fc + 128 * (tj + 1)],
                                           w2_sb[:, 1024 * (nW % 2) + 512 * eq:
                                                 1024 * (nW % 2) + 512 * (eq + 1)],
                                           start=(fc == 0), stop=False)
                    sem.inc("w2mm", mm, 6)
                for tj in range(NJC):
                    for eq in range(2):
                        sem.inc("w2bias", TE.matmul(
                            bank(2 * tj + eq, w=512), ones_bf[0:1, 0:128],
                            b2_r[0:1, 1024 * halfe + 512 * eq:1024 * halfe + 512 * (eq + 1)],
                            start=False, stop=True))
                SE.wait_ge(sem.get("w2bias"), 6 * (halfe + 1))
                for tj in range(NJC):
                    for eq in range(2):
                        pv = SE.activation(
                            out=proc[:, D * tj + 1024 * halfe + 512 * eq:
                                     D * tj + 1024 * halfe + 512 * (eq + 1)],
                            in_=bank(2 * tj + eq, w=512), func=AF.Copy)
                        sem.inc("procev", pv)
            if "proc" in dbg:
                SE.wait_ge(sem.get("procev"), 12)
                sem.inc("dbg", SE.dma_start(out=dbg["proc"][:, :], in_=proc[:, :]), 16)


def emit_epilogue(nc, sem, marks, dbg, ps, bank, TE, VE, SE, GP, SY,
                  x_in, y_out, stats, ctxd, woe, boeff_r, sigmask,
                  ones_f32, sv, ss, vch, sch, PT_sb, proc):
    with nc.sbuf_tensor("xep", [128, 2 * D], F32) as xep, \
         nc.sbuf_tensor("gep", [128, D], F32) as gep, \
         nc.sbuf_tensor("uep", [128, D], F32) as uep, \
         nc.sbuf_tensor("oep", [128, 2 * D], F32) as oep:
        SY.wait_ge(sem.get("procev"), 12)
        gsil_marks = []
        for tt in range(2):
            sem.inc("ld_xe", SY.dma_start(out=xep[:, D * tt:D * (tt + 1)],
                                          in_=x_in[128 * tt:128 * (tt + 1), :]), 16)
        TE.wait_ge(sem.get("procev"), 12)
        TE.wait_ge(sv, marks["ptbuild"])
        sem.wait(TE, "cdone")
        TE.wait_ge(sv, marks["boeff"])
        for tt in range(NT):
            slot = tt % 2
            if tt >= 1:
                TE.wait_ge(ss, gsil_marks[tt - 1])
            for eq in range(4):
                for jc in range(NJC):
                    mm = TE.matmul(bank(eq),
                                   PT_sb[:, TOK * jc + 128 * tt:TOK * jc + 128 * (tt + 1)],
                                   proc[:, D * jc + 512 * eq:D * jc + 512 * (eq + 1)],
                                   start=(jc == 0), stop=(jc == NJC - 1))
            sem.inc("scat", mm)
            VE.wait_ge(sem.get("ld_xe"), 16 * (tt + 1))
            VE.wait_ge(sem.get("scat"), tt + 1)
            VE.wait_ge(sv, marks["sigmask"])
            if tt >= 1:
                VE.wait_ge(ss, gsil_marks[tt - 1])
            nv = vch(VE.tensor_scalar(out=uep[:, :], in0=xep[:, D * slot:D * (slot + 1)],
                                      scalar1=stats[:, tt:tt + 1],
                                      scalar2=stats[:, NT + tt:NT + tt + 1],
                                      op0=ALU.mult, op1=ALU.subtract))
            VE.wait_ge(sv, nv)
            nv = vch(VE.scalar_tensor_tensor(out=uep[:, :], in0=ps[:, 0:2048],
                                             scalar=sigmask[:, tt:tt + 1], in1=uep[:, :],
                                             op0=ALU.mult, op1=ALU.add))
            SE.wait_ge(sv, nv)
            gs = SE.activation(out=gep[:, :], in_=uep[:, :], func=AF.Silu)
            gsil_marks.append(sch(gs))
            for eq in range(4):
                sb_ = 4 + eq % 2
                n_s = 4 * tt + eq
                if n_s >= 2:
                    TE.wait_ge(sem.get("sev"), n_s - 1)
                for pcc in range(8):
                    TE.matmul(bank(sb_),
                              ctxd[:, TOK * pcc + 128 * tt:TOK * pcc + 128 * (tt + 1)],
                              woe[:, D * pcc + 512 * eq:D * pcc + 512 * (eq + 1)],
                              start=(pcc == 0), stop=False)
                sem.inc("smm", TE.matmul(bank(sb_), ones_f32[0:1, 0:128],
                                         boeff_r[0:1, 512 * eq:512 * (eq + 1)],
                                         start=False, stop=True))
                VE.wait_ge(sem.get("smm"), n_s + 1)
                VE.wait_ge(ss, gsil_marks[tt])
                if tt >= 2:
                    VE.wait_ge(sem.get("yout"), 16 * (tt - 1))
                nv = vch(VE.tensor_mul(out=oep[:, D * slot + 512 * eq:D * slot + 512 * (eq + 1)],
                                       in0=bank(sb_), in1=gep[:, 512 * eq:512 * (eq + 1)]))
                VE.wait_ge(sv, nv)
                va = VE.tensor_add(out=oep[:, D * slot + 512 * eq:D * slot + 512 * (eq + 1)],
                                   in0=oep[:, D * slot + 512 * eq:D * slot + 512 * (eq + 1)],
                                   in1=xep[:, D * slot + 512 * eq:D * slot + 512 * (eq + 1)])
                sem.inc("sev", va)
            SY.wait_ge(sem.get("sev"), sem.val("sev"))
            sem.inc("yout", SY.dma_start(out=y_out[128 * tt:128 * (tt + 1), :],
                                         in_=oep[:, D * slot:D * (slot + 1)]), 16)
            if tt + 2 < NT:
                SY.wait_ge(sem.get("sev"), sem.val("sev"))
                sem.inc("ld_xe", SY.dma_start(out=xep[:, D * slot:D * (slot + 1)],
                                              in_=x_in[128 * (tt + 2):128 * (tt + 3), :]), 16)


def _pack_lhsT(w):
    w = np.asarray(w, dtype=np.float32)
    E, F = w.shape
    nf = F // 128
    na = E // 128
    t = w.reshape(na, 128, nf, 128).transpose(2, 1, 0, 3).reshape(nf * 128, E)
    return np.ascontiguousarray(t)


def _prep_inputs(core, **inp):
    X = np.ascontiguousarray(np.asarray(inp["L0_attn_x"], dtype=np.float32).reshape(S, D))
    r = core
    xo = np.concatenate([X[512 * r:512 * (r + 1)], X[512 * (15 - r):512 * (16 - r)]], axis=0)
    m = {
        "x_own": xo,
        "role": np.array([[r]], dtype=np.int32),
        "Wqp": _pack_lhsT(inp["L0_attn_Wq"]), "bq": np.asarray(inp["L0_attn_bq"]).reshape(1, -1),
        "Wlp": _pack_lhsT(inp["L0_attn_Wl"]), "bl": np.asarray(inp["L0_attn_bl"]).reshape(1, -1),
        "Wk": inp["L0_attn_Wk"], "bk": np.asarray(inp["L0_attn_bk"]).reshape(1, -1),
        "Wv": inp["L0_attn_Wv"], "bv": np.asarray(inp["L0_attn_bv"]).reshape(1, -1),
        "Wq2kv": inp["L0_attn_Wq2kv"], "Wkv2h": inp["L0_attn_Wkv2h"],
        "bkv2h": np.asarray(inp["L0_attn_bkv2h"]).reshape(-1, 1),
        "Wo": inp["L0_attn_Wo"], "bo": np.asarray(inp["L0_attn_bo"]).reshape(1, -1),
        "rw": np.asarray(inp["L0_ffn_router_weight"]).reshape(1, -1),
        "rb": np.asarray(inp["L0_ffn_router_bias"]).reshape(1, 1),
        "w1p": _pack_lhsT(inp["L0_ffn_w1"]), "b1": np.asarray(inp["L0_ffn_b1"]).reshape(1, -1),
        "w2": inp["L0_ffn_w2"], "b2": np.asarray(inp["L0_ffn_b2"]).reshape(1, -1),
    }
    return {k: np.ascontiguousarray(np.asarray(v, dtype=(np.int32 if k == "role" else np.float32)))
            for k, v in m.items()}


def kernel(**inputs):
    nc = build(DEBUG, PHASES)
    in_maps = [_prep_inputs(c, **inputs) for c in range(NCORE)]
    res = run_bass_kernel_spmd(nc, in_maps, core_ids=list(range(NCORE)))
    Y = np.empty((S, D), dtype=np.float32)
    for r in range(NCORE):
        yr = np.asarray(res.results[r]["y"])
        Y[512 * r:512 * (r + 1)] = yr[0:512]
        Y[512 * (15 - r):512 * (16 - r)] = yr[512:1024]
    return Y.reshape(1, S, D)


def kernel_dev(debug, phases, **inputs):
    nc = build(debug, phases)
    in_maps = [_prep_inputs(c, **inputs) for c in range(NCORE)]
    res = run_bass_kernel_spmd(nc, in_maps, core_ids=list(range(NCORE)))
    return res


# version 11 (final)
# speedup vs baseline: 1.1687x; 1.1367x over previous
from contextlib import ExitStack

import numpy as np
import ml_dtypes

import concourse.bass as bass
import concourse.mybir as mybir
from concourse.bass_utils import run_bass_kernel_spmd

F32 = mybir.dt.float32
BF16 = mybir.dt.bfloat16
I32 = mybir.dt.int32
AF = mybir.ActivationFunctionType
ALU = mybir.AluOpType
AX = mybir.AxisListType

S = 8192
D = 2048
NH = 16
NKV = 4
KV = 64
LAT = 1024
DFF = 8192
KSEL = 2048
NCORE = 8
TOK = 1024
NT = 8
CAP = 384
NJC = 3
EPS = 1e-5
BIG = 50000.0
NITER_BIS = 30

DEBUG = ()
PHASES = ("p1", "comm", "attn", "ffn", "epi")


def kblock_src(kb, g):
    rb = kb if kb < 8 else 15 - kb
    loc = 0 if kb < 8 else 512
    krow = NKV * KV * rb + KV * g
    kcol = loc
    vrow = NKV * TOK * rb + TOK * g + loc
    return krow, kcol, vrow


class Sems:
    def __init__(self, nc, stack):
        self.nc = nc
        self.stack = stack
        self.h = {}
        self.n = {}

    def get(self, name):
        if name not in self.h:
            self.h[name] = self.stack.enter_context(self.nc.semaphore(name))
        if name not in self.n:
            self.n[name] = 0
        return self.h[name]

    def inc(self, name, instr, k=1):
        instr.then_inc(self.get(name), k)
        self.n[name] += k
        return self.n[name]

    def val(self, name):
        self.get(name)
        return self.n[name]

    def wait(self, eng, name, v=None):
        eng.wait_ge(self.get(name), self.n[name] if v is None else v)


def build(debug=(), phases=PHASES):
    nc = bass.Bass("TRN2", target_bir_lowering=False)
    TE, VE, SE, GP, SY = nc.tensor, nc.vector, nc.scalar, nc.gpsimd, nc.sync

    def din(name, shape, dt=F32):
        return nc.dram_tensor(name, shape, dt, kind="ExternalInput").ap()

    x_in = din("x_own", [TOK, D])
    role_in = din("role", [1, 1], I32)
    Wq = din("Wqp", [16 * 128, D]); bq = din("bq", [1, D])
    Wl = din("Wlp", [8 * 128, D]); bl = din("bl", [1, LAT])
    Wk = din("Wk", [LAT, NKV * KV]); bk = din("bk", [1, NKV * KV])
    Wv = din("Wv", [LAT, NKV * KV]); bv = din("bv", [1, NKV * KV])
    Wq2kv = din("Wq2kv", [128, KV])
    Wkv2h = din("Wkv2h", [KV, 128]); bkv2h = din("bkv2h", [128, 1])
    Wo = din("Wo", [D, D]); bo = din("bo", [1, D])
    rw = din("rw", [1, D]); rb = din("rb", [1, 1])
    w1 = din("w1p", [64 * 128, D]); b1 = din("b1", [1, DFF])
    w2 = din("w2", [DFF, D]); b2 = din("b2", [1, D])
    y_out = nc.dram_tensor("y", [TOK, D], F32, kind="ExternalOutput").ap()

    dbg_specs = {
        "xn_tok": ([128, NT * D], BF16),
        "scores": ([128, NT], F32),
        "qt": ([65, NH * TOK], BF16),
        "lat": ([128, NT * LAT], BF16),
        "s_all": ([NCORE * 1026, 1], F32),
        "bis": ([1, 8], F32),
        "posrow": ([1, TOK], F32),
        "ctxd": ([128, NT * TOK], BF16),
        "sel": ([128, 16 * CAP], BF16),
        "proc": ([128, NJC * D], BF16),
        "woe": ([128, NT * D], BF16),
        "hff": ([128, 4 * CAP], BF16),
        "boeff": ([1, D], F32),
        "sigmask": ([128, NT], F32),
    }
    dbg = {}
    for name in debug:
        shp, dt = dbg_specs[name]
        dbg[name] = nc.dram_tensor("dbg_" + name, shp, dt, kind="ExternalOutput").ap()

    ksend = nc.dram_tensor("ksend", [NKV * KV, TOK], BF16).ap()
    k_all = nc.dram_tensor("k_all", [NCORE * NKV * KV, TOK], BF16).ap()
    vsend = nc.dram_tensor("vsend", [NKV * TOK, KV], BF16).ap()
    v_all = nc.dram_tensor("v_all", [NCORE * NKV * TOK, KV], BF16).ap()
    ssend = nc.dram_tensor("ssend", [1026, 1], F32).ap()
    s_all = nc.dram_tensor("s_all", [NCORE * 1026, 1], F32).ap()
    xn_dram = nc.dram_tensor("xn_dram", [TOK, D], BF16).ap()
    wkvT_dram = nc.dram_tensor("wkvT_dram", [KV, 128], BF16).ap()

    est = ExitStack()
    with est:
        sem = Sems(nc, est)

        def sb(name, shape, dt, **kw):
            return est.enter_context(nc.sbuf_tensor(name, shape, dt, **kw))

        xn_tok = sb("xn_tok", [128, NT * D], BF16)
        ctxd = sb("ctxd", [128, NT * TOK], BF16)
        stats = sb("stats", [128, 2 * NT], F32)
        scown = sb("scown", [128, NT], F32)
        sig_sb = sb("sig_sb", [128, NT], F32)
        sigmask = sb("sigmask", [128, NT], F32)
        mask128 = sb("mask128", [128, NT], F32)
        ones_bf = sb("ones_bf", [1, 512], BF16)
        ones_f32 = sb("ones_f32", [1, 128], F32)
        ones64 = sb("ones64", [64, 1], F32)
        ones128 = sb("ones128", [128, 1], F32)
        ones1x64f = sb("ones1x64f", [1, 64], F32)
        rbcol = sb("rbcol", [128, 1], F32)
        kmax2 = sb("kmax2", [1, 2], F32)
        wq2kv_sb = sb("wq2kv_sb", [128, KV], BF16)
        wkvT_sb = sb("wkvT_sb", [128, KV], BF16)
        bkvcol = sb("bkvcol", [128, 1], BF16)
        role_sb = sb("role_sb", [1, 1], I32)
        rb_sb = sb("rb_sb", [1, 1], F32)
        bq_r = sb("bq_r", [1, D], BF16)
        bl_r = sb("bl_r", [1, LAT], BF16)
        bk_r = sb("bk_r", [1, NKV * KV], BF16)
        bv_r = sb("bv_r", [1, NKV * KV], BF16)
        b2_r = sb("b2_r", [1, D], BF16)
        boeff_r = sb("boeff_r", [1, D], F32)
        t32a = sb("t32a", [32, 32], F32)
        t32b = sb("t32b", [32, 32], F32)
        bis = sb("bis", [1, 8], F32)
        z32 = sb("z32", [1, 32], F32)
        k8 = sb("k8", [1, 8], F32)
        pos128 = sb("pos128", [128, NT], F32)
        qt_stack = ExitStack()
        qt = qt_stack.enter_context(nc.sbuf_tensor("qt", [65, NH * TOK], BF16, side="right"))
        bis_stack = ExitStack()
        scg = bis_stack.enter_context(nc.sbuf_tensor("scg", [32, 256], F32, side="right"))
        cmpb = bis_stack.enter_context(nc.sbuf_tensor("cmpb", [32, 256], F32, side="right"))
        sc_row = sb("sc_row", [1, TOK], F32)
        mask_row = sb("mask_row", [1, TOK], F32)
        pos_row = sb("pos_row", [1, TOK], F32)
        zrow = sb("zrow", [1, TOK], F32)


        ps = est.enter_context(nc.psum_tensor("ps", [128, 4096], F32))

        def bank(b, w=512, p0=0, pn=128):
            return ps[p0:p0 + pn, 512 * b:512 * b + w]

        sv = sem.get("v")
        ss = sem.get("s")
        marks = {}

        def vch(instr):
            return sem.inc("v", instr)

        def sch(instr):
            return sem.inc("s", instr)

        sem.inc("role", SY.dma_start(out=role_sb[:, :], in_=role_in[:, :]), 16)
        sem.inc("ld_rb", SY.dma_start(out=rb_sb[:, :], in_=rb[:, :]), 16)

        for src, dst in ((bq, bq_r), (bl, bl_r), (bk, bk_r), (bv, bv_r), (b2, b2_r)):
            sem.inc("ld_bias", GP.dma_start(out=dst[:, :], in_=src[:, :]), 16)
        sem.inc("ld_wq2kv", GP.dma_start(out=wq2kv_sb[:, :], in_=Wq2kv[:, :]), 16)
        sem.inc("ld_bkv", GP.dma_start(out=bkvcol[:, :], in_=bkv2h[:, :]), 16)
        s_wkv0 = sem.inc("ld_wkvT0", GP.dma_start(out=wkvT_dram[:, :], in_=Wkv2h[:, :]), 16)
        SY.wait_ge(sem.get("ld_wkvT0"), s_wkv0)
        sem.inc("ld_wkvT", SY.dma_start_transpose(out=wkvT_sb[:, 0:KV], in_=wkvT_dram[:, :]), 16)

        VE.memset(ones_bf[:, :], 1.0)
        VE.memset(ones_f32[:, :], 1.0)
        VE.memset(ones64[:, :], 1.0)
        VE.memset(ones128[:, :], 1.0)
        VE.memset(ones1x64f[:, :], 1.0)
        VE.memset(zrow[:, :], 0.0)
        VE.memset(z32[:, :], 0.0)
        VE.memset(t32a[:, :], 0.0)
        VE.memset(t32b[:, :], 0.0)
        sem.inc("const", VE.memset(bis[:, :], 0.0))

        with nc.sbuf_tensor("w_rep", [128, D], F32) as w_rep, \
             nc.sbuf_tensor("rw_r", [1, D], F32, side="right") as rw_r, \
             nc.sbuf_tensor("xst", [128, 2 * D], F32) as xst, \
             nc.sbuf_tensor("xnf", [128, 2 * D], F32) as xnf:

            sem.inc("ld_rw", SY.dma_start(out=rw_r[:, :], in_=rw[:, :]), 16)
            for tt in range(2):
                sem.inc("ld_x", SY.dma_start(out=xst[:, D * tt:D * (tt + 1)],
                                             in_=x_in[128 * tt:128 * (tt + 1), :]), 16)

            sem.wait(TE, "ld_rw")
            sem.wait(TE, "ld_rb")
            sem.wait(TE, "const")
            for i in range(4):
                sem.inc("wrepmm", TE.matmul(bank(i), ones_f32[0:1, 0:128],
                                            rw_r[0:1, 512 * i:512 * (i + 1)],
                                            start=True, stop=True))
            sem.inc("wrepmm", TE.matmul(bank(4, w=1), ones_f32[0:1, 0:128],
                                        rb_sb[0:1, 0:1], start=True, stop=True))
            VE.wait_ge(sem.get("wrepmm"), sem.val("wrepmm"))
            for i in range(4):
                vch(VE.tensor_copy(w_rep[:, 512 * i:512 * (i + 1)], bank(i)))
            sem.inc("wrep", VE.tensor_copy(rbcol[:, :], bank(4, w=1)))

            sem.wait(VE, "wrep")
            cast_s = []
            for tt in range(NT):
                slot = tt % 2
                xs = xst[:, D * slot:D * (slot + 1)]
                xf = xnf[:, D * slot:D * (slot + 1)]
                m_c = sig_sb[:, tt:tt + 1]
                v_c = pos128[:, tt:tt + 1]
                VE.wait_ge(sem.get("ld_x"), 16 * (tt + 1))
                if tt >= 2:
                    VE.wait_ge(ss, cast_s[tt - 2])
                nv = vch(VE.scalar_tensor_tensor(out=xf, in0=xs, scalar=1.0, in1=xs,
                                                 op0=ALU.mult, op1=ALU.mult,
                                                 accum_out=stats[:, tt:tt + 1]))
                nv = vch(VE.tensor_reduce(out=stats[:, NT + tt:NT + tt + 1], in_=xs,
                                          axis=AX.X, op=ALU.add))
                VE.wait_ge(sv, nv)
                nv = vch(VE.tensor_scalar(out=m_c, in0=stats[:, NT + tt:NT + tt + 1],
                                          scalar1=1.0 / D, scalar2=None, op0=ALU.mult))
                VE.wait_ge(sv, nv)
                nv = vch(VE.tensor_scalar(out=v_c, in0=stats[:, tt:tt + 1],
                                          scalar1=1.0 / D, scalar2=None, op0=ALU.mult))
                VE.wait_ge(sv, nv)
                nv = vch(VE.scalar_tensor_tensor(out=v_c, in0=m_c, scalar=m_c, in1=v_c,
                                                 op0=ALU.mult, op1=ALU.subtract))
                VE.wait_ge(sv, nv)
                nv = vch(VE.tensor_scalar(out=v_c, in0=v_c, scalar1=-1.0, scalar2=EPS,
                                          op0=ALU.mult, op1=ALU.add))
                VE.wait_ge(sv, nv)
                nv = vch(VE.reciprocal(out=v_c, in_=v_c))
                SE.wait_ge(sv, nv)
                ns = sch(SE.activation(out=stats[:, tt:tt + 1], in_=v_c, func=AF.Sqrt))
                VE.wait_ge(ss, ns)
                nv = vch(VE.tensor_mul(out=stats[:, NT + tt:NT + tt + 1], in0=m_c,
                                       in1=stats[:, tt:tt + 1]))
                VE.wait_ge(sv, nv)
                nv = vch(VE.tensor_scalar(out=xf, in0=xs, scalar1=stats[:, tt:tt + 1],
                                          scalar2=stats[:, NT + tt:NT + tt + 1],
                                          op0=ALU.mult, op1=ALU.subtract))
                VE.wait_ge(sv, nv)
                nv = vch(VE.scalar_tensor_tensor(out=xs, in0=xf, scalar=1.0,
                                                 in1=w_rep[:, :], op0=ALU.mult, op1=ALU.mult,
                                                 accum_out=scown[:, tt:tt + 1]))
                if tt + 2 < NT:
                    SY.wait_ge(sv, nv)
                    sem.inc("ld_x", SY.dma_start(out=xst[:, D * slot:D * (slot + 1)],
                                                 in_=x_in[128 * (tt + 2):128 * (tt + 3), :]), 16)
                SE.wait_ge(sv, nv)
                si = SE.activation(out=xn_tok[:, D * tt:D * (tt + 1)], in_=xf, func=AF.Copy)
                cast_s.append(sch(si))

            SY.wait_ge(ss, sem.val("s"))
            s_xnw = sem.inc("xnw", SY.dma_start(
                out=xn_dram.rearrange("(a p) d -> p a d", p=128), in_=xn_tok[:, :]), 16)

        with nc.sbuf_tensor("xn_d", [128, 16 * TOK], BF16) as xn_d, \
             nc.sbuf_tensor("qful", [128, 2 * TOK], BF16) as qful, \
             nc.sbuf_tensor("lat_sb", [128, NT * TOK], BF16) as lat_sb, \
             nc.sbuf_tensor("wq_sb", [128, 2 * 2048], BF16) as wq_sb, \
             nc.sbuf_tensor("wk_sb", [128, NT * 256], BF16) as wk_sb, \
             nc.sbuf_tensor("wv_sb", [128, NT * 256], BF16) as wv_sb, \
             nc.sbuf_tensor("qsq", [64, TOK], F32) as qsq, \
             nc.sbuf_tensor("kd_sb", [128, 2 * TOK], BF16) as kd_sb, \
             nc.sbuf_tensor("vs_sb", [128, 256], BF16) as vs_sb, \
             nc.sbuf_tensor("ksq2", [128, TOK], F32) as ksq2:

            SY.wait_ge(sem.get("xnw"), s_xnw)
            for ec in range(16):
                sem.inc("xnd", SY.dma_start_transpose(
                    out=xn_d[:, TOK * ec:TOK * (ec + 1)],
                    in_=xn_dram[:, 128 * ec:128 * (ec + 1)]), 16)

            GP.wait_ge(sem.get("xnw"), s_xnw)
            for dc in range(16):
                if dc >= 2:
                    GP.wait_ge(sem.get("qfmm"), 2 * (dc - 1))
                for half in range(2):
                    sem.inc("ld_wq", GP.dma_start(
                        out=wq_sb[:, 2048 * (dc % 2) + 1024 * half:
                                  2048 * (dc % 2) + 1024 * (half + 1)],
                        in_=Wq[128 * dc:128 * (dc + 1),
                               1024 * half:1024 * (half + 1)]), 16)
            for lc in range(8):
                if lc >= 2:
                    GP.wait_ge(sem.get("latmm"), 2 * (lc - 1))
                else:
                    GP.wait_ge(sem.get("qfmm"), 30 + 2 * lc)
                for half in range(2):
                    sem.inc("ld_wl", GP.dma_start(
                        out=wq_sb[:, 2048 * (lc % 2) + 1024 * half:
                                  2048 * (lc % 2) + 1024 * (half + 1)],
                        in_=Wl[128 * lc:128 * (lc + 1),
                               1024 * half:1024 * (half + 1)]), 16)
            sem.inc("ld_wk", GP.dma_start(
                out=wk_sb[:, :], in_=Wk.rearrange("(a p) d -> p a d", p=128)), 16)
            sem.inc("ld_wv", GP.dma_start(
                out=wv_sb[:, :], in_=Wv.rearrange("(a p) d -> p a d", p=128)), 16)

            TE.wait_ge(sem.get("xnd"), sem.val("xnd"))
            sem.wait(TE, "ld_bias")
            sem.wait(TE, "ld_wq2kv")
            qnve_marks = []
            pend_q = []

            def emit_qfull(h):
                slot = h % 2
                b0 = 2 * slot
                TE.wait_ge(sem.get("ld_wq"), 32 * (h + 1))
                if h >= 2:
                    TE.wait_ge(sem.get("qevac"), 2 * (h - 1))
                for th in range(2):
                    for ec in range(16):
                        TE.matmul(bank(b0 + th),
                                  wq_sb[:, 2048 * slot + 128 * ec:2048 * slot + 128 * (ec + 1)],
                                  xn_d[:, TOK * ec + 512 * th:TOK * ec + 512 * (th + 1)],
                                  start=(ec == 0), stop=False)
                    sem.inc("qfmm", TE.matmul(bank(b0 + th), bq_r[0:1, 128 * h:128 * (h + 1)],
                                              ones_bf[0:1, 0:512], start=False, stop=True))
                SE.wait_ge(sem.get("qfmm"), 2 * (h + 1))
                if h >= 2:
                    SE.wait_ge(sem.get("qbmm"), 2 * (h - 1))
                se1 = SE.activation(out=qful[:, TOK * slot:TOK * slot + 512],
                                    in_=bank(b0), func=AF.Copy)
                se2 = SE.activation(out=qful[:, TOK * slot + 512:TOK * (slot + 1)],
                                    in_=bank(b0 + 1), func=AF.Copy)
                sem.inc("qevac", se1)
                sem.inc("qevac", se2)

            def emit_qb(h):
                slot = h % 2
                TE.wait_ge(sem.get("qevac"), 2 * (h + 1))
                if h >= 1:
                    TE.wait_ge(sem.get("qsqa"), h)
                for th in range(2):
                    sem.inc("qbmm", TE.matmul(bank(4 + th, w=512, pn=64), wq2kv_sb[:, :],
                                              qful[:, TOK * slot + 512 * th:
                                                    TOK * slot + 512 * (th + 1)],
                                              start=True, stop=True))
                SE.wait_ge(sem.get("qbmm"), 2 * (h + 1))
                if h >= 1:
                    SE.wait_ge(sv, qnve_marks[h - 1])
                e1 = SE.activation(out=qt[0:64, TOK * h:TOK * h + 512],
                                   in_=bank(4, pn=64), func=AF.Copy)
                e2 = SE.activation(out=qt[0:64, TOK * h + 512:TOK * (h + 1)],
                                   in_=bank(5, pn=64), func=AF.Copy)
                SE.activation(out=qsq[:, 0:512], in_=bank(4, pn=64), func=AF.Square)
                sq2 = SE.activation(out=qsq[:, 512:1024], in_=bank(5, pn=64), func=AF.Square)
                sem.inc("qtev", e1)
                sem.inc("qtev", e2)
                sem.inc("qsqa", sq2)

            def emit_qnorm(h):
                TE.wait_ge(sem.get("qsqa"), h + 1)
                if h >= 1:
                    TE.wait_ge(sv, qnve_marks[h - 1])
                TE.matmul(bank(6, w=512, pn=1), ones64[:, :], qsq[:, 0:512],
                          start=True, stop=True)
                sem.inc("qnmm", TE.matmul(bank(7, w=512, pn=1), ones64[:, :],
                                          qsq[:, 512:1024], start=True, stop=True))
                VE.wait_ge(sem.get("qnmm"), h + 1)
                vch(VE.tensor_scalar(out=qt[64:65, TOK * h:TOK * h + 512],
                                     in0=bank(6, w=512, pn=1),
                                     scalar1=-0.5, scalar2=None, op0=ALU.mult))
                v2 = VE.tensor_scalar(out=qt[64:65, TOK * h + 512:TOK * (h + 1)],
                                      in0=bank(7, w=512, pn=1),
                                      scalar1=-0.5, scalar2=None, op0=ALU.mult)
                qnve_marks.append(vch(v2))
                marks["qnve_last"] = qnve_marks[-1]

            for h in range(NH):
                emit_qfull(h)
                if h >= 1:
                    emit_qb(h - 1)
                    emit_qnorm(h - 1)
            emit_qb(NH - 1)
            emit_qnorm(NH - 1)

            TE.wait_ge(sem.get("qevac"), 2 * NH)
            for lc in range(8):
                slot = lc % 2
                b0 = 2 * slot
                TE.wait_ge(sem.get("ld_wl"), 32 * (lc + 1))
                if lc >= 2:
                    TE.wait_ge(sem.get("latev"), 2 * (lc - 1))
                for th in range(2):
                    for ec in range(16):
                        TE.matmul(bank(b0 + th),
                                  wq_sb[:, 2048 * slot + 128 * ec:2048 * slot + 128 * (ec + 1)],
                                  xn_d[:, TOK * ec + 512 * th:TOK * ec + 512 * (th + 1)],
                                  start=(ec == 0), stop=False)
                    sem.inc("latmm", TE.matmul(bank(b0 + th), bl_r[0:1, 128 * lc:128 * (lc + 1)],
                                               ones_bf[0:1, 0:512], start=False, stop=True))
                SE.wait_ge(sem.get("latmm"), 2 * (lc + 1))
                l1 = SE.activation(out=lat_sb[:, TOK * lc:TOK * lc + 512],
                                   in_=bank(b0), func=AF.Silu)
                l2 = SE.activation(out=lat_sb[:, TOK * lc + 512:TOK * (lc + 1)],
                                   in_=bank(b0 + 1), func=AF.Silu)
                sem.inc("latev", l1)
                sem.inc("latev", l2)

            sem.wait(TE, "ld_wk")
            sem.wait(TE, "ld_wv")
            TE.wait_ge(sem.get("latev"), 16)
            for dt in range(2):
                for th in range(2):
                    b_ = 2 * dt + th
                    for lc in range(8):
                        TE.matmul(bank(b_),
                                  wk_sb[:, 256 * lc + 128 * dt:256 * lc + 128 * (dt + 1)],
                                  lat_sb[:, TOK * lc + 512 * th:TOK * lc + 512 * (th + 1)],
                                  start=(lc == 0), stop=False)
                    sem.inc("kdmm", TE.matmul(bank(b_), bk_r[0:1, 128 * dt:128 * (dt + 1)],
                                              ones_bf[0:1, 0:512], start=False, stop=True))
            VE.wait_ge(sem.get("kdmm"), 4)
            for dt in range(2):
                for th in range(2):
                    kv_ = VE.tensor_copy(kd_sb[:, TOK * dt + 512 * th:TOK * dt + 512 * (th + 1)],
                                         bank(2 * dt + th))
                    sem.inc("kdev", kv_)
            SY.wait_ge(sem.get("kdev"), 4)
            sem.inc("ksend", SY.dma_start(
                out=ksend.rearrange("(a p) t -> p a t", p=128), in_=kd_sb[:, :]), 16)
            for wave in range(2):
                SE.wait_ge(sem.get("kdev"), 4)
                if wave == 1:
                    SE.wait_ge(sem.get("knmm2"), 4)
                ns = sch(SE.activation(out=ksq2[:, :], in_=kd_sb[:, TOK * wave:TOK * (wave + 1)],
                                       func=AF.Square))
                TE.wait_ge(ss, ns)
                if wave == 1:
                    TE.wait_ge(sv, marks["knorm0"])
                for i in range(4):
                    g_half = i // 2
                    th = i % 2
                    sem.inc("knmm2", TE.matmul(
                        bank(4 + i, w=512, p0=64 * g_half, pn=1),
                        ones128[64 * g_half:64 * (g_half + 1), :],
                        ksq2[64 * g_half:64 * (g_half + 1), 512 * th:512 * (th + 1)],
                        start=True, stop=True))
                VE.wait_ge(sem.get("knmm2"), 4 * (wave + 1))
                for i in range(4):
                    g_half = i // 2
                    nv = vch(VE.tensor_reduce(out=k8[0:1, 4 * wave + i:4 * wave + i + 1],
                                              in_=bank(4 + i, w=512, p0=64 * g_half, pn=1),
                                              axis=AX.X, op=ALU.max))
                marks["knorm0"] = nv
            VE.wait_ge(sv, marks["knorm0"])
            nv = vch(VE.tensor_reduce(out=t32a[0:1, 0:1], in_=k8[:, :], axis=AX.X, op=ALU.max))
            for tt in range(NT):
                if tt >= 1:
                    TE.wait_ge(sem.get("kvev"), tt)
                for lc in range(8):
                    TE.matmul(bank(1, w=256),
                              lat_sb[:, TOK * lc + 128 * tt:TOK * lc + 128 * (tt + 1)],
                              wv_sb[:, 256 * lc:256 * (lc + 1)],
                              start=(lc == 0), stop=False)
                sem.inc("kvmm", TE.matmul(bank(1, w=256), ones_bf[0:1, 0:128], bv_r[0:1, :],
                                          start=False, stop=True))
                VE.wait_ge(sem.get("kvmm"), tt + 1)
                if tt >= 1:
                    VE.wait_ge(sem.get("kvsend"), 64 * tt)
                ve2 = VE.tensor_copy(vs_sb[:, :], bank(1, w=256))
                sem.inc("kvev", ve2)
                SY.wait_ge(sem.get("kvev"), tt + 1)
                for g in range(NKV):
                    sem.inc("kvsend", SY.dma_start(
                        out=vsend[TOK * g + 128 * tt:TOK * g + 128 * (tt + 1), :],
                        in_=vs_sb[:, KV * g:KV * (g + 1)]), 16)
            SY.wait_ge(sv, nv)
            sem.inc("ssend", SY.dma_start(
                out=ssend[0:TOK, :].rearrange("(p a) b -> p (a b)", p=128), in_=scown[:, :]), 16)
            sem.inc("ssend", SY.dma_start(out=ssend[1024:1025, :], in_=t32a[0:1, 0:1]), 16)

            if "comm" in phases:
                GP.wait_ge(sem.get("ksend"), sem.val("ksend"))
                GP.wait_ge(sem.get("kvsend"), sem.val("kvsend"))
                GP.wait_ge(sem.get("ssend"), sem.val("ssend"))
                sem.inc("cc", GP.collective_compute(
                    "AllGather", ALU.bypass, replica_groups=[list(range(NCORE))],
                    ins=[ksend.opt()], outs=[k_all.opt()]))
                sem.inc("cc", GP.collective_compute(
                    "AllGather", ALU.bypass, replica_groups=[list(range(NCORE))],
                    ins=[vsend.opt()], outs=[v_all.opt()]))
                sem.inc("cc", GP.collective_compute(
                    "AllGather", ALU.bypass, replica_groups=[list(range(NCORE))],
                    ins=[ssend.opt()], outs=[s_all.opt()]))

            if "xn_tok" in dbg:
                sem.inc("dbg", SE.dma_start(out=dbg["xn_tok"][:, :], in_=xn_tok[:, :]), 16)
            if "scores" in dbg:
                SE.wait_ge(sv, sem.val("v"))
                sem.inc("dbg", SE.dma_start(out=dbg["scores"][:, :], in_=scown[:, :]), 16)
            if "qt" in dbg and "comm" not in phases:
                SE.wait_ge(sv, qnve_marks[-1])
                sem.inc("dbg", SE.dma_start(out=dbg["qt"][:, :], in_=qt[:, :]), 16)
            if "lat" in dbg:
                SE.wait_ge(sem.get("latev"), 16)
                sem.inc("dbg", SE.dma_start(out=dbg["lat"][:, :], in_=lat_sb[:, :]), 16)

        if "comm" in phases:
            SY.wait_ge(sem.get("cc"), 3)
            with nc.allow_non_contiguous_dma(reason="8-element strided kmax gather"):
                sem.inc("kg", SY.dma_start(
                    out=k8[:, :],
                    in_=s_all[1024:NCORE * 1026:1026, 0:1].rearrange("a b -> b a"),
                    allow_slow_non_contiguous=True), 16)
            for r in range(NCORE):
                sem.inc("scg", SY.dma_start(
                    out=scg[:, 32 * r:32 * (r + 1)],
                    in_=s_all[1026 * r:1026 * r + 1024, :].rearrange("(p a) b -> p (a b)", p=32)), 16)
            VE.wait_ge(sem.get("kg"), 16)
            nv = vch(VE.tensor_reduce(out=kmax2[:, 0:1], in_=k8[:, :], axis=AX.X, op=ALU.max))
            VE.wait_ge(sv, nv)
            nv = vch(VE.tensor_scalar(out=kmax2[:, 1:2], in0=kmax2[:, 0:1],
                                      scalar1=-0.5, scalar2=None, op0=ALU.mult))
            VE.wait_ge(sv, nv)
            VE.wait_ge(sv, marks["qnve_last"])
            vqb = VE.tensor_scalar(out=qt[64:65, :], in0=qt[64:65, :],
                                   scalar1=kmax2[0:1, 1:2], scalar2=None, op0=ALU.add)
            marks["qbias"] = vch(vqb)
            if "s_all" in dbg:
                sem.inc("dbg", SY.dma_start(out=dbg["s_all"][:, :], in_=s_all[:, :]), 16)
            if "qt" in dbg:
                SE.wait_ge(sv, marks["qbias"])
                sem.inc("dbg", SE.dma_start(out=dbg["qt"][:, :], in_=qt[:, :]), 16)

        woe = sb("woe", [128, NT * D], BF16)
        if "fold" in phases or "attn" in phases or "epi" in phases:
            with nc.sbuf_tensor("wo_sb", [128, 2 * D], BF16) as wo_sb, \
                 nc.sbuf_tensor("bo_acc", [1, D], F32, side="right") as bo_acc:
                GP.wait_ge(sem.get("kvmm"), NT)
                GP.wait_ge(sem.get("kdmm"), 4)
                for h in range(NH):
                    if h >= 2:
                        GP.wait_ge(sem.get("woev"), h - 1)
                    sem.inc("ld_wo", GP.dma_start(
                        out=wo_sb[:, D * (h % 2):D * (h % 2 + 1)],
                        in_=Wo[128 * h:128 * (h + 1), :]), 16)
                sem.wait(TE, "ld_wkvT")
                sem.wait(TE, "ld_bkv")
                TE.wait_ge(sem.get("kvmm"), NT)
                for h in range(NH):
                    TE.wait_ge(sem.get("ld_wo"), 16 * (h + 1))
                    if h >= 1:
                        TE.wait_ge(sem.get("woev"), h)
                    for eq in range(4):
                        TE.matmul(bank(eq, w=512, pn=64), wkvT_sb[:, :],
                                  wo_sb[:, D * (h % 2) + 512 * eq:D * (h % 2) + 512 * (eq + 1)],
                                  start=True, stop=True)
                        bo_mm = TE.matmul(
                            bank(4 + eq, w=512, pn=1), bkvcol[:, :],
                            wo_sb[:, D * (h % 2) + 512 * eq:D * (h % 2) + 512 * (eq + 1)],
                            start=(h == 0), stop=(h == NH - 1))
                    sem.inc("womm", bo_mm)
                    VE.wait_ge(sem.get("womm"), h + 1)
                    pc = h // 2
                    p0 = 64 * (h % 2)
                    for eq in range(4):
                        wv_ = VE.tensor_copy(
                            woe[p0:p0 + 64, D * pc + 512 * eq:D * pc + 512 * (eq + 1)],
                            bank(eq, w=512, pn=64))
                    sem.inc("woev", wv_)
                sem.inc("ld_bo", SY.dma_start(out=bo_acc[:, :], in_=bo[:, :]), 16)
                VE.wait_ge(sem.get("ld_bo"), sem.val("ld_bo"))
                VE.wait_ge(sem.get("womm"), sem.val("womm"))
                for eq in range(4):
                    bv_ = VE.tensor_add(out=boeff_r[0:1, 512 * eq:512 * (eq + 1)],
                                        in0=bo_acc[0:1, 512 * eq:512 * (eq + 1)],
                                        in1=bank(4 + eq, w=512, pn=1))
                    marks["boeff"] = vch(bv_)
                if "woe" in dbg:
                    SE.wait_ge(sem.get("woev"), NH)
                    sem.inc("dbg", SE.dma_start(out=dbg["woe"][:, :], in_=woe[:, :]), 16)
                if "boeff" in dbg:
                    SE.wait_ge(sv, marks["boeff"])
                    sem.inc("dbg", SE.dma_start(out=dbg["boeff"][:, :], in_=boeff_r[:, :]), 16)

        bg_ops = []
        if "comm" in phases:
            bstate = {"nv": None}

            def bg(f):
                bg_ops.append(f)

            def bgv(make):
                def run():
                    if bstate["nv"] is not None:
                        VE.wait_ge(sv, bstate["nv"])
                    ins_ = make()
                    bstate["nv"] = vch(ins_)
                    return ins_
                bg(run)

            def w_scg():
                VE.wait_ge(sem.get("scg"), sem.val("scg"))
            bg(w_scg)
            bgv(lambda: VE.tensor_reduce(out=t32a[:, 0:1], in_=scg[:, :], axis=AX.X, op=ALU.max))
            bgv(lambda: VE.transpose(out=t32b[:, :], in_=t32a[:, :]))
            bgv(lambda: VE.tensor_reduce(out=bis[0:1, 1:2], in_=t32b[0:1, :], axis=AX.X, op=ALU.max))
            bgv(lambda: VE.tensor_reduce(out=t32a[:, 0:1], in_=scg[:, :], axis=AX.X, op=ALU.min))
            bgv(lambda: VE.transpose(out=t32b[:, :], in_=t32a[:, :]))
            bgv(lambda: VE.tensor_reduce(out=bis[0:1, 0:1], in_=t32b[0:1, :], axis=AX.X, op=ALU.min))
            bgv(lambda: VE.tensor_scalar(out=bis[0:1, 0:1], in0=bis[0:1, 0:1],
                                         scalar1=-1.0, scalar2=None, op0=ALU.add))
            bgv(lambda: VE.tensor_scalar(out=bis[0:1, 1:2], in0=bis[0:1, 1:2],
                                         scalar1=1e-3, scalar2=None, op0=ALU.add))

            def mk_mid():
                bgv(lambda: VE.scalar_tensor_tensor(out=bis[0:1, 2:3], in0=bis[0:1, 0:1],
                                                    scalar=1.0, in1=bis[0:1, 1:2],
                                                    op0=ALU.mult, op1=ALU.add))
                bgv(lambda: VE.tensor_scalar(out=bis[0:1, 2:3], in0=bis[0:1, 2:3],
                                             scalar1=0.5, scalar2=None, op0=ALU.mult))
                bgv(lambda: VE.tensor_scalar(out=t32a[0:1, :], in0=z32[:, :],
                                             scalar1=bis[0:1, 2:3], scalar2=None, op0=ALU.add))
                bgv(lambda: VE.transpose(out=t32b[:, :], in_=t32a[:, :]))
            mk_mid()
            for _ in range(NITER_BIS):
                bgv(lambda: VE.tensor_scalar(out=cmpb[:, :], in0=scg[:, :],
                                             scalar1=t32b[:, 0:1], scalar2=None, op0=ALU.is_gt))
                bgv(lambda: VE.tensor_reduce(out=t32a[:, 0:1], in_=cmpb[:, :],
                                             axis=AX.X, op=ALU.add))
                bgv(lambda: VE.transpose(out=t32b[:, :], in_=t32a[:, :]))
                bgv(lambda: VE.tensor_reduce(out=bis[0:1, 3:4], in_=t32b[0:1, :],
                                             axis=AX.X, op=ALU.add))
                bgv(lambda: VE.tensor_scalar(out=bis[0:1, 4:5], in0=bis[0:1, 3:4],
                                             scalar1=float(KSEL), scalar2=None, op0=ALU.is_ge))
                bgv(lambda: VE.tensor_sub(out=bis[0:1, 5:6], in0=bis[0:1, 2:3],
                                          in1=bis[0:1, 0:1]))
                bgv(lambda: VE.scalar_tensor_tensor(out=bis[0:1, 0:1], in0=bis[0:1, 4:5],
                                                    scalar=bis[0:1, 5:6], in1=bis[0:1, 0:1],
                                                    op0=ALU.mult, op1=ALU.add))
                bgv(lambda: VE.tensor_sub(out=bis[0:1, 6:7], in0=bis[0:1, 1:2],
                                          in1=bis[0:1, 2:3]))
                bgv(lambda: VE.scalar_tensor_tensor(out=bis[0:1, 1:2], in0=bis[0:1, 4:5],
                                                    scalar=bis[0:1, 6:7], in1=bis[0:1, 2:3],
                                                    op0=ALU.mult, op1=ALU.add))
                mk_mid()

            def _fin():
                if bstate["nv"] is not None:
                    VE.wait_ge(sv, bstate["nv"])
                ins_ = VE.tensor_copy(bis[0:1, 7:8], bis[0:1, 0:1])
                bstate["nv"] = vch(ins_)
                marks["bisdone"] = bstate["nv"]
            bg(_fin)

        def drain_bg(k):
            npop = min(k, len(bg_ops))
            for _ in range(npop):
                bg_ops.pop(0)()

        if "attn" in phases:
            with nc.sbuf_tensor("ktb", [65, 2 * 512], BF16) as ktb, \
                 nc.sbuf_tensor("vtb", [128, 2 * 4 * 65], BF16) as vtb, \
                 nc.sbuf_tensor("expb", [128, 2 * 1024], BF16) as expb, \
                 nc.sbuf_tensor("recr", [1, 2 * 512], F32) as recr, \
                 nc.sbuf_tensor("recrep", [128, 512], F32) as recrep, \
                 nc.sbuf_tensor("masks", [128, 4 * 512], BF16) as masks, \
                 nc.sbuf_tensor("mscr", [128, 512], F32) as mscr:

                GP.wait_ge(sem.get("womm"), sem.val("womm"))
                for d_ in range(4):
                    if d_ > 0:
                        GP.wait_ge(sem.get("mskdone"), d_)
                    s_mio = sem.inc("mskio", GP.iota(
                        mscr[:, :], pattern=[[1, 512]], base=-128 * d_, channel_multiplier=-1,
                        allow_small_or_imprecise_dtypes=True))
                    VE.wait_ge(sem.get("mskio"), s_mio)
                    vm = VE.tensor_scalar(out=masks[:, 512 * d_:512 * (d_ + 1)], in0=mscr[:, :],
                                          scalar1=0.0, scalar2=None, op0=ALU.is_ge)
                    sem.inc("mskdone", vm)
                for sl in range(2):
                    for j in range(4):
                        VE.memset(vtb[:, 260 * sl + 65 * j + 64:260 * sl + 65 * j + 65], 1.0)
                sem.inc("ktones", VE.memset(ktb[64:65, :], 1.0))

                engs = [TE, VE, SE, GP, SY]
                regs = nc.alloc_registers("role_r", engines=[e.engine for e in engs])
                for eng in engs:
                    eng.wait_ge(sem.get("role"), 16)
                nc.regs_load(regs, role_sb[0:1, 0:1])
                TE.wait_ge(sv, marks["qbias"])
                sem.wait(TE, "ktones")
                sem.wait(TE, "mskdone")
                TE.wait_ge(sem.get("womm"), sem.val("womm"))
                SY.wait_ge(sem.get("womm"), sem.val("womm"))
                SY.wait_ge(sem.get("cc"), 2)
                GP.wait_ge(sem.get("cc"), 2)
                SE.wait_ge(sem.get("woev"), sem.val("woev"))

                base_sems = dict(sem.n)
                base_bg = list(bg_ops)
                arm_end = None
                for arm_role in range(NCORE):
                    sem.n = dict(base_sems)
                    bg_ops.clear()
                    bg_ops.extend(base_bg)
                    if "comm" in phases:
                        bstate["nv"] = None
                    with nc.If_cmp(regs, arm_role, "IS_EQ"):
                        emit_attention(nc, sem, arm_role, qt, ktb, vtb, expb, recr,
                                       recrep, masks, ctxd, k_all, v_all, ps, bank,
                                       TE, VE, SE, GP, SY, ones1x64f, drain_bg)
                    nc.end_ifs()
                    if arm_end is None:
                        arm_end = dict(sem.n)
                    else:
                        diff = {k: (arm_end.get(k), sem.n.get(k))
                                for k in set(arm_end) | set(sem.n)
                                if arm_end.get(k) != sem.n.get(k)}
                        assert not diff, f"non-uniform arm semaphores: {diff}"
                sem.n = dict(arm_end)
                bg_ops.clear()

                if "ctxd" in dbg:
                    SE.wait_ge(sem.get("cdone"), sem.val("cdone"))
                    sem.inc("dbg", SE.dma_start(out=dbg["ctxd"][:, :], in_=ctxd[:, :]), 16)
        else:
            drain_bg(10 ** 9)

        if "comm" in phases:
            drain_bg(10 ** 9)
            SY.wait_ge(sv, marks["bisdone"])
            sem.inc("scrow", SY.dma_start(out=sc_row[:, :], in_=scown[:, :]), 16)
            VE.wait_ge(sem.get("scrow"), 16)
            VE.wait_ge(sv, sem.val("v"))
            nv = vch(VE.tensor_scalar(out=mask_row[:, :], in0=sc_row[:, :],
                                      scalar1=bis[0:1, 0:1], scalar2=None, op0=ALU.is_gt))
            VE.wait_ge(sv, nv)
            nv = vch(VE.tensor_tensor_scan(out=pos_row[:, :], data0=mask_row[:, :],
                                           data1=zrow[:, :], initial=0.0,
                                           op0=ALU.add, op1=ALU.add))
            VE.wait_ge(sv, nv)
            nv = vch(VE.tensor_sub(out=pos_row[:, :], in0=pos_row[:, :], in1=mask_row[:, :]))
            VE.wait_ge(sv, nv)
            nv = vch(VE.tensor_scalar(out=pos_row[:, :], in0=pos_row[:, :],
                                      scalar1=-BIG, scalar2=None, op0=ALU.add))
            VE.wait_ge(sv, nv)
            nv = vch(VE.tensor_mul(out=pos_row[:, :], in0=pos_row[:, :], in1=mask_row[:, :]))
            VE.wait_ge(sv, nv)
            vps = VE.tensor_scalar(out=pos_row[:, :], in0=pos_row[:, :],
                                   scalar1=BIG, scalar2=None, op0=ALU.add)
            nv = vch(vps)
            marks["posrow"] = nv
            SY.wait_ge(sv, nv)
            sem.inc("pos128", SY.dma_start(out=pos128[:, :], in_=pos_row[:, :]), 16)
            sem.inc("m128", SY.dma_start(out=mask128[:, :], in_=mask_row[:, :]), 16)
            SE.wait_ge(sem.get("scrow"), 16)
            sgm = SE.activation(out=sig_sb[:, :], in_=scown[:, :], func=AF.Sigmoid,
                                bias=rbcol[:, 0:1])
            sem.inc("sig", sgm)
            VE.wait_ge(sem.get("sig"), 1)
            VE.wait_ge(sem.get("m128"), 16)
            vsm = VE.tensor_mul(out=sigmask[:, :], in0=sig_sb[:, :], in1=mask128[:, :])
            marks["sigmask"] = vch(vsm)
            if "bis" in dbg:
                SE.wait_ge(sv, marks["bisdone"])
                sem.inc("dbg", SE.dma_start(out=dbg["bis"][:, :], in_=bis[:, :]), 16)
            if "posrow" in dbg:
                SE.wait_ge(sv, marks["posrow"])
                sem.inc("dbg", SE.dma_start(out=dbg["posrow"][:, :], in_=pos_row[:, :]), 16)
            if "sigmask" in dbg:
                SE.wait_ge(sv, marks["sigmask"])
                sem.inc("dbg", SE.dma_start(out=dbg["sigmask"][:, :], in_=sigmask[:, :]), 16)

        bis_stack.close()
        qt_stack.close()

        if "ffn" in phases:
            with nc.sbuf_tensor("PT_sb", [128, NJC * TOK], BF16) as PT_sb, \
                 nc.sbuf_tensor("proc", [128, NJC * D], BF16) as proc:
                emit_ffn(nc, sem, marks, dbg, ps, bank, TE, VE, SE, GP, SY,
                         xn_tok, pos128, pos_row, sigmask,
                         ones_bf, ones_f32, b2_r, w1, b1, w2, sv, ss, vch, sch,
                         PT_sb, proc)
                if "epi" in phases:
                    emit_epilogue(nc, sem, marks, dbg, ps, bank, TE, VE, SE, GP, SY,
                                  x_in, y_out, stats, ctxd, woe, boeff_r, sigmask,
                                  ones_f32, sv, ss, vch, sch, PT_sb, proc)

        if "epi" not in phases or "ffn" not in phases:
            yz = sb("yz", [128, D], F32)
            VE.wait_ge(sv, sem.val("v"))
            nv = vch(VE.memset(yz[:, :], 0.0))
            SY.wait_ge(sv, nv)
            for tt in range(NT):
                sem.inc("yout", SY.dma_start(out=y_out[128 * tt:128 * (tt + 1), :],
                                             in_=yz[:, :]), 16)

        SY.wait_ge(sem.get("yout"), sem.val("yout"))
        if dbg:
            SY.wait_ge(sem.get("dbg"), sem.val("dbg"))
        GP.wait_ge(sem.get("yout"), sem.val("yout"))
    return nc


def emit_attention(nc, sem, role, qt, ktb, vtb, expb, recr, recrep, masks, ctxd,
                   k_all, v_all, ps, bank, TE, VE, SE, GP, SY, ones1x64f, drain_bg):
    import os
    max_groups = int(os.environ.get("MK_ATTN_GROUPS", "999"))
    max_blocks = int(os.environ.get("MK_ATTN_BLOCKS", "999"))
    """One switch arm. Event counts are identical for every role."""
    gidx = 0
    grp = 0
    nkst = 0
    nvst = 0
    pending = []
    qblocks = [(1, min(16 - role, max_blocks)), (0, min(role + 1, max_blocks))]

    def flush_pending():
        while pending:
            pending.pop(0)()

    for qb, nkb in qblocks:
        for g in range(NKV):
            for h2 in range(2):
                if grp >= max_groups:
                    continue
                hA = 4 * g + 2 * h2
                pc = 2 * g + h2
                cb = 4 + 2 * (grp % 2)
                g0 = gidx
                pend_ctx = [None]

                def flush_ctx():
                    if pend_ctx[0] is not None:
                        f = pend_ctx[0]
                        pend_ctx[0] = None
                        f()

                for kb in range(nkb):
                    sl = (gidx // 4) % 2
                    krow, kcol, vrow = kblock_src(kb, g)
                    blk = gidx // 4
                    if blk >= 2:
                        SY.wait_ge(sem.get("ctec"), 4 * blk - 4)
                        GP.wait_ge(sem.get("ctec"), 4 * blk - 4)
                    nkst = sem.inc("kst", SY.dma_start(
                        out=ktb[0:64, 512 * sl:512 * (sl + 1)],
                        in_=k_all[krow:krow + KV, kcol:kcol + 512]), 16)
                    nvst = sem.inc("vst", GP.dma_start(
                        out=vtb.rearrange("p (s a d) -> p s a d", s=2, a=4)[:, sl, :, 0:64],
                        in_=v_all[vrow:vrow + 512, :].rearrange(
                            "(a p) d -> p a d", p=128)), 16)
                    diag = (kb == nkb - 1)
                    TE.wait_ge(sem.get("kst"), nkst)
                    TE.wait_ge(sem.get("vst"), nvst)
                    for i4 in range(4):
                        sb_ = 2 * (gidx % 2)
                        if gidx >= 2:
                            TE.wait_ge(sem.get("expdone"), gidx - 1)
                        for hh in range(2):
                            mm = TE.matmul(
                                bank(sb_ + hh),
                                ktb[:, 512 * sl + 128 * i4:512 * sl + 128 * (i4 + 1)],
                                qt[:, TOK * (hA + hh) + 512 * qb:
                                   TOK * (hA + hh) + 512 * (qb + 1)],
                                start=True, stop=True)
                        sem.inc("scmm", mm)
                        flush_ctx()
                        SE.wait_ge(sem.get("scmm"), gidx + 1)
                        if gidx >= 2:
                            SE.wait_ge(sem.get("ctec"), gidx - 1)
                        ex = SE.activation(out=expb[:, 1024 * (gidx % 2):1024 * (gidx % 2 + 1)],
                                           in_=ps[:, 512 * sb_:512 * sb_ + 1024],
                                           func=AF.Exp, scale=0.125)
                        sem.inc("expdone", ex)
                        if diag:
                            VE.wait_ge(sem.get("expdone"), gidx + 1)
                            for hh in range(2):
                                vm = VE.tensor_mul(
                                    out=expb[:, 1024 * (gidx % 2) + 512 * hh:
                                             1024 * (gidx % 2) + 512 * (hh + 1)],
                                    in0=expb[:, 1024 * (gidx % 2) + 512 * hh:
                                             1024 * (gidx % 2) + 512 * (hh + 1)],
                                    in1=masks[:, 512 * i4:512 * (i4 + 1)])
                                sem.inc("mskmul", vm)
                        first = (kb == 0 and i4 == 0)
                        last = (kb == nkb - 1 and i4 == 3)
                        g_cur = gidx
                        sl_cur = sl
                        i4_cur = i4
                        grp_cur = grp
                        cb_cur = cb
                        diag_cur = diag
                        mskval = sem.val("mskmul")
                        vstval = nvst

                        def emit_ctx(g_cur=g_cur, sl_cur=sl_cur, i4_cur=i4_cur,
                                     grp_cur=grp_cur, cb_cur=cb_cur, first=first,
                                     last=last, diag_cur=diag_cur, mskval=mskval,
                                     vstval=vstval):
                            if diag_cur:
                                TE.wait_ge(sem.get("mskmul"), mskval)
                            else:
                                TE.wait_ge(sem.get("expdone"), g_cur + 1)
                            if first and grp_cur >= 2:
                                TE.wait_ge(sem.get("cdone"), grp_cur - 1)
                            for hh in range(2):
                                cm = TE.matmul(
                                    bank(cb_cur + hh, w=512, pn=65),
                                    vtb[:, 260 * sl_cur + 65 * i4_cur:
                                        260 * sl_cur + 65 * (i4_cur + 1)],
                                    expb[:, 1024 * (g_cur % 2) + 512 * hh:
                                         1024 * (g_cur % 2) + 512 * (hh + 1)],
                                    start=first, stop=last)
                                if last:
                                    sem.inc("ctxg", cm)
                            if last:
                                sem.inc("ctec", TE.nop())
                            else:
                                sem.inc("ctec", cm)

                        pend_ctx[0] = emit_ctx
                        gidx += 1
                        if gidx == g0 + 2:
                            flush_pending()
                flush_ctx()
                pending.append(make_group_epilogue(
                    nc, sem, grp, qb, pc, cb, qt, recr, recrep, ctxd, ps, bank,
                    TE, VE, SE, ones1x64f))
                grp += 1
                drain_bg(26)
    flush_pending()
    drain_bg(10 ** 9)


def make_group_epilogue(nc, sem, grp, qb, pc, cb, qt, recr, recrep, ctxd, ps, bank,
                        TE, VE, SE, ones1x64f):
    def run():
        col = TOK * pc + 512 * qb
        SE.wait_ge(sem.get("ctxg"), 2 * (grp + 1))
        SE.activation(out=ctxd[0:64, col:col + 512],
                      in_=bank(cb, w=512, pn=64), func=AF.Copy)
        SE.activation(out=ctxd[64:128, col:col + 512],
                      in_=bank(cb + 1, w=512, pn=64), func=AF.Copy)
        SE.activation(out=recr[0:1, 0:512], in_=bank(cb, w=512, p0=64, pn=1),
                      func=AF.Copy)
        c2 = SE.activation(out=recr[0:1, 512:1024], in_=bank(cb + 1, w=512, p0=64, pn=1),
                           func=AF.Copy)
        sem.inc("ctevac", c2)
        TE.wait_ge(sem.get("ctevac"), grp + 1)
        TE.matmul(bank(cb, w=512, pn=64), ones1x64f[:, :], recr[0:1, 0:512],
                  start=True, stop=True)
        rr = TE.matmul(bank(cb, w=512, p0=64, pn=64), ones1x64f[:, :],
                       recr[0:1, 512:1024], start=True, stop=True)
        sem.inc("rrep", rr)
        VE.wait_ge(sem.get("rrep"), grp + 1)
        VE.reciprocal(out=recrep[:, :], in_=bank(cb, w=512))
        vs_ = VE.tensor_mul(out=ctxd[:, col:col + 512], in0=ctxd[:, col:col + 512],
                            in1=recrep[:, :])
        sem.inc("cdone", vs_)
    return run


def emit_ffn(nc, sem, marks, dbg, ps, bank, TE, VE, SE, GP, SY,
             xn_tok, pos128, pos_row, sigmask,
             ones_bf, ones_f32, b2_r, w1, b1, w2, sv, ss, vch, sch,
             PT_sb, proc):
    with nc.sbuf_tensor("hff", [128, 64 * CAP], BF16) as hff:
        P_sb = hff
        with nc.sbuf_tensor("posrep", [128, TOK], F32) as posrep, \
             nc.sbuf_tensor("iota384", [128, CAP], F32) as iota384, \
             nc.sbuf_tensor("jcol", [128, NJC], F32) as jcol:
            GP.wait_ge(sem.get("cdone"), sem.val("cdone"))
            sem.inc("ffio", GP.iota(iota384[:, :], pattern=[[1, CAP]], base=0,
                                    channel_multiplier=0,
                                    allow_small_or_imprecise_dtypes=True))
            for jc in range(NJC):
                sem.inc("ffio", GP.iota(jcol[:, jc:jc + 1], pattern=[[0, 1]],
                                        base=128 * jc, channel_multiplier=1,
                                        allow_small_or_imprecise_dtypes=True))
            TE.wait_ge(sv, marks["posrow"])
            TE.wait_ge(sem.get("cdone"), sem.val("cdone"))
            pos_local = pos_row.rearrange("b (p a) -> b a p", p=128)
            for i in range(2):
                sem.inc("prep", TE.matmul(bank(6 + i), ones_f32[0:1, 0:128],
                                          pos_local[0:1, 4 * i:4 * (i + 1), :],
                                          start=True, stop=True))
            VE.wait_ge(sem.get("prep"), 2)
            VE.wait_ge(sv, sem.val("v"))
            for i in range(2):
                vch(VE.tensor_copy(posrep[:, 512 * i:512 * (i + 1)], bank(6 + i)))
            VE.wait_ge(sem.get("ffio"), 4)
            VE.wait_ge(sem.get("pos128"), 16)
            VE.wait_ge(sv, sem.val("v"))
            for tt in range(NT):
                pb = VE.tensor_scalar(out=P_sb[:, CAP * tt:CAP * (tt + 1)],
                                      in0=iota384[:, :], scalar1=pos128[:, tt:tt + 1],
                                      scalar2=None, op0=ALU.is_equal)
                marks["pbuild"] = vch(pb)
            for jc in range(NJC):
                ptb = VE.tensor_scalar(out=PT_sb[:, TOK * jc:TOK * (jc + 1)],
                                       in0=posrep[:, :], scalar1=jcol[:, jc:jc + 1],
                                       scalar2=None, op0=ALU.is_equal)
                marks["ptbuild"] = vch(ptb)

        with nc.sbuf_tensor("sel", [128, 16 * CAP], BF16) as sel, \
             nc.sbuf_tensor("w1_sb", [128, 2 * 2048], BF16) as w1_sb, \
             nc.sbuf_tensor("b1buf", [1, 1024], BF16) as b1buf:
            TE.wait_ge(sv, marks["pbuild"])
            for ec in range(16):
                bnk = ec % 2
                if ec >= 2:
                    TE.wait_ge(sem.get("selev"), ec - 1)
                for tt in range(NT):
                    mm = TE.matmul(bank(bnk, w=CAP),
                                   xn_tok[:, D * tt + 128 * ec:D * tt + 128 * (ec + 1)],
                                   P_sb[:, CAP * tt:CAP * (tt + 1)],
                                   start=(tt == 0), stop=(tt == NT - 1))
                sem.inc("selmm", mm)
                SE.wait_ge(sem.get("selmm"), ec + 1)
                sev = SE.activation(out=sel[:, CAP * ec:CAP * (ec + 1)],
                                    in_=bank(bnk, w=CAP), func=AF.Copy)
                sem.inc("selev", sev)
            if "sel" in dbg:
                SE.wait_ge(sem.get("selev"), 16)
                sem.inc("dbg", SE.dma_start(out=dbg["sel"][:, :], in_=sel[:, :]), 16)

            TE.wait_ge(sem.get("selev"), 16)
            for fc in range(64):
                if fc == 0:
                    GP.wait_ge(sv, marks["ptbuild"])
                if fc % 8 == 0:
                    GP.wait_ge(sem.get("w1mm"), fc)
                    sem.inc("ld_b1", GP.dma_start(
                        out=b1buf[:, :], in_=b1[0:1, 1024 * (fc // 8):1024 * (fc // 8 + 1)]), 16)
                if fc >= 2:
                    GP.wait_ge(sem.get("w1mm"), fc - 1)
                sem.inc("ld_w1", GP.dma_start(
                    out=w1_sb[:, 2048 * (fc % 2):2048 * (fc % 2 + 1)],
                    in_=w1[128 * fc:128 * (fc + 1), :]), 16)
                bnk = fc % 4
                TE.wait_ge(sem.get("ld_w1"), 16 * (fc + 1))
                TE.wait_ge(sem.get("ld_b1"), 16 * (fc // 8 + 1))
                if fc >= 4:
                    TE.wait_ge(sem.get("hgel"), fc - 3)
                for ec in range(16):
                    TE.matmul(bank(bnk, w=CAP),
                              w1_sb[:, 2048 * (fc % 2) + 128 * ec:
                                    2048 * (fc % 2) + 128 * (ec + 1)],
                              sel[:, CAP * ec:CAP * (ec + 1)],
                              start=(ec == 0), stop=False)
                sem.inc("w1mm", TE.matmul(bank(bnk, w=CAP),
                                          b1buf[0:1, 128 * (fc % 8):128 * (fc % 8 + 1)],
                                          ones_bf[0:1, 0:CAP], start=False, stop=True))
                SE.wait_ge(sem.get("w1mm"), fc + 1)
                hg = SE.activation(out=hff[:, CAP * fc:CAP * (fc + 1)],
                                   in_=bank(bnk, w=CAP), func=AF.Gelu_apprx_tanh)
                sem.inc("hgel", hg)
            if "hff" in dbg:
                SE.wait_ge(sem.get("hgel"), 64)
                sem.inc("dbg", SE.dma_start(out=dbg["hff"][:, :], in_=hff[:, 0:4 * CAP]), 16)

        with nc.sbuf_tensor("w2_sb", [128, 2 * 1024], BF16) as w2_sb:
            for halfe in range(2):
                for fc in range(64):
                    nW = 64 * halfe + fc
                    if nW == 0:
                        GP.wait_ge(sem.get("w1mm"), 64)
                    if nW >= 2:
                        GP.wait_ge(sem.get("w2mm"), 6 * (nW - 1))
                    sem.inc("ld_w2", GP.dma_start(
                        out=w2_sb[:, 1024 * (nW % 2):1024 * (nW % 2 + 1)],
                        in_=w2[128 * fc:128 * (fc + 1), 1024 * halfe:1024 * (halfe + 1)]), 16)
                    TE.wait_ge(sem.get("ld_w2"), 16 * (nW + 1))
                    if fc == 0:
                        TE.wait_ge(sem.get("hgel"), 64)
                        if halfe == 1:
                            TE.wait_ge(sem.get("procev"), 6)
                    for tj in range(NJC):
                        for eq in range(2):
                            mm = TE.matmul(bank(2 * tj + eq, w=512),
                                           hff[:, CAP * fc + 128 * tj:CAP * fc + 128 * (tj + 1)],
                                           w2_sb[:, 1024 * (nW % 2) + 512 * eq:
                                                 1024 * (nW % 2) + 512 * (eq + 1)],
                                           start=(fc == 0), stop=False)
                    sem.inc("w2mm", mm, 6)
                for tj in range(NJC):
                    for eq in range(2):
                        sem.inc("w2bias", TE.matmul(
                            bank(2 * tj + eq, w=512), ones_bf[0:1, 0:128],
                            b2_r[0:1, 1024 * halfe + 512 * eq:1024 * halfe + 512 * (eq + 1)],
                            start=False, stop=True))
                SE.wait_ge(sem.get("w2bias"), 6 * (halfe + 1))
                for tj in range(NJC):
                    for eq in range(2):
                        pv = SE.activation(
                            out=proc[:, D * tj + 1024 * halfe + 512 * eq:
                                     D * tj + 1024 * halfe + 512 * (eq + 1)],
                            in_=bank(2 * tj + eq, w=512), func=AF.Copy)
                        sem.inc("procev", pv)
            if "proc" in dbg:
                SE.wait_ge(sem.get("procev"), 12)
                sem.inc("dbg", SE.dma_start(out=dbg["proc"][:, :], in_=proc[:, :]), 16)


def emit_epilogue(nc, sem, marks, dbg, ps, bank, TE, VE, SE, GP, SY,
                  x_in, y_out, stats, ctxd, woe, boeff_r, sigmask,
                  ones_f32, sv, ss, vch, sch, PT_sb, proc):
    with nc.sbuf_tensor("xep", [128, 2 * D], F32) as xep, \
         nc.sbuf_tensor("gep", [128, D], F32) as gep, \
         nc.sbuf_tensor("uep", [128, D], F32) as uep, \
         nc.sbuf_tensor("oep", [128, 2 * D], F32) as oep:
        SY.wait_ge(sem.get("procev"), 12)
        gsil_marks = []
        for tt in range(2):
            sem.inc("ld_xe", SY.dma_start(out=xep[:, D * tt:D * (tt + 1)],
                                          in_=x_in[128 * tt:128 * (tt + 1), :]), 16)
        TE.wait_ge(sem.get("procev"), 12)
        TE.wait_ge(sv, marks["ptbuild"])
        sem.wait(TE, "cdone")
        TE.wait_ge(sv, marks["boeff"])
        for tt in range(NT):
            slot = tt % 2
            if tt >= 1:
                TE.wait_ge(ss, gsil_marks[tt - 1])
            for eq in range(4):
                for jc in range(NJC):
                    mm = TE.matmul(bank(eq),
                                   PT_sb[:, TOK * jc + 128 * tt:TOK * jc + 128 * (tt + 1)],
                                   proc[:, D * jc + 512 * eq:D * jc + 512 * (eq + 1)],
                                   start=(jc == 0), stop=(jc == NJC - 1))
            sem.inc("scat", mm)
            VE.wait_ge(sem.get("ld_xe"), 16 * (tt + 1))
            VE.wait_ge(sem.get("scat"), tt + 1)
            VE.wait_ge(sv, marks["sigmask"])
            if tt >= 1:
                VE.wait_ge(ss, gsil_marks[tt - 1])
            nv = vch(VE.tensor_scalar(out=uep[:, :], in0=xep[:, D * slot:D * (slot + 1)],
                                      scalar1=stats[:, tt:tt + 1],
                                      scalar2=stats[:, NT + tt:NT + tt + 1],
                                      op0=ALU.mult, op1=ALU.subtract))
            VE.wait_ge(sv, nv)
            nv = vch(VE.scalar_tensor_tensor(out=uep[:, :], in0=ps[:, 0:2048],
                                             scalar=sigmask[:, tt:tt + 1], in1=uep[:, :],
                                             op0=ALU.mult, op1=ALU.add))
            SE.wait_ge(sv, nv)
            gs = SE.activation(out=gep[:, :], in_=uep[:, :], func=AF.Silu)
            gsil_marks.append(sch(gs))
            for eq in range(4):
                sb_ = 4 + eq % 2
                n_s = 4 * tt + eq
                if n_s >= 2:
                    TE.wait_ge(sem.get("sev"), n_s - 1)
                for pcc in range(8):
                    TE.matmul(bank(sb_),
                              ctxd[:, TOK * pcc + 128 * tt:TOK * pcc + 128 * (tt + 1)],
                              woe[:, D * pcc + 512 * eq:D * pcc + 512 * (eq + 1)],
                              start=(pcc == 0), stop=False)
                sem.inc("smm", TE.matmul(bank(sb_), ones_f32[0:1, 0:128],
                                         boeff_r[0:1, 512 * eq:512 * (eq + 1)],
                                         start=False, stop=True))
                VE.wait_ge(sem.get("smm"), n_s + 1)
                VE.wait_ge(ss, gsil_marks[tt])
                if tt >= 2:
                    VE.wait_ge(sem.get("yout"), 16 * (tt - 1))
                nv = vch(VE.tensor_mul(out=oep[:, D * slot + 512 * eq:D * slot + 512 * (eq + 1)],
                                       in0=bank(sb_), in1=gep[:, 512 * eq:512 * (eq + 1)]))
                VE.wait_ge(sv, nv)
                va = VE.tensor_add(out=oep[:, D * slot + 512 * eq:D * slot + 512 * (eq + 1)],
                                   in0=oep[:, D * slot + 512 * eq:D * slot + 512 * (eq + 1)],
                                   in1=xep[:, D * slot + 512 * eq:D * slot + 512 * (eq + 1)])
                sem.inc("sev", va)
            SY.wait_ge(sem.get("sev"), sem.val("sev"))
            sem.inc("yout", SY.dma_start(out=y_out[128 * tt:128 * (tt + 1), :],
                                         in_=oep[:, D * slot:D * (slot + 1)]), 16)
            if tt + 2 < NT:
                SY.wait_ge(sem.get("sev"), sem.val("sev"))
                sem.inc("ld_xe", SY.dma_start(out=xep[:, D * slot:D * (slot + 1)],
                                              in_=x_in[128 * (tt + 2):128 * (tt + 3), :]), 16)


def _pack_lhsT(w):
    w = np.asarray(w, dtype=np.float32)
    E, F = w.shape
    nf = F // 128
    na = E // 128
    t = w.reshape(na, 128, nf, 128).transpose(2, 1, 0, 3).reshape(nf * 128, E)
    return np.ascontiguousarray(t)


def _prep_inputs(core, **inp):
    X = np.ascontiguousarray(np.asarray(inp["L0_attn_x"], dtype=np.float32).reshape(S, D))
    r = core
    xo = np.concatenate([X[512 * r:512 * (r + 1)], X[512 * (15 - r):512 * (16 - r)]], axis=0)
    m = {
        "x_own": xo,
        "role": np.array([[r]], dtype=np.int32),
        "Wqp": _pack_lhsT(inp["L0_attn_Wq"]), "bq": np.asarray(inp["L0_attn_bq"]).reshape(1, -1),
        "Wlp": _pack_lhsT(inp["L0_attn_Wl"]), "bl": np.asarray(inp["L0_attn_bl"]).reshape(1, -1),
        "Wk": inp["L0_attn_Wk"], "bk": np.asarray(inp["L0_attn_bk"]).reshape(1, -1),
        "Wv": inp["L0_attn_Wv"], "bv": np.asarray(inp["L0_attn_bv"]).reshape(1, -1),
        "Wq2kv": inp["L0_attn_Wq2kv"], "Wkv2h": inp["L0_attn_Wkv2h"],
        "bkv2h": np.asarray(inp["L0_attn_bkv2h"]).reshape(-1, 1),
        "Wo": inp["L0_attn_Wo"], "bo": np.asarray(inp["L0_attn_bo"]).reshape(1, -1),
        "rw": np.asarray(inp["L0_ffn_router_weight"]).reshape(1, -1),
        "rb": np.asarray(inp["L0_ffn_router_bias"]).reshape(1, 1),
        "w1p": _pack_lhsT(inp["L0_ffn_w1"]), "b1": np.asarray(inp["L0_ffn_b1"]).reshape(1, -1),
        "w2": inp["L0_ffn_w2"], "b2": np.asarray(inp["L0_ffn_b2"]).reshape(1, -1),
    }
    return {k: np.ascontiguousarray(np.asarray(v, dtype=(np.int32 if k == "role" else np.float32)))
            for k, v in m.items()}


def kernel(**inputs):
    nc = build(DEBUG, PHASES)
    in_maps = [_prep_inputs(c, **inputs) for c in range(NCORE)]
    res = run_bass_kernel_spmd(nc, in_maps, core_ids=list(range(NCORE)))
    Y = np.empty((S, D), dtype=np.float32)
    for r in range(NCORE):
        yr = np.asarray(res.results[r]["y"])
        Y[512 * r:512 * (r + 1)] = yr[0:512]
        Y[512 * (15 - r):512 * (16 - r)] = yr[512:1024]
    return Y.reshape(1, S, D)


def kernel_dev(debug, phases, **inputs):
    nc = build(debug, phases)
    in_maps = [_prep_inputs(c, **inputs) for c in range(NCORE)]
    res = run_bass_kernel_spmd(nc, in_maps, core_ids=list(range(NCORE)))
    return res
